# version 08
# baseline (speedup 1.0000x reference)
import numpy as np
import ml_dtypes
import concourse.bass as bass
import concourse.mybir as mybir
from concourse.bass_utils import run_bass_kernel_spmd

AF = mybir.ActivationFunctionType
OP = mybir.AluOpType
F32 = mybir.dt.float32
BF16 = mybir.dt.bfloat16
I32 = mybir.dt.int32
U8 = mybir.dt.uint8
XAX = mybir.AxisListType.X
bf16 = ml_dtypes.bfloat16

T = 1024
NPAIR = 43
WRING = 12
INTER = 11008
ENGINES = ("sync", "vector", "scalar", "tensor", "gpsimd")


def _host_consts():
    n = np.arange(4096, dtype=np.float64)
    Om = np.exp(-2j * np.pi * n / 4096.0)
    w = (1.0 - Om) / (1.0 + Om)
    c = 2.0 / (1.0 + Om)
    f = lambda a: np.ascontiguousarray(a, dtype=np.float32)
    w0 = np.arange(64)
    E1 = np.exp(2j * np.pi * np.outer(w0, w0) / 64.0)[:, ::-1]
    TW = np.exp(2j * np.pi * np.outer(w0, w0) / 4096.0)[:, ::-1]
    E2 = np.exp(2j * np.pi * np.outer(w0, w0) / 64.0)[:, ::-1]
    return {
        "wre": f(np.broadcast_to(w.real, (64, 4096))),
        "wim": f(np.broadcast_to(w.imag, (64, 4096))),
        "cre64": f(c.real.reshape(64, 64)),
        "cim64": f(c.imag.reshape(64, 64)),
        "E1c": f(np.block([[E1.real, E1.imag], [-E1.imag, E1.real]])),
        "TWr": f(TW.real), "TWi": f(TW.imag),
        "E2c": f(np.vstack([E2.real, -E2.imag])),
        "identf": f(np.eye(128)),
        "esl": np.ascontiguousarray(
            (np.arange(63)[None, :] == 31).astype(np.float32)
            * np.ones((128, 1), np.float32)).astype(bf16),
        "identb": np.eye(128, dtype=np.float32).astype(bf16),
    }


_CONSTS = _host_consts()


class Sem:
    def __init__(self, nc, name):
        self.h = nc.alloc_semaphore(name)
        self.name = name
        self.n = 0


class Plan:
    def __init__(self):
        self.ops = {e: [] for e in ENGINES}

    def op(self, eng, fn, waits=(), incs=()):
        wl = []
        for s, v in waits:
            v = max(0, v)
            assert v <= s.n, f"wait {s.name}>={v} but only {s.n} planned"
            if v > 0:
                wl.append((s.h, v))
        il = []
        for s, k in incs:
            s.n += k
            il.append((s.h, k))
        self.ops[eng].append((fn, wl, il))


def build_program(debug=False):
    nc = bass.Bass("TRN2", target_bir_lowering=False)

    x_h = nc.dram_tensor("x", [T, 4096], F32, kind="ExternalInput")
    qwgu_h = nc.dram_tensor("qwgu", [4096, 2752], I32, kind="ExternalInput")
    qzgu_h = nc.dram_tensor("qzgu", [32 * 2752], I32, kind="ExternalInput")
    sgupm_h = nc.dram_tensor("sgupm", [NPAIR * 32 * 512], F32, kind="ExternalInput")
    sgun_h = nc.dram_tensor("sgun", [32 * 22016], F32, kind="ExternalInput")
    qwdn_h = nc.dram_tensor("qwdn", [INTER, 512], I32, kind="ExternalInput")
    qzdn_h = nc.dram_tensor("qzdn", [86 * 512], I32, kind="ExternalInput")
    sdn_h = nc.dram_tensor("sdn", [86 * 4096], F32, kind="ExternalInput")
    lam_h = nc.dram_tensor("lam", [128, 1], F32, kind="ExternalInput")
    vct_h = nc.dram_tensor("vct", [64, 128], F32, kind="ExternalInput")
    bp_h = nc.dram_tensor("bp", [64, 1], F32, kind="ExternalInput")
    ctpq_h = nc.dram_tensor("ctpq", [1, 384], F32, kind="ExternalInput")
    dls_h = nc.dram_tensor("dls", [1, 2], F32, kind="ExternalInput")
    c_h = {k: nc.dram_tensor("c_" + k, list(v.shape),
                             BF16 if v.dtype == bf16 else F32, kind="ExternalInput")
           for k, v in _CONSTS.items()}
    out_h = nc.dram_tensor("out", [T, 4096], F32, kind="ExternalOutput")

    _k = dict(kind="ExternalOutput") if debug else {}
    if debug:
        mrow_dbg = nc.dram_tensor("mrow_dbg", [128, 64], F32, kind="ExternalOutput")
        xt2_dbg = nc.dram_tensor("xt2_dbg", [128, 4096], F32, kind="ExternalOutput")
        xn2_dbg = nc.dram_tensor("xn2_dbg", [128, 4096], F32, kind="ExternalOutput")
    h_hbm = nc.dram_tensor("h_scr", [INTER, T], BF16, **_k)
    s_hbm = nc.dram_tensor("s_scr", [T, 4096], F32, **_k)
    wdn_hbm = nc.dram_tensor("wdn_scr", [INTER, 4096], BF16, **_k)
    krev_hbm = nc.dram_tensor("krev_scr", [4352], F32, **_k)
    kk_hbm = nc.dram_tensor("kk_scr", [8, 4096], F32, **_k)
    zsneg_hbm = nc.dram_tensor("zsneg_scr", [32 * 22016], BF16)
    sgupm_b_hbm = nc.dram_tensor("sgupm_b_scr", [NPAIR * 32 * 512], BF16)
    zdn_hbm = nc.dram_tensor("zdn_scr", [86 * 4096], BF16)
    sdnb_hbm = nc.dram_tensor("sdnb_scr", [86 * 4096], BF16)
    zdn2 = zdn_hbm.ap().rearrange("(g d) -> g d", d=4096)
    sdnb2 = sdnb_hbm.ap().rearrange("(g d) -> g d", d=4096)
    bnc_hbm = nc.dram_tensor("bounce_scr", [2048], F32, **_k)

    cs = {k: nc.alloc_sbuf_tensor("k_" + k, list(v.shape),
                                  BF16 if v.dtype == bf16 else F32)
          for k, v in _CONSTS.items() if k not in ("wre", "wim")}
    lam_sb = nc.alloc_sbuf_tensor("lam_sb", [128, 1], F32)
    vct_sb = nc.alloc_sbuf_tensor("vct_sb", [64, 128], F32)
    bp_sb = nc.alloc_sbuf_tensor("bp_sb", [64, 1], F32)
    ctpq_sb = nc.alloc_sbuf_tensor("ctpq_sb", [1, 384], F32)
    dls_sb = nc.alloc_sbuf_tensor("dls_sb", [1, 2], F32)
    step_sb = nc.alloc_sbuf_tensor("step_sb", [1, 4], F32)
    ts64_sb = nc.alloc_sbuf_tensor("ts64_sb", [64, 1], F32)
    d128_sb = nc.alloc_sbuf_tensor("d128_sb", [128, 1], F32)
    vmat_sb = nc.alloc_sbuf_tensor("vmat_sb", [128, 8], F32)
    mrow_sb = nc.alloc_sbuf_tensor("mrow_sb", [128, 8], F32)
    mrowall_sb = nc.alloc_sbuf_tensor("mrowall_sb", [128, 64], F32)
    hank_sb = nc.alloc_sbuf_tensor("hank_sb", [128, 256], F32)
    xt32_sb = nc.alloc_sbuf_tensor("xt32_sb", [32, 1024], BF16)
    zsn_sb = nc.alloc_sbuf_tensor("zsn_sb", [32, 2 * 512], BF16)

    ARENA = 161792
    arena = nc.alloc_sbuf_tensor("arena", [128, ARENA], U8)

    def view(off, parts, n, dt):
        esz = {F32: 4, I32: 4, BF16: 2, U8: 1}[dt]
        assert off % 4 == 0 and off + n * esz <= ARENA, (off, n)
        ap = arena[0:parts, off:off + n * esz]
        if dt != U8:
            ap = ap.bitcast(dt)
        return ap

    xnT = view(0, 128, 32768, BF16).rearrange("p (j t) -> p j t", t=1024)
    tbuf = view(65536, 128, 4608, BF16)
    C0 = 74752
    zq_gu = view(90112, 128, 688, I32)
    zi_gu = view(90112 + 2752, 128, 5504, I32)
    zb_gu = view(90112 + 24768, 128, 5504, BF16)
    sf2_gu = view(90112 + 35776, 128, 5504, F32)
    zb2_gu = view(90112 + 57792, 128, 5504, BF16)
    sf_gu = view(90112, 128, 5504, F32)
    sb_gu = view(90112 + 22016, 128, 5504, BF16)
    zq_dn = view(90112, 128, 344, I32)
    zi_dn = view(90112 + 1376, 128, 2752, I32)
    zb_dn = view(90112 + 12384, 128, 2752, BF16)
    sf_dn = view(90112 + 17888, 128, 2752, F32)
    sb_dn = view(90112 + 28896, 128, 2752, BF16)
    wre_v = view(90112, 64, 1024, F32)
    wim_v = view(90112 + 4096, 64, 1024, F32)
    dre_v = view(90112 + 8192, 64, 1024, F32)
    dim_v = view(90112 + 12288, 64, 1024, F32)
    rin_v = view(90112 + 16384, 64, 1024, F32)
    tmp_v = view(90112 + 20480, 64, 1024, F32)
    rcat_v = view(90112 + 24576, 128, 1024, F32)
    kk8_v = view(90112 + 28672, 8, 4096, F32)
    vstage_v = view(90112 + 45056, 1, 1024, F32)
    vrow_v = view(90112 + 49152, 1, 1024, F32)
    brow_v = view(90112 + 53248, 1, 128, F32)
    bst_v = view(90112 + 53760, 128, 1, F32)
    ast_v = view(90112 + 54272, 128, 64, F32)
    s1_v = view(90112 + 54528, 64, 128, F32)
    s2_v = view(90112 + 55040, 128, 64, F32)
    krev_v = view(90112 + 55296, 64, 64, F32)
    krevt_v = view(90112 + 55552, 64, 64, F32)
    KCN = ["k00r", "k00i", "k01r", "k01i", "k10r", "k10i", "k11r", "k11i",
           "ta", "tb", "tc", "td", "te", "tf", "atr", "ati"]
    kc_v = {nm: view(90112 + 55808 + i * 256, 64, 64, F32) for i, nm in enumerate(KCN)}
    xt_v = view(C0, 128, 2 * 4096, F32)
    xn_v = view(C0 + 32768, 128, 2 * 4096, F32)
    wbf_v = view(C0, 128, WRING * 2048, BF16)
    pk_v = view(C0 + 49152, 128, 4 * 256, I32)
    wq_v = view(C0 + 53248, 128, 2048, I32)
    sr_v = view(C0 + 69632, 128, 2 * 2048, BF16)
    hbf_v = view(C0 + 77824, 128, 4 * 512, BF16)
    sil_v = view(C0 + 81920, 128, 2 * 512, F32)
    stile_v = view(157696, 128, 2 * 512, F32)
    pkd_v = view(88064, 128, 2 * 512, I32)
    wqd_v = view(92160, 128, 4096, I32)
    zrd_v = view(108544, 128, 2 * 4096, BF16)
    srd_v = view(124928, 128, 2 * 4096, BF16)
    wdb_v = view(141312, 128, 2 * 4096, BF16)
    h_v = view(0, 128, 86 * 512, BF16)
    wt_v = view(88064, 128, 24 * 512, BF16)
    sx_v = view(112640, 128, 8 * 512, F32)
    xx_v = view(129024, 128, 8 * 512, F32)
    ob_v = view(145408, 128, 4 * 512, F32)
    g3_v = view(153600, 128, 2 * 512, F32)
    t3_v = view(157696, 128, 2 * 512, F32)

    ps = [nc.alloc_psum_tensor("ps%d" % i, [128, 512], F32) for i in range(8)]

    SN = ["ld0", "v0", "a0", "t0", "g0", "ph",
          "pk1", "rep1", "dq1", "wf1", "mm1", "ea1", "ev1", "ho1",
          "cmm", "cev", "cso",
          "pkd", "repd", "wdq", "wout",
          "hld", "wt3", "sx3", "mm3", "g3", "e3", "fin", "vch", "mmd"]
    S = {name: Sem(nc, "s_" + name) for name in SN}
    (ld0, v0, a0, t0, g0, ph, pk1, rep1, dq1, wf1, mm1, ea1, ev1, ho1,
     cmm, cev, cso, pkd, repd, wdq, wout, hld, wt3, sx3, mm3, g3s, e3,
     fin, vch, mmd) = [S[n] for n in SN]
    P = Plan()

    def barrier(prewaits=()):
        pw = {e: [] for e in ENGINES}
        for e, s, vv in prewaits:
            assert vv <= s.n
            pw[e].append((s.h, vv))
        for e in ENGINES:
            ph.n += 1
            wl = pw[e]
            if e in ("tensor", "scalar", "vector", "gpsimd"):
                P.ops[e].append((lambda eng: eng.drain(), [], []))
            P.ops[e].append(
                (lambda eng: eng.sem_inc(ph.h, 1), wl, []))
        tgt = ph.n
        for e in ENGINES:
            P.ops[e].append((None, [(ph.h, tgt)], []))

    def vchain(steps, waits0=(), incs_last=()):
        for i, fn in enumerate(steps):
            w = list(waits0) + [(vch, vch.n)] if i == 0 else [(vch, vch.n)]
            inc = list(incs_last) if i == len(steps) - 1 else [(vch, 1)]
            P.op("vector", fn, waits=w, incs=inc)

    def dma(eng, dst, src, sem, waits=()):
        P.op(eng, lambda e, d=dst, s=src: e.dma_start(out=d, in_=s),
             waits=waits, incs=[(sem, 16)])

    small = [(lam_sb[:], lam_h[:]), (vct_sb[:], vct_h[:]), (bp_sb[:], bp_h[:]),
             (ctpq_sb[:], ctpq_h[:]), (dls_sb[:], dls_h[:])]
    small += [(cs[k][:], c_h[k][:]) for k in cs]
    for dst, src in small:
        dma("sync", dst, src, ld0)
    N_SMALL = ld0.n

    dma("sync", zq_gu, qzgu_h.ap().rearrange("(p w) -> p w", p=128), ld0)
    zq_mark = ld0.n
    wvj = zi_gu.rearrange("p (w j) -> p w j", j=8)
    for j in range(8):
        P.op("vector",
             lambda e, j=j: e.tensor_scalar(
                 out=wvj[:, :, j], in0=zq_gu, scalar1=28 - 4 * j, scalar2=28,
                 op0=OP.logical_shift_left, op1=OP.logical_shift_right),
             waits=[(ld0, zq_mark)])
    P.op("vector", lambda e: e.tensor_copy(out=zb_gu, in_=zi_gu), incs=[(v0, 1)])
    dma("sync", sf2_gu, bass.AP(sgun_h, 0, [[5504, 128], [1, 5504]]), ld0)
    sf2_mark = ld0.n
    P.op("vector", lambda e: e.scalar_tensor_tensor(
        out=zb2_gu, in0=zb_gu, scalar=-1.0, in1=sf2_gu, op0=OP.mult, op1=OP.mult),
        waits=[(ld0, sf2_mark), (vch, vch.n)], incs=[(v0, 1)])
    P.op("gpsimd", lambda e: e.dma_start(
        out=zsneg_hbm.ap().rearrange("(p w) -> p w", p=128), in_=zb2_gu),
        waits=[(v0, v0.n)], incs=[(g0, 16)])
    ZPM_G0 = g0.n

    dma("sync", sf_gu, bass.AP(sgupm_h, 0, [[5504, 128], [1, 5504]]), ld0,
        waits=[(g0, ZPM_G0)])
    P.op("vector", lambda e: e.tensor_copy(out=sb_gu, in_=sf_gu),
         waits=[(ld0, ld0.n)], incs=[(v0, 1)])
    P.op("gpsimd", lambda e: e.dma_start(
        out=bass.AP(sgupm_b_hbm, 0, [[5504, 128], [1, 5504]]), in_=sb_gu),
        waits=[(v0, v0.n)], incs=[(g0, 16)])
    SPM_G0 = g0.n

    dma("sync", zq_dn, qzdn_h.ap().rearrange("(p w) -> p w", p=128), ld0,
        waits=[(g0, SPM_G0)])
    dma("sync", sf_dn, bass.AP(sdn_h, 0, [[2752, 128], [1, 2752]]), ld0)
    dn_mark = ld0.n
    wvj2 = zi_dn.rearrange("p (w j) -> p w j", j=8)
    for j in range(8):
        P.op("vector",
             lambda e, j=j: e.tensor_scalar(
                 out=wvj2[:, :, j], in0=zq_dn, scalar1=28 - 4 * j, scalar2=28,
                 op0=OP.logical_shift_left, op1=OP.logical_shift_right),
             waits=[(ld0, dn_mark)])
    P.op("vector", lambda e: e.tensor_copy(out=zb_dn, in_=zi_dn), incs=[(v0, 1)])
    zbdn_v0 = v0.n
    P.op("vector", lambda e: e.tensor_copy(out=sb_dn, in_=sf_dn), incs=[(v0, 1)])
    sbdn_v0 = v0.n
    P.op("gpsimd", lambda e: e.dma_start(
        out=zdn_hbm.ap().rearrange("(p w) -> p w", p=128), in_=zb_dn),
        waits=[(v0, zbdn_v0)], incs=[(g0, 16)])
    P.op("gpsimd", lambda e: e.dma_start(
        out=sdnb_hbm.ap().rearrange("(p w) -> p w", p=128), in_=sb_dn),
        waits=[(v0, sbdn_v0)], incs=[(g0, 16)])
    STAGE_G0 = g0.n

    P.op("scalar", lambda e: e.activation(out=step_sb[:, 0:1], in_=dls_sb[:, 1:2],
                                          func=AF.Exp),
         waits=[(ld0, N_SMALL)])
    P.op("scalar", lambda e: e.drain(), incs=[(a0, 1)])
    vchain([
        lambda e: e.reciprocal(out=step_sb[:, 1:2], in_=step_sb[:, 0:1]),
        lambda e: e.tensor_scalar(out=step_sb[:, 2:3], in0=step_sb[:, 1:2],
                                  scalar1=2.0, scalar2=None, op0=OP.mult),
    ], waits0=[(a0, a0.n)], incs_last=[(v0, 1)])
    P.op("gpsimd", lambda e: e.dma_start(out=bnc_hbm[0:1], in_=step_sb[0:1, 2:3]),
         waits=[(v0, v0.n)], incs=[(g0, 16)])
    P.op("gpsimd", lambda e: e.dma_start(out=bnc_hbm[1:2], in_=dls_sb[0:1, 0:1]),
         incs=[(g0, 16)])
    P.op("gpsimd", lambda e: e.dma_start(
        out=ts64_sb[:], in_=bass.AP(bnc_hbm, 0, [[0, 64], [1, 1]])),
        waits=[(g0, g0.n)], incs=[(g0, 16)])
    P.op("gpsimd", lambda e: e.dma_start(
        out=d128_sb[:], in_=bass.AP(bnc_hbm, 1, [[0, 128], [1, 1]])),
        incs=[(g0, 16)])
    BNC_G0 = g0.n

    P.op("tensor", lambda e: e.matmul(ps[0][:, 0:1], vct_sb[:], bp_sb[:],
                                      start=True, stop=True),
         waits=[(ld0, N_SMALL)])
    P.op("tensor", lambda e: e.drain(), incs=[(t0, 1)])
    P.op("vector", lambda e: e.tensor_copy(out=bst_v, in_=ps[0][:, 0:1]),
         waits=[(t0, t0.n), (g0, STAGE_G0)], incs=[(v0, 1)])
    P.op("gpsimd", lambda e: e.dma_start(out=bnc_hbm[128:256], in_=bst_v),
         waits=[(v0, v0.n)], incs=[(g0, 16)])
    P.op("gpsimd", lambda e: e.dma_start(out=brow_v, in_=bnc_hbm[128:256]),
         waits=[(g0, g0.n)], incs=[(g0, 16)])
    BROW_G0 = g0.n

    ct_re, ct_im = ctpq_sb[:, 0:64], ctpq_sb[:, 64:128]
    p_re, p_im = ctpq_sb[:, 128:192], ctpq_sb[:, 192:256]
    q_re, q_im = ctpq_sb[:, 256:320], ctpq_sb[:, 320:384]
    b_re, b_im = brow_v[:, 0:64], brow_v[:, 64:128]
    w1, w2 = vrow_v[:, 0:64], vrow_v[:, 64:128]
    vre, vim = vrow_v[:, 128:192], vrow_v[:, 192:256]
    vst = vstage_v.rearrange("o (n c) -> o n c", c=8)
    vsteps = []
    for kk, (ar, ai, br, bi) in enumerate(
            [(ct_re, ct_im, b_re, b_im), (ct_re, ct_im, p_re, p_im),
             (q_re, q_im, b_re, b_im), (q_re, q_im, p_re, p_im)]):
        cA, cB = 2 * kk, 2 * kk + 1
        def m1(e, ar=ar, br=br, ai=ai, bi=bi):
            e.tensor_tensor(out=w1, in0=ar, in1=br, op=OP.mult)
            return e.tensor_tensor(out=w2, in0=ai, in1=bi, op=OP.mult)
        def m2(e):
            return e.tensor_tensor(out=vre, in0=w1, in1=w2, op=OP.add)
        def m3(e, ar=ar, bi=bi, ai=ai, br=br):
            e.tensor_tensor(out=w1, in0=ar, in1=bi, op=OP.mult)
            return e.tensor_tensor(out=w2, in0=ai, in1=br, op=OP.mult)
        def m4(e):
            return e.tensor_tensor(out=vim, in0=w1, in1=w2, op=OP.subtract)
        def m5(e, cA=cA, cB=cB):
            e.tensor_copy(out=vst[:, 0:64, cA], in_=vre)
            e.tensor_copy(out=vst[:, 64:128, cA], in_=vim)
            e.tensor_copy(out=vst[:, 0:64, cB], in_=vim)
            return e.tensor_scalar(out=vst[:, 64:128, cB], in0=vre,
                                   scalar1=-1.0, scalar2=None, op0=OP.mult)
        vsteps += [m1, m2, m3, m4, m5]
    vchain(vsteps, waits0=[(g0, BROW_G0)], incs_last=[(v0, 1)])
    P.op("gpsimd", lambda e: e.dma_start(out=bnc_hbm[256:1280], in_=vstage_v),
         waits=[(v0, v0.n)], incs=[(g0, 16)])
    P.op("gpsimd", lambda e: e.dma_start(
        out=vmat_sb[:], in_=bass.AP(bnc_hbm, 256, [[8, 128], [1, 8]])),
        waits=[(g0, g0.n)], incs=[(g0, 16)])
    VMAT_G0 = g0.n

    deno_marks = {}
    for q in range(4):
        sl = slice(q * 1024, q * 1024 + 1024)
        qwl = [(v0, sbdn_v0)] if q == 0 else [(v0, deno_marks[q - 1])]
        dma("sync", wre_v, c_h["wre"][0:64, sl], ld0, waits=qwl)
        dma("sync", wim_v, c_h["wim"][0:64, sl], ld0)
        qld = ld0.n

        def dn1(e):
            e.tensor_scalar(out=dre_v, in0=wre_v, scalar1=ts64_sb[:],
                            scalar2=lam_sb[0:64, :], op0=OP.mult, op1=OP.subtract)
            return e.tensor_scalar(out=dim_v, in0=wim_v, scalar1=ts64_sb[:],
                                   scalar2=lam_sb[64:128, :], op0=OP.mult,
                                   op1=OP.subtract)
        def dn2(e):
            e.tensor_tensor(out=rin_v, in0=dre_v, in1=dre_v, op=OP.mult)
            return e.tensor_tensor(out=tmp_v, in0=dim_v, in1=dim_v, op=OP.mult)
        def dn3(e):
            return e.tensor_tensor(out=rin_v, in0=rin_v, in1=tmp_v, op=OP.add)
        def dn4(e):
            return e.reciprocal(out=rin_v, in_=rin_v)
        def dn5(e):
            e.tensor_tensor(out=rcat_v[0:64, :], in0=dre_v, in1=rin_v, op=OP.mult)
            return e.tensor_tensor(out=rcat_v[64:128, :], in0=dim_v, in1=rin_v,
                                   op=OP.mult)
        vchain([dn1, dn2, dn3, dn4, dn5],
               waits0=[(ld0, qld), (g0, BNC_G0), (t0, t0.n)], incs_last=[(v0, 1)])
        deno_marks[q] = v0.n
        for half in range(2):
            P.op("tensor", lambda e, hh=half: e.matmul(
                ps[1][0:8, :], vmat_sb[:], rcat_v[:, hh * 512:hh * 512 + 512],
                start=True, stop=True),
                waits=[(v0, v0.n), (cev, cev.n), (g0, VMAT_G0)])
            P.op("tensor", lambda e: e.drain(), incs=[(t0, 1)])
            P.op("vector", lambda e, q=q, hh=half: e.tensor_copy(
                out=kk8_v[:, (q * 2 + hh) * 512:(q * 2 + hh) * 512 + 512],
                in_=ps[1][0:8, :]),
                waits=[(t0, t0.n)], incs=[(cev, 1)])

    P.op("gpsimd", lambda e: e.dma_start(out=kk_hbm[:], in_=kk8_v),
         waits=[(cev, cev.n)], incs=[(g0, 16)])
    for r, nm in enumerate(KCN[:8]):
        P.op("gpsimd", lambda e, r=r, nm=nm: e.dma_start(
            out=kc_v[nm], in_=kk_hbm[r].rearrange("(a b) -> a b", a=64)),
            waits=[(g0, g0.n)], incs=[(g0, 16)])
    KC_G0 = g0.n

    K_ = kc_v
    def _tt(o, a, b, op):
        return lambda e: e.tensor_tensor(out=K_[o], in0=K_[a], in1=K_[b], op=op)
    at_steps = [
        lambda e: e.tensor_scalar(out=K_["ta"], in0=K_["k11r"], scalar1=1.0,
                                  scalar2=None, op0=OP.add),
        _tt("tb", "ta", "ta", OP.mult),
        _tt("tc", "k11i", "k11i", OP.mult),
        _tt("tb", "tb", "tc", OP.add),
        lambda e: e.reciprocal(out=K_["tb"], in_=K_["tb"]),
        _tt("tc", "ta", "tb", OP.mult),
        _tt("td", "k11i", "tb", OP.mult),
        _tt("ta", "k01r", "tc", OP.mult),
        _tt("tb", "k01i", "td", OP.mult),
        _tt("te", "ta", "tb", OP.add),
        _tt("ta", "k01i", "tc", OP.mult),
        _tt("tb", "k01r", "td", OP.mult),
        _tt("tf", "ta", "tb", OP.subtract),
        _tt("ta", "te", "k10r", OP.mult),
        _tt("tb", "tf", "k10i", OP.mult),
        _tt("ta", "ta", "tb", OP.subtract),
        _tt("tc", "te", "k10i", OP.mult),
        _tt("td", "tf", "k10r", OP.mult),
        _tt("tc", "tc", "td", OP.add),
        _tt("ta", "k00r", "ta", OP.subtract),
        _tt("tc", "k00i", "tc", OP.subtract),
        lambda e: e.tensor_tensor(out=K_["te"], in0=cs["cre64"][:], in1=K_["ta"], op=OP.mult),
        lambda e: e.tensor_tensor(out=K_["tf"], in0=cs["cim64"][:], in1=K_["tc"], op=OP.mult),
        _tt("atr", "te", "tf", OP.subtract),
        lambda e: e.tensor_tensor(out=K_["te"], in0=cs["cre64"][:], in1=K_["tc"], op=OP.mult),
        lambda e: e.tensor_tensor(out=K_["tf"], in0=cs["cim64"][:], in1=K_["ta"], op=OP.mult),
        _tt("ati", "te", "tf", OP.add),
        lambda e: e.tensor_copy(out=ast_v[0:64, :], in_=K_["atr"]),
        lambda e: e.tensor_copy(out=ast_v[64:128, :], in_=K_["ati"]),
    ]
    vchain(at_steps, waits0=[(g0, KC_G0)], incs_last=[(v0, 1)])
    if debug:
        ast_dbg = nc.dram_tensor("ast_dbg", [128, 64], F32, kind="ExternalOutput")
        rc_dbg = nc.dram_tensor("rc_dbg", [128, 1024], F32, kind="ExternalOutput")
        P.op("gpsimd", lambda e: e.dma_start(out=ast_dbg[:], in_=ast_v),
             waits=[(v0, v0.n)], incs=[(g0, 16)])
        P.op("gpsimd", lambda e: e.dma_start(out=rc_dbg[:], in_=rcat_v),
             waits=[(v0, v0.n)], incs=[(g0, 16)])
    P.op("tensor", lambda e: e.matmul(ps[1][0:64, 0:128], ast_v, cs["E1c"][:],
                                      start=True, stop=True),
         waits=[(v0, v0.n)])
    P.op("tensor", lambda e: e.drain(), incs=[(t0, 1)])
    P.op("vector", lambda e: e.tensor_copy(out=s1_v, in_=ps[1][0:64, 0:128]),
         waits=[(t0, t0.n)], incs=[(v0, 1)])

    s1r, s1i = s1_v[:, 0:64], s1_v[:, 64:128]
    twa, twb = kc_v["ta"], kc_v["tb"]
    tw_steps = [
        lambda e: (e.tensor_tensor(out=twa, in0=s1r, in1=cs["TWr"][:], op=OP.mult),
                   e.tensor_tensor(out=twb, in0=s1i, in1=cs["TWi"][:], op=OP.mult))[-1],
        lambda e: e.tensor_tensor(out=s2_v[0:64, :], in0=twa, in1=twb, op=OP.subtract),
        lambda e: (e.tensor_tensor(out=twa, in0=s1r, in1=cs["TWi"][:], op=OP.mult),
                   e.tensor_tensor(out=twb, in0=s1i, in1=cs["TWr"][:], op=OP.mult))[-1],
        lambda e: e.tensor_tensor(out=s2_v[64:128, :], in0=twa, in1=twb, op=OP.add),
    ]
    vchain(tw_steps, incs_last=[(v0, 1)])
    P.op("tensor", lambda e: e.matmul(ps[1][0:64, 0:64], s2_v, cs["E2c"][:],
                                      start=True, stop=True),
         waits=[(v0, v0.n)])
    P.op("tensor", lambda e: e.drain(), incs=[(t0, 1)])
    P.op("vector", lambda e: e.tensor_scalar(
        out=krev_v, in0=ps[1][0:64, 0:64], scalar1=1.0 / 4096.0, scalar2=None,
        op0=OP.mult), waits=[(t0, t0.n)], incs=[(v0, 1)])
    P.op("tensor", lambda e: e.transpose(ps[2][0:64, 0:64], krev_v,
                                         cs["identf"][0:64, 0:64]),
         waits=[(v0, v0.n)])
    P.op("tensor", lambda e: e.drain(), incs=[(t0, 1)])
    P.op("vector", lambda e: e.tensor_copy(out=krevt_v, in_=ps[2][0:64, 0:64]),
         waits=[(t0, t0.n)], incs=[(v0, 1)])
    P.op("vector", lambda e: e.memset(vrow_v[:, 0:256], 0.0), incs=[(v0, 1)])
    P.op("gpsimd", lambda e: e.dma_start(
        out=bass.AP(krev_hbm, 0, [[64, 64], [1, 64]]), in_=krevt_v),
        waits=[(v0, v0.n)], incs=[(g0, 16)])
    P.op("gpsimd", lambda e: e.dma_start(out=krev_hbm[4096:4352],
                                         in_=vrow_v[0:1, 0:256]),
         incs=[(g0, 16)])
    KREV_G0 = g0.n

    P.op("vector", lambda e: e.memset(tbuf[:, 0:512], 0.0), incs=[(v0, 1)])
    hk_marks = []
    for d in range(32):
        wl = [(g0, KREV_G0)]
        if d >= 2:
            wl.append((v0, hk_marks[d - 2]))
        dma("sync", hank_sb[:, (d % 2) * 128:(d % 2) * 128 + 128],
            bass.AP(krev_hbm, 3968 - 128 * d, [[1, 128], [1, 128]]), ld0,
            waits=wl)
        mk = ld0.n
        P.op("vector", lambda e, d=d: e.tensor_copy(
            out=tbuf[:, 512 + 128 * d:640 + 128 * d],
            in_=bass.AP(hank_sb, (d % 2) * 128 + 127, [[256, 128], [-1, 128]])),
            waits=[(ld0, mk)], incs=[(v0, 1)])
        hk_marks.append(v0.n)
    P.op("vector", lambda e: e.scalar_tensor_tensor(
        out=tbuf[:, 512:640],
        in0=cs["identb"][:], scalar=d128_sb[:], in1=tbuf[:, 512:640],
        op0=OP.mult, op1=OP.add),
        waits=[(g0, BNC_G0)], incs=[(v0, 1)])
    TBUF_V0 = v0.n
    TBUF_T0 = t0.n

    aff_marks = {}
    tp_marks = {}
    tpe_marks = {}
    for tt in range(8):
        slot = tt % 2
        xts = xt_v[:, slot * 4096:slot * 4096 + 4096]
        xns = xn_v[:, slot * 4096:slot * 4096 + 4096]
        wl = [(v0, TBUF_V0)] if tt < 2 else [(v0, aff_marks[tt - 2])]
        dma("sync", xts, x_h[tt * 128:tt * 128 + 128, :], ld0, waits=wl)
        xmk = ld0.n
        awl = [(ld0, xmk)]
        if tt >= 2:
            awl.append((t0, tpe_marks[tt - 2]))
        else:
            awl += [(v0, TBUF_V0), (t0, TBUF_T0), (g0, KREV_G0)]
        P.op("scalar", lambda e, xns=xns, xts=xts: e.activation(
            out=xns, in_=xts, func=AF.Square), waits=awl)
        P.op("scalar", lambda e: e.drain(), incs=[(a0, 1)])
        amk = a0.n

        def st1(e, xts=xts, xns=xns):
            e.tensor_reduce(out=mrow_sb[:, 0:1], in_=xts, axis=XAX, op=OP.add)
            return e.tensor_reduce(out=mrow_sb[:, 1:2], in_=xns, axis=XAX, op=OP.add)
        def st2(e):
            e.tensor_scalar(out=mrow_sb[:, 2:3], in0=mrow_sb[:, 0:1],
                            scalar1=1.0 / 4096.0, scalar2=None, op0=OP.mult)
            return e.tensor_scalar(out=mrow_sb[:, 3:4], in0=mrow_sb[:, 1:2],
                                   scalar1=1.0 / 4096.0, scalar2=None, op0=OP.mult)
        def st3(e):
            return e.tensor_tensor(out=mrow_sb[:, 4:5], in0=mrow_sb[:, 2:3],
                                   in1=mrow_sb[:, 2:3], op=OP.mult)
        def st4(e):
            return e.tensor_tensor(out=mrow_sb[:, 5:6], in0=mrow_sb[:, 3:4],
                                   in1=mrow_sb[:, 4:5], op=OP.subtract)
        vchain([st1, st2, st3, st4], waits0=[(ld0, xmk), (a0, amk)],
               incs_last=[(v0, 1)])
        P.op("scalar", lambda e: e.activation(out=mrow_sb[:, 6:7],
                                              in_=mrow_sb[:, 5:6],
                                              func=AF.Sqrt, bias=np.float32(1e-5)),
             waits=[(v0, v0.n)])
        P.op("scalar", lambda e: e.drain(), incs=[(a0, 1)])

        def rs1(e):
            return e.reciprocal(out=mrow_sb[:, 7:8], in_=mrow_sb[:, 6:7])
        def rs2(e):
            return e.tensor_tensor(out=mrow_sb[:, 0:1], in0=mrow_sb[:, 2:3],
                                   in1=mrow_sb[:, 7:8], op=OP.mult)
        def rs3(e):
            return e.tensor_scalar(out=mrow_sb[:, 1:2], in0=mrow_sb[:, 0:1],
                                   scalar1=-1.0, scalar2=None, op0=OP.mult)
        vchain([rs1, rs2, rs3], waits0=[(a0, a0.n)], incs_last=[(v0, 1)])
        P.op("vector", lambda e, tt=tt: e.tensor_copy(
            out=mrowall_sb[:, tt * 8:tt * 8 + 8], in_=mrow_sb[:, 0:8]),
            waits=[(v0, v0.n)])
        P.op("vector", lambda e, xns=xns, xts=xts: e.tensor_scalar(
            out=xns, in0=xts, scalar1=mrow_sb[:, 7:8], scalar2=mrow_sb[:, 1:2],
            op0=OP.mult, op1=OP.add), waits=[(v0, v0.n)], incs=[(v0, 1)])
        aff_marks[tt] = v0.n
        if debug and tt == 2:
            P.op("gpsimd", lambda e, xts=xts: e.dma_start(out=xt2_dbg[:], in_=xts),
                 waits=[(v0, aff_marks[tt])], incs=[(g0, 16)])
            P.op("gpsimd", lambda e, xns=xns: e.dma_start(out=xn2_dbg[:], in_=xns),
                 waits=[(v0, aff_marks[tt])], incs=[(g0, 16)])
        for gi in range(8):
            bank = ps[6 + gi % 2]
            twl = [(v0, aff_marks[tt])]
            if (tt, gi) > (0, 1):
                pj = (tt, gi - 2) if gi >= 2 else (tt - 1, 6 + gi)
                twl.append((v0, tp_marks[pj]))
            for i in range(4):
                jc = gi * 4 + i
                P.op("tensor", lambda e, xns=xns, jc=jc, i=i, bank=bank: e.transpose(
                    bank[:, i * 128:i * 128 + 128],
                    xns[:, jc * 128:jc * 128 + 128], cs["identf"][:]),
                    waits=twl if i == 0 else [])
            P.op("tensor", lambda e: e.drain(), incs=[(t0, 1)])
            tmark = t0.n
            P.op("vector", lambda e, gi=gi, tt=tt, bank=bank: e.tensor_copy(
                out=xnT[:, gi * 4:gi * 4 + 4, tt * 128:tt * 128 + 128],
                in_=bank[:, 0:512].rearrange("p (a b) -> p a b", a=4)),
                waits=[(t0, tmark)], incs=[(v0, 1)])
            tp_marks[(tt, gi)] = v0.n
        tpe_marks[tt] = t0.n

    if debug:
        P.op("gpsimd", lambda e: e.dma_start(out=mrow_dbg[:], in_=mrowall_sb[:]),
             waits=[(v0, v0.n)], incs=[(g0, 16)])
        xnT_dbg = nc.dram_tensor("xnT_dbg", [128, 32768], BF16, kind="ExternalOutput")
        tb_dbg = nc.dram_tensor("tbuf_dbg", [128, 4608], BF16, kind="ExternalOutput")
        P.op("gpsimd", lambda e: e.dma_start(
            out=xnT_dbg[:], in_=xnT.rearrange("p j t -> p (j t)")),
            waits=[(v0, v0.n)], incs=[(g0, 16)])
        P.op("gpsimd", lambda e: e.dma_start(out=tb_dbg[:], in_=tbuf),
            waits=[(v0, v0.n)], incs=[(g0, 16)])
    for half in range(2):
        for kc in range(32):
            P.op("tensor", lambda e, kc=kc, half=half: e.matmul(
                ps[4][0:32, :], cs["esl"][:, 31 - kc:63 - kc],
                xnT[:, kc, half * 512:half * 512 + 512],
                start=(kc == 0), stop=(kc == 31)),
                waits=[(v0, v0.n)] if kc == 0 else [])
        P.op("tensor", lambda e: e.drain(), incs=[(t0, 1)])
        P.op("vector", lambda e, half=half: e.tensor_copy(
            out=xt32_sb[0:32, half * 512:half * 512 + 512], in_=ps[4][0:32, :]),
            waits=[(t0, t0.n)], incs=[(v0, 1)])
    XT_DONE = v0.n

    barrier(prewaits=[("gpsimd", g0, g0.n), ("sync", ld0, ld0.n),
                      ("vector", v0, v0.n), ("scalar", a0, a0.n),
                      ("tensor", t0, t0.n)])

    qw4 = qwgu_h.ap().rearrange("(kc p) w -> p kc w", p=128)
    spm = sgupm_b_hbm.ap().rearrange("(pr kc l) -> pr kc l", kc=32, l=512)
    pkr = pk_v.rearrange("p (s g k w) -> p s g k w", s=4, g=2, k=4)
    wqr = wq_v.rearrange("p (k g w j) -> p k g w j", k=4, g=2, w=32)
    wbr = wbf_v.rearrange("p (s n) -> p s n", s=WRING)

    pk_marks, rep_marks, dq_marks = {}, {}, {}
    mm1_marks, ea_marks, ev_marks, ho_marks = {}, {}, {}, {}

    def p1_dmas(p, hq):
        G = p * 8 + hq
        s4 = G % 4
        wl = [(dq1, G - 3)] if G >= 4 else []
        for gi, woff in ((0, p * 32), (1, 1376 + p * 32)):
            dma("sync", pkr[:, s4, gi],
                qw4[:, 4 * hq:4 * hq + 4, woff:woff + 32], pk1, waits=wl)
        pk_marks[G] = pk1.n
        wl = [(dq1, G - 1)] if G >= 2 else []
        s2 = G % 2
        srr = sr_v.rearrange("p (s a l) -> p s a l", s=2, a=4)
        dma("sync", srr[:, s2],
            spm[p, 4 * hq:4 * hq + 4, :].partition_broadcast(128), rep1, waits=wl)
        rep_marks[G] = rep1.n

    def p1_deq(p, hq):
        G = p * 8 + hq
        s4, s2, sw = G % 4, G % 2, G % WRING

        def unpack(e):
            last = None
            for gi in range(2):
                for j in range(8):
                    last = e.tensor_scalar(
                        out=wqr[:, :, gi, :, j], in0=pkr[:, s4, gi],
                        scalar1=28 - 4 * j, scalar2=28,
                        op0=OP.logical_shift_left, op1=OP.logical_shift_right)
            return last

        wl = [(pk1, pk_marks[G])]
        if G >= WRING:
            wl.append((wf1, G - WRING + 1))
        P.op("vector", unpack, waits=wl, incs=[(vch, 1)])
        P.op("vector", lambda e: e.tensor_copy(out=wbr[:, sw], in_=wq_v),
             waits=[(vch, vch.n)], incs=[(vch, 1)])
        P.op("vector", lambda e: e.tensor_tensor(
            out=wbr[:, sw], in0=wbr[:, sw],
            in1=sr_v[:, s2 * 2048:s2 * 2048 + 2048], op=OP.mult),
            waits=[(rep1, rep_marks[G]), (vch, vch.n)], incs=[(dq1, 1)])
        dq_marks[G] = dq1.n

    def p1_zsn(p):
        wl = [(mm1, 2 * (p - 1))] if p >= 2 else []
        zv = zsneg_hbm.ap().rearrange("(g n) -> g n", n=22016)
        dma("sync", zsn_sb[0:32, (p % 2) * 512:(p % 2) * 512 + 256],
            zv[:, p * 256:p * 256 + 256], rep1, waits=wl)
        dma("sync", zsn_sb[0:32, (p % 2) * 512 + 256:(p % 2) * 512 + 512],
            zv[:, 11008 + p * 256:11008 + p * 256 + 256], rep1)
        zsn_marks[p] = rep1.n

    def p1_pe(p, half):
        u = 2 * p + half
        st = u % 2
        for c in range(4):
            cwl = [(rep1, zsn_marks[p]), (v0, XT_DONE)] if c == 0 else []
            if c == 0 and u >= 2:
                cwl += [(ea1, ea_marks[u - 2]), (ev1, ev_marks[u - 2])]
            P.op("tensor", lambda e, c=c, st=st, half=half, p=p: e.matmul(
                ps[4 * st + c][:],
                zsn_sb[0:32, (p % 2) * 512 + c * 128:(p % 2) * 512 + c * 128 + 128],
                xt32_sb[0:32, half * 512:half * 512 + 512],
                start=True, stop=False), waits=cwl)
        for kc in range(32):
            G = p * 8 + kc // 4
            sw = G % WRING
            wl = []
            if kc % 4 == 0:
                wl = [(dq1, dq_marks[G])]
            for c in range(4):
                P.op("tensor", lambda e, sw=sw, kc=kc, c=c, half=half, st=st: e.matmul(
                    ps[4 * st + c][:],
                    wbr[:, sw, (kc % 4) * 512 + c * 128:(kc % 4) * 512 + c * 128 + 128],
                    xnT[:, kc, half * 512:half * 512 + 512],
                    start=False, stop=(kc == 31)),
                    waits=wl if c == 0 else [])
        P.op("tensor", lambda e: e.drain(), incs=[(mm1, 1)])
        if half == 1:
            P.op("tensor", lambda e: e.nop(), incs=[(wf1, 8)])
        mm1_marks[u] = mm1.n

    def p1_act(p, half):
        u = 2 * p + half
        st = u % 2
        for c01 in range(2):
            wl = []
            if c01 == 0:
                wl.append((mm1, mm1_marks[u]))
                if u >= 1:
                    wl.append((ev1, ev_marks[u - 1]))
            P.op("scalar", lambda e, st=st, c01=c01: e.activation(
                out=sil_v[:, c01 * 512:c01 * 512 + 512], in_=ps[4 * st + c01][:],
                func=AF.Silu),
                waits=wl)
        P.op("scalar", lambda e: e.drain(), incs=[(ea1, 1)])
        ea_marks[u] = ea1.n

    def p1_evac(p, half):
        u = 2 * p + half
        st = u % 2
        for c01 in range(2):
            hs = 2 * (u % 2) + c01
            wl = [(ea1, ea_marks[u])]
            if u >= 2:
                wl.append((ho1, ho_marks.get((u - 2, c01), 0)))
            P.op("vector", lambda e, hs=hs, st=st, c01=c01: e.tensor_tensor(
                out=hbf_v[:, hs * 512:hs * 512 + 512],
                in0=sil_v[:, c01 * 512:c01 * 512 + 512],
                in1=ps[4 * st + 2 + c01][:], op=OP.mult),
                waits=wl, incs=[(ev1, 1)])
        ev_marks[u] = ev1.n

    def p1_hout(p, half):
        u = 2 * p + half
        for c01 in range(2):
            hs = 2 * (u % 2) + c01
            chunk = 2 * p + c01
            P.op("gpsimd", lambda e, hs=hs, chunk=chunk, half=half: e.dma_start(
                out=h_hbm[chunk * 128:chunk * 128 + 128,
                          half * 512:half * 512 + 512],
                in_=hbf_v[:, hs * 512:hs * 512 + 512]),
                waits=[(ev1, ev_marks[u] - 1 + c01)], incs=[(ho1, 16)])
            ho_marks[(u, c01)] = ho1.n

    zsn_marks = {}
    for p in range(NPAIR + 1):
        if p < NPAIR:
            p1_zsn(p)
            for hq in range(4):
                p1_dmas(p, hq)
                p1_deq(p, hq)
        if p >= 1:
            p1_pe(p - 1, 0)
            p1_act(p - 1, 0)
            p1_evac(p - 1, 0)
            p1_hout(p - 1, 0)
            p1_pe(p - 1, 1)
        if p < NPAIR:
            for hq in range(4, 8):
                p1_dmas(p, hq)
                p1_deq(p, hq)
        if p >= 1:
            p1_act(p - 1, 1)
            p1_evac(p - 1, 1)
            p1_hout(p - 1, 1)

    barrier(prewaits=[("gpsimd", ho1, ho1.n), ("vector", ev1, ev1.n),
                      ("scalar", ea1, ea1.n), ("tensor", ev1, ev1.n),
                      ("sync", rep1, rep1.n)])


    hre = h_hbm.ap().rearrange("(c p) t -> p c t", p=128)
    qd4 = qwdn_h.ap().rearrange("(kc p) w -> p kc w", p=128)
    wqdr = wqd_v.rearrange("p (w j) -> p w j", j=8)
    pkd_marks, repd_marks, wdq_marks = {}, {}, {}
    def p25_chunk(cch):
        s2 = cch % 2
        wl = [(wdq, cch - 1)] if cch >= 2 else []
        dma("sync", pkd_v[:, s2 * 512:s2 * 512 + 512], qd4[:, cch, :], pkd,
            waits=wl)
        pkd_marks[cch] = pkd.n
        dma("sync", zrd_v[:, s2 * 4096:s2 * 4096 + 4096],
            zdn2[cch:cch + 1, :].partition_broadcast(128).squeeze(1),
            repd, waits=wl)
        dma("sync", srd_v[:, s2 * 4096:s2 * 4096 + 4096],
            sdnb2[cch:cch + 1, :].partition_broadcast(128).squeeze(1), repd)
        repd_marks[cch] = repd.n

        def unpackd(e, s2=s2):
            last = None
            for j in range(8):
                last = e.tensor_scalar(
                    out=wqdr[:, :, j], in0=pkd_v[:, s2 * 512:s2 * 512 + 512],
                    scalar1=28 - 4 * j, scalar2=28,
                    op0=OP.logical_shift_left, op1=OP.logical_shift_right)
            return last

        wl = [(pkd, pkd_marks[cch])]
        if cch >= 2:
            wl.append((wout, 16 * (cch - 1)))
        P.op("vector", unpackd, waits=wl, incs=[(vch, 1)])
        P.op("vector", lambda e, s2=s2: e.tensor_tensor(
            out=wdb_v[:, s2 * 4096:s2 * 4096 + 4096], in0=wqd_v,
            in1=zrd_v[:, s2 * 4096:s2 * 4096 + 4096], op=OP.subtract),
            waits=[(repd, repd_marks[cch]), (vch, vch.n)], incs=[(vch, 1)])
        P.op("vector", lambda e, s2=s2: e.tensor_tensor(
            out=wdb_v[:, s2 * 4096:s2 * 4096 + 4096],
            in0=wdb_v[:, s2 * 4096:s2 * 4096 + 4096],
            in1=srd_v[:, s2 * 4096:s2 * 4096 + 4096], op=OP.mult),
            waits=[(vch, vch.n)], incs=[(wdq, 1)])
        wdq_marks[cch] = wdq.n
        P.op("gpsimd", lambda e, cch=cch, s2=s2: e.dma_start(
            out=wdn_hbm[cch * 128:cch * 128 + 128, :],
            in_=wdb_v[:, s2 * 4096:s2 * 4096 + 4096]),
            waits=[(wdq, wdq_marks[cch])], incs=[(wout, 16)])

    cev_marks, cmm_marks = {}, {}
    p25_done = 0
    for k in range(64):
        tt, ip = k // 8, k % 8
        bank = ps[6 + k % 2]
        njc = min(32, 4 * ip + 4)
        wl = [(cev, cev_marks.get(k - 2, 0))]
        for jc in range(njc):
            P.op("tensor", lambda e, jc=jc, tt=tt, ip=ip, bank=bank: e.matmul(
                bank[:], xnT[:, jc, tt * 128:tt * 128 + 128],
                tbuf[:, 512 + 128 * (4 * ip - jc):512 + 128 * (4 * ip - jc) + 512],
                start=(jc == 0), stop=(jc == njc - 1)),
                waits=wl if jc == 0 else [])
        P.op("tensor", lambda e: e.drain(), incs=[(cmm, 1)])
        cmm_marks[k] = cmm.n
        P.op("vector", lambda e, k=k, bank=bank: e.tensor_copy(
            out=stile_v[:, (k % 2) * 512:(k % 2) * 512 + 512], in_=bank[:]),
            waits=[(cmm, cmm_marks[k]), (cso, 16 * (k - 1))], incs=[(cev, 1)])
        cev_marks[k] = cev.n
        P.op("gpsimd", lambda e, k=k, tt=tt, ip=ip: e.dma_start(
            out=s_hbm[tt * 128:tt * 128 + 128, ip * 512:ip * 512 + 512],
            in_=stile_v[:, (k % 2) * 512:(k % 2) * 512 + 512]),
            waits=[(cev, cev_marks[k])], incs=[(cso, 16)])
        want = (k + 1) * 86 // 64
        while p25_done < want:
            p25_chunk(p25_done)
            p25_done += 1


    hvr0 = h_v.rearrange("p (c t) -> p c t", c=86)
    for hc in range(86):
        dma("sync", hvr0[:, hc], hre[:, hc, 0:512], hld,
            waits=[(cmm, 64)] if hc == 0 else [])

    barrier(prewaits=[("gpsimd", wout, wout.n), ("vector", wdq, wdq.n),
                      ("gpsimd", cso, cso.n), ("vector", cev, cev.n),
                      ("tensor", cev, cev.n)])

    g3_marks, e3_marks, sx_marks, fin_marks = {}, {}, {}, {}
    mm3_marks = {}

    def p3_sx(u):
        half, panel = u // 8, u % 8
        for t in range(4):
            ttg = half * 4 + t
            slot = (u * 4 + t) % 8
            wl = []
            if u >= 2:
                wl = [(e3, e3_marks.get((u - 2, t), 0))]
            dma("scalar", sx_v[:, slot * 512:slot * 512 + 512],
                s_hbm[ttg * 128:ttg * 128 + 128, panel * 512:panel * 512 + 512],
                sx3, waits=wl)
            dma("scalar", xx_v[:, slot * 512:slot * 512 + 512],
                x_h[ttg * 128:ttg * 128 + 128, panel * 512:panel * 512 + 512],
                sx3)
            sx_marks[(u, t)] = sx3.n

    for u in range(16):
        half, panel = u // 8, u % 8
        if panel == 0 and half == 1:
            wl = [(mmd, 8 * half)]
            hvr = h_v.rearrange("p (c t) -> p c t", c=86)
            for hc in range(86):
                dma("sync", hvr[:, hc], hre[:, hc, half * 512:half * 512 + 512],
                    hld, waits=wl if hc == 0 else [])
        if u == 0:
            p3_sx(0)
        st = u % 2
        for cch in range(86):
            G = u * 86 + cch
            if G == 0:
                for Gp in range(2):
                    dma("sync", wt_v[:, (Gp % 24) * 512:(Gp % 24) * 512 + 512],
                        wdn_hbm[Gp * 128:Gp * 128 + 128, 0:512], wt3)
            Gn = G + 2
            if Gn < 16 * 86:
                un, cn = Gn // 86, Gn % 86
                gw = [(mm3, Gn - 23)] if Gn >= 24 else []
                dma("sync", wt_v[:, (Gn % 24) * 512:(Gn % 24) * 512 + 512],
                    wdn_hbm[cn * 128:cn * 128 + 128,
                            (un % 8) * 512:(un % 8) * 512 + 512], wt3, waits=gw)
            wl = [(wt3, 16 * min(G + 2, 16 * 86))]
            if cch == 0:
                wl.append((hld, 16 * 86 * (half + 1)))
                if u >= 1:
                    wl.append((g3s, g3s.n))
                    wl.append((e3, e3.n))
                    wl.append((fin, fin.n))
            sw = G % 24
            for t in range(4):
                P.op("tensor", lambda e, sw=sw, cch=cch, t=t, st=st, half=half: e.matmul(
                    ps[4 * st + t][:],
                    h_v[:, cch * 512 + t * 128:cch * 512 + t * 128 + 128],
                    wt_v[:, sw * 512:sw * 512 + 512],
                    start=(cch == 0), stop=(cch == 85)),
                    waits=wl if t == 0 else [],
                    incs=[(mm3, 1)] if t == 3 else [])
        P.op("tensor", lambda e: e.drain(), incs=[(mmd, 1)])
        mm3_marks[u] = mmd.n
        if u < 15:
            p3_sx(u + 1)
        for t in range(4):
            gsl = t % 2
            wl = [(mmd, mm3_marks[u])] if t == 0 else []
            if t >= 2:
                wl.append((e3, e3_marks[(u, t - 2)]))
            elif u >= 1:
                wl.append((e3, e3_marks[(u - 1, t + 2)]))
            P.op("scalar", lambda e, gsl=gsl, st=st, t=t: e.activation(
                out=g3_v[:, gsl * 512:gsl * 512 + 512], in_=ps[4 * st + t][:],
                func=AF.Silu), waits=wl)
            P.op("scalar", lambda e: e.drain(), incs=[(g3s, 1)])
            g3_marks[u] = g3s.n
            slot = (u * 4 + t) % 8
            wl = [(g3s, g3s.n), (sx3, sx_marks[(u, t)])]
            if u >= 1:
                wl.append((fin, fin_marks.get((u - 1, t), 0)))
            P.op("vector", lambda e, gsl=gsl, slot=slot: e.tensor_tensor(
                out=t3_v[:, gsl * 512:gsl * 512 + 512],
                in0=g3_v[:, gsl * 512:gsl * 512 + 512],
                in1=sx_v[:, slot * 512:slot * 512 + 512], op=OP.mult),
                waits=wl, incs=[(vch, 1)])
            P.op("vector", lambda e, gsl=gsl, slot=slot, t=t: e.tensor_tensor(
                out=ob_v[:, t * 512:t * 512 + 512],
                in0=t3_v[:, gsl * 512:gsl * 512 + 512],
                in1=xx_v[:, slot * 512:slot * 512 + 512], op=OP.add),
                waits=[(vch, vch.n)], incs=[(e3, 1)])
            e3_marks[(u, t)] = e3.n
            ttg = half * 4 + t
            P.op("gpsimd", lambda e, t=t, ttg=ttg, panel=panel: e.dma_start(
                out=out_h[ttg * 128:ttg * 128 + 128, panel * 512:panel * 512 + 512],
                in_=ob_v[:, t * 512:t * 512 + 512]),
                waits=[(e3, e3_marks[(u, t)])], incs=[(fin, 16)])
            fin_marks[(u, t)] = fin.n

    P.op("gpsimd", lambda e: e.nop(), waits=[(fin, fin.n)])

    with nc.Block() as block:
        for ename in ENGINES:
            def emit(eng, ename=ename):
                for fn, wl, il in P.ops[ename]:
                    for h, vv in wl:
                        eng.wait_ge(h, vv)
                    if fn is not None:
                        inst = fn(eng)
                        for h, k in il:
                            inst.then_inc(h, k)
            getattr(block, ename)(emit)

    return nc


_PROG = None


def kernel(**inputs):
    global _PROG
    if _PROG is None:
        _PROG = build_program()
    nc = _PROG

    f = lambda a: np.ascontiguousarray(a, dtype=np.float32)
    x = f(inputs["L0_attn_u"]).reshape(8192, 4096)
    lam = np.concatenate([f(inputs["L0_attn_Lambda_ri"][:, 0]),
                          f(inputs["L0_attn_Lambda_ri"][:, 1])]).reshape(128, 1)
    vct = np.concatenate([f(inputs["L0_attn_Vc_ri"][..., 0]).T,
                          f(inputs["L0_attn_Vc_ri"][..., 1]).T], axis=1)
    ctpq = np.concatenate([
        f(inputs["L0_attn_Ct_ri"][0, :, 0]), f(inputs["L0_attn_Ct_ri"][0, :, 1]),
        f(inputs["L0_attn_p_ri"][:, 0]), f(inputs["L0_attn_p_ri"][:, 1]),
        f(inputs["L0_attn_q_ri"][:, 0]), f(inputs["L0_attn_q_ri"][:, 1])]
    ).reshape(1, 384)
    dls = np.array([[inputs["L0_attn_D"][0], inputs["L0_attn_log_step"][0]]],
                   dtype=np.float32)
    sg = f(inputs["L0_ffn_scales_gu"])
    sgupm = np.empty((NPAIR, 32, 512), np.float32)
    for p in range(NPAIR):
        sgupm[p, :, 0:256] = sg[:, p * 256:p * 256 + 256]
        sgupm[p, :, 256:512] = sg[:, 11008 + p * 256:11008 + p * 256 + 256]
    base = {
        "qwgu": np.ascontiguousarray(inputs["L0_ffn_qweight_gu"], np.int32),
        "qzgu": np.ascontiguousarray(inputs["L0_ffn_qzeros_gu"], np.int32).reshape(-1),
        "sgupm": sgupm.reshape(-1),
        "sgun": sg.reshape(-1),
        "qwdn": np.ascontiguousarray(inputs["L0_ffn_qweight_dn"], np.int32),
        "qzdn": np.ascontiguousarray(inputs["L0_ffn_qzeros_dn"], np.int32).reshape(-1),
        "sdn": f(inputs["L0_ffn_scales_dn"]).reshape(-1),
        "lam": lam, "vct": np.ascontiguousarray(vct),
        "bp": f(inputs["L0_attn_Bp"]), "ctpq": ctpq, "dls": dls,
    }
    for k, v in _CONSTS.items():
        base["c_" + k] = v

    in_maps = []
    for c in range(8):
        m = dict(base)
        m["x"] = np.ascontiguousarray(x[c * 1024:(c + 1) * 1024])
        in_maps.append(m)

    res = run_bass_kernel_spmd(nc, in_maps, core_ids=list(range(8)))
    out = np.concatenate([np.asarray(r["out"]) for r in res.results], axis=0)
    return out.reshape(2, 4096, 4096).astype(np.float32)


# version 10
# speedup vs baseline: 1.1424x; 1.1424x over previous
import numpy as np
import ml_dtypes
import concourse.bass as bass
import concourse.mybir as mybir
from concourse.bass_utils import run_bass_kernel_spmd

AF = mybir.ActivationFunctionType
OP = mybir.AluOpType
F32 = mybir.dt.float32
BF16 = mybir.dt.bfloat16
I32 = mybir.dt.int32
U8 = mybir.dt.uint8
XAX = mybir.AxisListType.X
bf16 = ml_dtypes.bfloat16

T = 1024
NPAIR = 43
WRING = 12
INTER = 11008
ENGINES = ("sync", "vector", "scalar", "tensor", "gpsimd")


def _host_consts():
    n = np.arange(4096, dtype=np.float64)
    Om = np.exp(-2j * np.pi * n / 4096.0)
    w = (1.0 - Om) / (1.0 + Om)
    c = 2.0 / (1.0 + Om)
    f = lambda a: np.ascontiguousarray(a, dtype=np.float32)
    w0 = np.arange(64)
    E1 = np.exp(2j * np.pi * np.outer(w0, w0) / 64.0)[:, ::-1]
    TW = np.exp(2j * np.pi * np.outer(w0, w0) / 4096.0)[:, ::-1]
    E2 = np.exp(2j * np.pi * np.outer(w0, w0) / 64.0)[:, ::-1]
    return {
        "wre": f(np.broadcast_to(w.real, (64, 4096))),
        "wim": f(np.broadcast_to(w.imag, (64, 4096))),
        "cre64": f(c.real.reshape(64, 64)),
        "cim64": f(c.imag.reshape(64, 64)),
        "E1c": f(np.block([[E1.real, E1.imag], [-E1.imag, E1.real]])),
        "TWr": f(TW.real), "TWi": f(TW.imag),
        "E2c": f(np.vstack([E2.real, -E2.imag])),
        "identf": f(np.eye(128)),
        "esl": np.ascontiguousarray(
            (np.arange(63)[None, :] == 31).astype(np.float32)
            * np.ones((128, 1), np.float32)).astype(bf16),
        "identb": np.eye(128, dtype=np.float32).astype(bf16),
    }


_CONSTS = _host_consts()


class Sem:
    def __init__(self, nc, name):
        self.h = nc.alloc_semaphore(name)
        self.name = name
        self.n = 0


class Plan:
    def __init__(self):
        self.ops = {e: [] for e in ENGINES}

    def op(self, eng, fn, waits=(), incs=()):
        wl = []
        for s, v in waits:
            v = max(0, v)
            assert v <= s.n, f"wait {s.name}>={v} but only {s.n} planned"
            if v > 0:
                wl.append((s.h, v))
        il = []
        for s, k in incs:
            s.n += k
            il.append((s.h, k))
        self.ops[eng].append((fn, wl, il))


def build_program(debug=False):
    nc = bass.Bass("TRN2", target_bir_lowering=False)

    x_h = nc.dram_tensor("x", [T, 4096], F32, kind="ExternalInput")
    qwgu_h = nc.dram_tensor("qwgu", [4096, 2752], I32, kind="ExternalInput")
    qzgu_h = nc.dram_tensor("qzgu", [32 * 2752], I32, kind="ExternalInput")
    sgupm_h = nc.dram_tensor("sgupm", [NPAIR * 32 * 512], F32, kind="ExternalInput")
    sgun_h = nc.dram_tensor("sgun", [32 * 22016], F32, kind="ExternalInput")
    qwdn_h = nc.dram_tensor("qwdn", [INTER, 512], I32, kind="ExternalInput")
    qzdn_h = nc.dram_tensor("qzdn", [86 * 512], I32, kind="ExternalInput")
    sdn_h = nc.dram_tensor("sdn", [86 * 4096], F32, kind="ExternalInput")
    lam_h = nc.dram_tensor("lam", [128, 1], F32, kind="ExternalInput")
    vct_h = nc.dram_tensor("vct", [64, 128], F32, kind="ExternalInput")
    bp_h = nc.dram_tensor("bp", [64, 1], F32, kind="ExternalInput")
    ctpq_h = nc.dram_tensor("ctpq", [1, 384], F32, kind="ExternalInput")
    dls_h = nc.dram_tensor("dls", [1, 2], F32, kind="ExternalInput")
    c_h = {k: nc.dram_tensor("c_" + k, list(v.shape),
                             BF16 if v.dtype == bf16 else F32, kind="ExternalInput")
           for k, v in _CONSTS.items()}
    out_h = nc.dram_tensor("out", [T, 4096], F32, kind="ExternalOutput")

    _k = dict(kind="ExternalOutput") if debug else {}
    if debug:
        mrow_dbg = nc.dram_tensor("mrow_dbg", [128, 64], F32, kind="ExternalOutput")
        xt2_dbg = nc.dram_tensor("xt2_dbg", [128, 4096], F32, kind="ExternalOutput")
        xn2_dbg = nc.dram_tensor("xn2_dbg", [128, 4096], F32, kind="ExternalOutput")
    h_hbm = nc.dram_tensor("h_scr", [INTER, T], BF16, **_k)
    s_hbm = nc.dram_tensor("s_scr", [T, 4096], F32, **_k)
    wdn_hbm = nc.dram_tensor("wdn_scr", [INTER, 4096], BF16, **_k)
    krev_hbm = nc.dram_tensor("krev_scr", [4352], F32, **_k)
    kk_hbm = nc.dram_tensor("kk_scr", [8, 4096], F32, **_k)
    zsneg_hbm = nc.dram_tensor("zsneg_scr", [32 * 22016], BF16)
    sgupm_b_hbm = nc.dram_tensor("sgupm_b_scr", [NPAIR * 32 * 512], BF16)
    zdn_hbm = nc.dram_tensor("zdn_scr", [86 * 4096], BF16)
    sdnb_hbm = nc.dram_tensor("sdnb_scr", [86 * 4096], BF16)
    zdn2 = zdn_hbm.ap().rearrange("(g d) -> g d", d=4096)
    sdnb2 = sdnb_hbm.ap().rearrange("(g d) -> g d", d=4096)
    bnc_hbm = nc.dram_tensor("bounce_scr", [2048], F32, **_k)

    cs = {k: nc.alloc_sbuf_tensor("k_" + k, list(v.shape),
                                  BF16 if v.dtype == bf16 else F32)
          for k, v in _CONSTS.items() if k not in ("wre", "wim")}
    lam_sb = nc.alloc_sbuf_tensor("lam_sb", [128, 1], F32)
    vct_sb = nc.alloc_sbuf_tensor("vct_sb", [64, 128], F32)
    bp_sb = nc.alloc_sbuf_tensor("bp_sb", [64, 1], F32)
    ctpq_sb = nc.alloc_sbuf_tensor("ctpq_sb", [1, 384], F32)
    dls_sb = nc.alloc_sbuf_tensor("dls_sb", [1, 2], F32)
    step_sb = nc.alloc_sbuf_tensor("step_sb", [1, 4], F32)
    ts64_sb = nc.alloc_sbuf_tensor("ts64_sb", [64, 1], F32)
    d128_sb = nc.alloc_sbuf_tensor("d128_sb", [128, 1], F32)
    vmat_sb = nc.alloc_sbuf_tensor("vmat_sb", [128, 8], F32)
    mrow_sb = nc.alloc_sbuf_tensor("mrow_sb", [128, 8], F32)
    mrowall_sb = nc.alloc_sbuf_tensor("mrowall_sb", [128, 64], F32)
    hank_sb = nc.alloc_sbuf_tensor("hank_sb", [128, 256], F32)
    xt32_sb = nc.alloc_sbuf_tensor("xt32_sb", [32, 1024], BF16)
    zsn_sb = nc.alloc_sbuf_tensor("zsn_sb", [32, 2 * 512], BF16)

    ARENA = 161792
    arena = nc.alloc_sbuf_tensor("arena", [128, ARENA], U8)

    def view(off, parts, n, dt):
        esz = {F32: 4, I32: 4, BF16: 2, U8: 1}[dt]
        assert off % 4 == 0 and off + n * esz <= ARENA, (off, n)
        ap = arena[0:parts, off:off + n * esz]
        if dt != U8:
            ap = ap.bitcast(dt)
        return ap

    xnT = view(0, 128, 32768, BF16).rearrange("p (j t) -> p j t", t=1024)
    tbuf = view(65536, 128, 4608, BF16)
    C0 = 74752
    zq_gu = view(90112, 128, 688, I32)
    zi_gu = view(90112 + 2752, 128, 5504, I32)
    zb_gu = view(90112 + 24768, 128, 5504, BF16)
    sf2_gu = view(90112 + 35776, 128, 5504, F32)
    zb2_gu = view(90112 + 57792, 128, 5504, BF16)
    sf_gu = view(90112, 128, 5504, F32)
    sb_gu = view(90112 + 22016, 128, 5504, BF16)
    zq_dn = view(90112, 86, 512, I32)
    zi_dn = view(90112 + 2048, 86, 4096, I32)
    zb_dn = view(90112 + 18432, 86, 4096, BF16)
    sf_dn = view(90112 + 26624, 86, 4096, F32)
    sb_dn = view(90112 + 43008, 86, 4096, BF16)
    wre_v = view(90112, 64, 1024, F32)
    wim_v = view(90112 + 4096, 64, 1024, F32)
    dre_v = view(90112 + 8192, 64, 1024, F32)
    dim_v = view(90112 + 12288, 64, 1024, F32)
    rin_v = view(90112 + 16384, 64, 1024, F32)
    tmp_v = view(90112 + 20480, 64, 1024, F32)
    rcat_v = view(90112 + 24576, 128, 1024, F32)
    kk8_v = view(90112 + 28672, 8, 4096, F32)
    vstage_v = view(90112 + 45056, 1, 1024, F32)
    vrow_v = view(90112 + 49152, 1, 1024, F32)
    brow_v = view(90112 + 53248, 1, 128, F32)
    bst_v = view(90112 + 53760, 128, 1, F32)
    ast_v = view(90112 + 54272, 128, 64, F32)
    s1_v = view(90112 + 54528, 64, 128, F32)
    s2_v = view(90112 + 55040, 128, 64, F32)
    krev_v = view(90112 + 55296, 64, 64, F32)
    krevt_v = view(90112 + 55552, 64, 64, F32)
    KCN = ["k00r", "k00i", "k01r", "k01i", "k10r", "k10i", "k11r", "k11i",
           "ta", "tb", "tc", "td", "te", "tf", "atr", "ati"]
    kc_v = {nm: view(90112 + 55808 + i * 256, 64, 64, F32) for i, nm in enumerate(KCN)}
    xt_v = view(C0, 128, 2 * 4096, F32)
    xn_v = view(C0 + 32768, 128, 2 * 4096, F32)
    wbf_v = view(C0, 128, WRING * 2048, BF16)
    pk_v = view(C0 + 49152, 128, 4 * 256, I32)
    wq_v = view(C0 + 53248, 128, 2048, I32)
    sr_v = view(C0 + 69632, 128, 2 * 2048, BF16)
    hbf_v = view(C0 + 77824, 128, 4 * 512, BF16)
    sil_v = view(C0 + 81920, 128, 2 * 512, F32)
    stile_v = view(157696, 128, 2 * 512, F32)
    pkd_v = view(88064, 128, 2 * 512, I32)
    wqd_v = view(92160, 128, 4096, I32)
    zrd_v = view(108544, 128, 2 * 4096, BF16)
    srd_v = view(124928, 128, 2 * 4096, BF16)
    wdb_v = view(141312, 128, 2 * 4096, BF16)
    h_v = view(0, 128, 86 * 512, BF16)
    wt_v = view(88064, 128, 24 * 512, BF16)
    sx_v = view(112640, 128, 8 * 512, F32)
    xx_v = view(129024, 128, 8 * 512, F32)
    ob_v = view(145408, 128, 4 * 512, F32)
    g3_v = view(153600, 128, 2 * 512, F32)
    t3_v = view(157696, 128, 2 * 512, F32)

    ps = [nc.alloc_psum_tensor("ps%d" % i, [128, 512], F32) for i in range(8)]

    SN = ["ld0", "v0", "a0", "t0", "g0", "ph",
          "pk1", "rep1", "dq1", "wf1", "mm1", "ea1", "ev1", "ho1",
          "cmm", "cev", "cso",
          "pkd", "repd", "wdq", "wout",
          "hld", "wt3", "sx3", "mm3", "g3", "e3", "fin", "vch", "mmd"]
    S = {name: Sem(nc, "s_" + name) for name in SN}
    (ld0, v0, a0, t0, g0, ph, pk1, rep1, dq1, wf1, mm1, ea1, ev1, ho1,
     cmm, cev, cso, pkd, repd, wdq, wout, hld, wt3, sx3, mm3, g3s, e3,
     fin, vch, mmd) = [S[n] for n in SN]
    P = Plan()

    def barrier(prewaits=()):
        pw = {e: [] for e in ENGINES}
        for e, s, vv in prewaits:
            assert vv <= s.n
            pw[e].append((s.h, vv))
        for e in ENGINES:
            ph.n += 1
            wl = pw[e]
            if e in ("tensor", "scalar", "vector", "gpsimd"):
                P.ops[e].append((lambda eng: eng.drain(), [], []))
            P.ops[e].append(
                (lambda eng: eng.sem_inc(ph.h, 1), wl, []))
        tgt = ph.n
        for e in ENGINES:
            P.ops[e].append((None, [(ph.h, tgt)], []))

    def vchain(steps, waits0=(), incs_last=()):
        for i, fn in enumerate(steps):
            w = list(waits0) + [(vch, vch.n)] if i == 0 else [(vch, vch.n)]
            inc = list(incs_last) if i == len(steps) - 1 else [(vch, 1)]
            P.op("vector", fn, waits=w, incs=inc)

    def dma(eng, dst, src, sem, waits=()):
        P.op(eng, lambda e, d=dst, s=src: e.dma_start(out=d, in_=s),
             waits=waits, incs=[(sem, 16)])

    small = [(lam_sb[:], lam_h[:]), (vct_sb[:], vct_h[:]), (bp_sb[:], bp_h[:]),
             (ctpq_sb[:], ctpq_h[:]), (dls_sb[:], dls_h[:])]
    small += [(cs[k][:], c_h[k][:]) for k in cs]
    for dst, src in small:
        dma("sync", dst, src, ld0)
    N_SMALL = ld0.n

    dma("sync", zq_gu, qzgu_h.ap().rearrange("(p w) -> p w", p=128), ld0)
    zq_mark = ld0.n
    wvj = zi_gu.rearrange("p (w j) -> p w j", j=8)
    for j in range(8):
        P.op("vector",
             lambda e, j=j: e.tensor_scalar(
                 out=wvj[:, :, j], in0=zq_gu, scalar1=28 - 4 * j, scalar2=28,
                 op0=OP.logical_shift_left, op1=OP.logical_shift_right),
             waits=[(ld0, zq_mark)])
    P.op("vector", lambda e: e.tensor_copy(out=zb_gu, in_=zi_gu), incs=[(v0, 1)])
    dma("sync", sf2_gu, bass.AP(sgun_h, 0, [[5504, 128], [1, 5504]]), ld0)
    sf2_mark = ld0.n
    P.op("vector", lambda e: e.scalar_tensor_tensor(
        out=zb2_gu, in0=zb_gu, scalar=-1.0, in1=sf2_gu, op0=OP.mult, op1=OP.mult),
        waits=[(ld0, sf2_mark), (vch, vch.n)], incs=[(v0, 1)])
    P.op("gpsimd", lambda e: e.dma_start(
        out=zsneg_hbm.ap().rearrange("(p w) -> p w", p=128), in_=zb2_gu),
        waits=[(v0, v0.n)], incs=[(g0, 16)])
    ZPM_G0 = g0.n

    dma("sync", sf_gu, bass.AP(sgupm_h, 0, [[5504, 128], [1, 5504]]), ld0,
        waits=[(g0, ZPM_G0)])
    P.op("vector", lambda e: e.tensor_copy(out=sb_gu, in_=sf_gu),
         waits=[(ld0, ld0.n)], incs=[(v0, 1)])
    P.op("gpsimd", lambda e: e.dma_start(
        out=bass.AP(sgupm_b_hbm, 0, [[5504, 128], [1, 5504]]), in_=sb_gu),
        waits=[(v0, v0.n)], incs=[(g0, 16)])
    SPM_G0 = g0.n

    dma("sync", zq_dn, qzdn_h.ap().rearrange("(g w) -> g w", g=86), ld0,
        waits=[(g0, SPM_G0)])
    dma("sync", sf_dn, bass.AP(sdn_h, 0, [[4096, 86], [1, 4096]]), ld0)
    dn_mark = ld0.n
    wvj2 = zi_dn.rearrange("g (C j c) -> g C j c", j=8, c=16)
    zqr = zq_dn.rearrange("g (C c) -> g C c", c=16)
    for j in range(8):
        P.op("vector",
             lambda e, j=j: e.tensor_scalar(
                 out=wvj2[:, :, j, :], in0=zqr, scalar1=28 - 4 * j, scalar2=28,
                 op0=OP.logical_shift_left, op1=OP.logical_shift_right),
             waits=[(ld0, dn_mark)])
    P.op("vector", lambda e: e.tensor_copy(out=zb_dn, in_=zi_dn), incs=[(v0, 1)])
    zbdn_v0 = v0.n
    P.op("vector", lambda e: e.tensor_copy(
        out=sb_dn.rearrange("g (C j c) -> g C j c", j=8, c=16),
        in_=sf_dn.rearrange("g (C c j) -> g C c j", c=16, j=8).transpose([0, 1, 3, 2])),
        incs=[(v0, 1)])
    sbdn_v0 = v0.n
    P.op("gpsimd", lambda e: e.dma_start(
        out=zdn_hbm.ap().rearrange("(g w) -> g w", g=86), in_=zb_dn),
        waits=[(v0, zbdn_v0)], incs=[(g0, 16)])
    P.op("gpsimd", lambda e: e.dma_start(
        out=sdnb_hbm.ap().rearrange("(g w) -> g w", g=86), in_=sb_dn),
        waits=[(v0, sbdn_v0)], incs=[(g0, 16)])
    STAGE_G0 = g0.n

    P.op("scalar", lambda e: e.activation(out=step_sb[:, 0:1], in_=dls_sb[:, 1:2],
                                          func=AF.Exp),
         waits=[(ld0, N_SMALL)])
    P.op("scalar", lambda e: e.drain(), incs=[(a0, 1)])
    vchain([
        lambda e: e.reciprocal(out=step_sb[:, 1:2], in_=step_sb[:, 0:1]),
        lambda e: e.tensor_scalar(out=step_sb[:, 2:3], in0=step_sb[:, 1:2],
                                  scalar1=2.0, scalar2=None, op0=OP.mult),
    ], waits0=[(a0, a0.n)], incs_last=[(v0, 1)])
    P.op("gpsimd", lambda e: e.dma_start(out=bnc_hbm[0:1], in_=step_sb[0:1, 2:3]),
         waits=[(v0, v0.n)], incs=[(g0, 16)])
    P.op("gpsimd", lambda e: e.dma_start(out=bnc_hbm[1:2], in_=dls_sb[0:1, 0:1]),
         incs=[(g0, 16)])
    P.op("gpsimd", lambda e: e.dma_start(
        out=ts64_sb[:], in_=bass.AP(bnc_hbm, 0, [[0, 64], [1, 1]])),
        waits=[(g0, g0.n)], incs=[(g0, 16)])
    P.op("gpsimd", lambda e: e.dma_start(
        out=d128_sb[:], in_=bass.AP(bnc_hbm, 1, [[0, 128], [1, 1]])),
        incs=[(g0, 16)])
    BNC_G0 = g0.n

    P.op("tensor", lambda e: e.matmul(ps[0][:, 0:1], vct_sb[:], bp_sb[:],
                                      start=True, stop=True),
         waits=[(ld0, N_SMALL)])
    P.op("tensor", lambda e: e.drain(), incs=[(t0, 1)])
    P.op("vector", lambda e: e.tensor_copy(out=bst_v, in_=ps[0][:, 0:1]),
         waits=[(t0, t0.n), (g0, STAGE_G0)], incs=[(v0, 1)])
    P.op("gpsimd", lambda e: e.dma_start(out=bnc_hbm[128:256], in_=bst_v),
         waits=[(v0, v0.n)], incs=[(g0, 16)])
    P.op("gpsimd", lambda e: e.dma_start(out=brow_v, in_=bnc_hbm[128:256]),
         waits=[(g0, g0.n)], incs=[(g0, 16)])
    BROW_G0 = g0.n

    ct_re, ct_im = ctpq_sb[:, 0:64], ctpq_sb[:, 64:128]
    p_re, p_im = ctpq_sb[:, 128:192], ctpq_sb[:, 192:256]
    q_re, q_im = ctpq_sb[:, 256:320], ctpq_sb[:, 320:384]
    b_re, b_im = brow_v[:, 0:64], brow_v[:, 64:128]
    w1, w2 = vrow_v[:, 0:64], vrow_v[:, 64:128]
    vre, vim = vrow_v[:, 128:192], vrow_v[:, 192:256]
    vst = vstage_v.rearrange("o (n c) -> o n c", c=8)
    vsteps = []
    for kk, (ar, ai, br, bi) in enumerate(
            [(ct_re, ct_im, b_re, b_im), (ct_re, ct_im, p_re, p_im),
             (q_re, q_im, b_re, b_im), (q_re, q_im, p_re, p_im)]):
        cA, cB = 2 * kk, 2 * kk + 1
        def m1(e, ar=ar, br=br, ai=ai, bi=bi):
            e.tensor_tensor(out=w1, in0=ar, in1=br, op=OP.mult)
            return e.tensor_tensor(out=w2, in0=ai, in1=bi, op=OP.mult)
        def m2(e):
            return e.tensor_tensor(out=vre, in0=w1, in1=w2, op=OP.add)
        def m3(e, ar=ar, bi=bi, ai=ai, br=br):
            e.tensor_tensor(out=w1, in0=ar, in1=bi, op=OP.mult)
            return e.tensor_tensor(out=w2, in0=ai, in1=br, op=OP.mult)
        def m4(e):
            return e.tensor_tensor(out=vim, in0=w1, in1=w2, op=OP.subtract)
        def m5(e, cA=cA, cB=cB):
            e.tensor_copy(out=vst[:, 0:64, cA], in_=vre)
            e.tensor_copy(out=vst[:, 64:128, cA], in_=vim)
            e.tensor_copy(out=vst[:, 0:64, cB], in_=vim)
            return e.tensor_scalar(out=vst[:, 64:128, cB], in0=vre,
                                   scalar1=-1.0, scalar2=None, op0=OP.mult)
        vsteps += [m1, m2, m3, m4, m5]
    vchain(vsteps, waits0=[(g0, BROW_G0)], incs_last=[(v0, 1)])
    P.op("gpsimd", lambda e: e.dma_start(out=bnc_hbm[256:1280], in_=vstage_v),
         waits=[(v0, v0.n)], incs=[(g0, 16)])
    P.op("gpsimd", lambda e: e.dma_start(
        out=vmat_sb[:], in_=bass.AP(bnc_hbm, 256, [[8, 128], [1, 8]])),
        waits=[(g0, g0.n)], incs=[(g0, 16)])
    VMAT_G0 = g0.n

    deno_marks = {}
    for q in range(4):
        sl = slice(q * 1024, q * 1024 + 1024)
        qwl = [(v0, sbdn_v0)] if q == 0 else [(v0, deno_marks[q - 1])]
        dma("sync", wre_v, c_h["wre"][0:64, sl], ld0, waits=qwl)
        dma("sync", wim_v, c_h["wim"][0:64, sl], ld0)
        qld = ld0.n

        def dn1(e):
            e.tensor_scalar(out=dre_v, in0=wre_v, scalar1=ts64_sb[:],
                            scalar2=lam_sb[0:64, :], op0=OP.mult, op1=OP.subtract)
            return e.tensor_scalar(out=dim_v, in0=wim_v, scalar1=ts64_sb[:],
                                   scalar2=lam_sb[64:128, :], op0=OP.mult,
                                   op1=OP.subtract)
        def dn2(e):
            e.tensor_tensor(out=rin_v, in0=dre_v, in1=dre_v, op=OP.mult)
            return e.tensor_tensor(out=tmp_v, in0=dim_v, in1=dim_v, op=OP.mult)
        def dn3(e):
            return e.tensor_tensor(out=rin_v, in0=rin_v, in1=tmp_v, op=OP.add)
        def dn4(e):
            return e.reciprocal(out=rin_v, in_=rin_v)
        def dn5(e):
            e.tensor_tensor(out=rcat_v[0:64, :], in0=dre_v, in1=rin_v, op=OP.mult)
            return e.tensor_tensor(out=rcat_v[64:128, :], in0=dim_v, in1=rin_v,
                                   op=OP.mult)
        vchain([dn1, dn2, dn3, dn4, dn5],
               waits0=[(ld0, qld), (g0, BNC_G0), (t0, t0.n)], incs_last=[(v0, 1)])
        deno_marks[q] = v0.n
        for half in range(2):
            P.op("tensor", lambda e, hh=half: e.matmul(
                ps[1][0:8, :], vmat_sb[:], rcat_v[:, hh * 512:hh * 512 + 512],
                start=True, stop=True),
                waits=[(v0, v0.n), (cev, cev.n), (g0, VMAT_G0)])
            P.op("tensor", lambda e: e.drain(), incs=[(t0, 1)])
            P.op("vector", lambda e, q=q, hh=half: e.tensor_copy(
                out=kk8_v[:, (q * 2 + hh) * 512:(q * 2 + hh) * 512 + 512],
                in_=ps[1][0:8, :]),
                waits=[(t0, t0.n)], incs=[(cev, 1)])

    P.op("gpsimd", lambda e: e.dma_start(out=kk_hbm[:], in_=kk8_v),
         waits=[(cev, cev.n)], incs=[(g0, 16)])
    for r, nm in enumerate(KCN[:8]):
        P.op("gpsimd", lambda e, r=r, nm=nm: e.dma_start(
            out=kc_v[nm], in_=kk_hbm[r].rearrange("(a b) -> a b", a=64)),
            waits=[(g0, g0.n)], incs=[(g0, 16)])
    KC_G0 = g0.n

    K_ = kc_v
    def _tt(o, a, b, op):
        return lambda e: e.tensor_tensor(out=K_[o], in0=K_[a], in1=K_[b], op=op)
    at_steps = [
        lambda e: e.tensor_scalar(out=K_["ta"], in0=K_["k11r"], scalar1=1.0,
                                  scalar2=None, op0=OP.add),
        _tt("tb", "ta", "ta", OP.mult),
        _tt("tc", "k11i", "k11i", OP.mult),
        _tt("tb", "tb", "tc", OP.add),
        lambda e: e.reciprocal(out=K_["tb"], in_=K_["tb"]),
        _tt("tc", "ta", "tb", OP.mult),
        _tt("td", "k11i", "tb", OP.mult),
        _tt("ta", "k01r", "tc", OP.mult),
        _tt("tb", "k01i", "td", OP.mult),
        _tt("te", "ta", "tb", OP.add),
        _tt("ta", "k01i", "tc", OP.mult),
        _tt("tb", "k01r", "td", OP.mult),
        _tt("tf", "ta", "tb", OP.subtract),
        _tt("ta", "te", "k10r", OP.mult),
        _tt("tb", "tf", "k10i", OP.mult),
        _tt("ta", "ta", "tb", OP.subtract),
        _tt("tc", "te", "k10i", OP.mult),
        _tt("td", "tf", "k10r", OP.mult),
        _tt("tc", "tc", "td", OP.add),
        _tt("ta", "k00r", "ta", OP.subtract),
        _tt("tc", "k00i", "tc", OP.subtract),
        lambda e: e.tensor_tensor(out=K_["te"], in0=cs["cre64"][:], in1=K_["ta"], op=OP.mult),
        lambda e: e.tensor_tensor(out=K_["tf"], in0=cs["cim64"][:], in1=K_["tc"], op=OP.mult),
        _tt("atr", "te", "tf", OP.subtract),
        lambda e: e.tensor_tensor(out=K_["te"], in0=cs["cre64"][:], in1=K_["tc"], op=OP.mult),
        lambda e: e.tensor_tensor(out=K_["tf"], in0=cs["cim64"][:], in1=K_["ta"], op=OP.mult),
        _tt("ati", "te", "tf", OP.add),
        lambda e: e.tensor_copy(out=ast_v[0:64, :], in_=K_["atr"]),
        lambda e: e.tensor_copy(out=ast_v[64:128, :], in_=K_["ati"]),
    ]
    vchain(at_steps, waits0=[(g0, KC_G0)], incs_last=[(v0, 1)])
    if debug:
        ast_dbg = nc.dram_tensor("ast_dbg", [128, 64], F32, kind="ExternalOutput")
        rc_dbg = nc.dram_tensor("rc_dbg", [128, 1024], F32, kind="ExternalOutput")
        P.op("gpsimd", lambda e: e.dma_start(out=ast_dbg[:], in_=ast_v),
             waits=[(v0, v0.n)], incs=[(g0, 16)])
        P.op("gpsimd", lambda e: e.dma_start(out=rc_dbg[:], in_=rcat_v),
             waits=[(v0, v0.n)], incs=[(g0, 16)])
    P.op("tensor", lambda e: e.matmul(ps[1][0:64, 0:128], ast_v, cs["E1c"][:],
                                      start=True, stop=True),
         waits=[(v0, v0.n)])
    P.op("tensor", lambda e: e.drain(), incs=[(t0, 1)])
    P.op("vector", lambda e: e.tensor_copy(out=s1_v, in_=ps[1][0:64, 0:128]),
         waits=[(t0, t0.n)], incs=[(v0, 1)])

    s1r, s1i = s1_v[:, 0:64], s1_v[:, 64:128]
    twa, twb = kc_v["ta"], kc_v["tb"]
    tw_steps = [
        lambda e: (e.tensor_tensor(out=twa, in0=s1r, in1=cs["TWr"][:], op=OP.mult),
                   e.tensor_tensor(out=twb, in0=s1i, in1=cs["TWi"][:], op=OP.mult))[-1],
        lambda e: e.tensor_tensor(out=s2_v[0:64, :], in0=twa, in1=twb, op=OP.subtract),
        lambda e: (e.tensor_tensor(out=twa, in0=s1r, in1=cs["TWi"][:], op=OP.mult),
                   e.tensor_tensor(out=twb, in0=s1i, in1=cs["TWr"][:], op=OP.mult))[-1],
        lambda e: e.tensor_tensor(out=s2_v[64:128, :], in0=twa, in1=twb, op=OP.add),
    ]
    vchain(tw_steps, incs_last=[(v0, 1)])
    P.op("tensor", lambda e: e.matmul(ps[1][0:64, 0:64], s2_v, cs["E2c"][:],
                                      start=True, stop=True),
         waits=[(v0, v0.n)])
    P.op("tensor", lambda e: e.drain(), incs=[(t0, 1)])
    P.op("vector", lambda e: e.tensor_scalar(
        out=krev_v, in0=ps[1][0:64, 0:64], scalar1=1.0 / 4096.0, scalar2=None,
        op0=OP.mult), waits=[(t0, t0.n)], incs=[(v0, 1)])
    P.op("tensor", lambda e: e.transpose(ps[2][0:64, 0:64], krev_v,
                                         cs["identf"][0:64, 0:64]),
         waits=[(v0, v0.n)])
    P.op("tensor", lambda e: e.drain(), incs=[(t0, 1)])
    P.op("vector", lambda e: e.tensor_copy(out=krevt_v, in_=ps[2][0:64, 0:64]),
         waits=[(t0, t0.n)], incs=[(v0, 1)])
    P.op("vector", lambda e: e.memset(vrow_v[:, 0:256], 0.0), incs=[(v0, 1)])
    P.op("gpsimd", lambda e: e.dma_start(
        out=bass.AP(krev_hbm, 0, [[64, 64], [1, 64]]), in_=krevt_v),
        waits=[(v0, v0.n)], incs=[(g0, 16)])
    P.op("gpsimd", lambda e: e.dma_start(out=krev_hbm[4096:4352],
                                         in_=vrow_v[0:1, 0:256]),
         incs=[(g0, 16)])
    KREV_G0 = g0.n

    P.op("vector", lambda e: e.memset(tbuf[:, 0:512], 0.0), incs=[(v0, 1)])
    hk_marks = []
    for d in range(32):
        wl = [(g0, KREV_G0)]
        if d >= 2:
            wl.append((v0, hk_marks[d - 2]))
        dma("sync", hank_sb[:, (d % 2) * 128:(d % 2) * 128 + 128],
            bass.AP(krev_hbm, 3968 - 128 * d, [[1, 128], [1, 128]]), ld0,
            waits=wl)
        mk = ld0.n
        P.op("vector", lambda e, d=d: e.tensor_copy(
            out=tbuf[:, 512 + 128 * d:640 + 128 * d],
            in_=bass.AP(hank_sb, (d % 2) * 128 + 127, [[256, 128], [-1, 128]])),
            waits=[(ld0, mk)], incs=[(v0, 1)])
        hk_marks.append(v0.n)
    P.op("vector", lambda e: e.scalar_tensor_tensor(
        out=tbuf[:, 512:640],
        in0=cs["identb"][:], scalar=d128_sb[:], in1=tbuf[:, 512:640],
        op0=OP.mult, op1=OP.add),
        waits=[(g0, BNC_G0)], incs=[(v0, 1)])
    TBUF_V0 = v0.n
    TBUF_T0 = t0.n

    aff_marks = {}
    tp_marks = {}
    tpe_marks = {}
    for tt in range(8):
        slot = tt % 2
        xts = xt_v[:, slot * 4096:slot * 4096 + 4096]
        xns = xn_v[:, slot * 4096:slot * 4096 + 4096]
        wl = [(v0, TBUF_V0)] if tt < 2 else [(v0, aff_marks[tt - 2])]
        dma("sync", xts, x_h[tt * 128:tt * 128 + 128, :], ld0, waits=wl)
        xmk = ld0.n
        awl = [(ld0, xmk)]
        if tt >= 2:
            awl.append((t0, tpe_marks[tt - 2]))
        else:
            awl += [(v0, TBUF_V0), (t0, TBUF_T0), (g0, KREV_G0)]
        P.op("scalar", lambda e, xns=xns, xts=xts: e.activation(
            out=xns, in_=xts, func=AF.Square), waits=awl)
        P.op("scalar", lambda e: e.drain(), incs=[(a0, 1)])
        amk = a0.n

        def st1(e, xts=xts, xns=xns):
            e.tensor_reduce(out=mrow_sb[:, 0:1], in_=xts, axis=XAX, op=OP.add)
            return e.tensor_reduce(out=mrow_sb[:, 1:2], in_=xns, axis=XAX, op=OP.add)
        def st2(e):
            e.tensor_scalar(out=mrow_sb[:, 2:3], in0=mrow_sb[:, 0:1],
                            scalar1=1.0 / 4096.0, scalar2=None, op0=OP.mult)
            return e.tensor_scalar(out=mrow_sb[:, 3:4], in0=mrow_sb[:, 1:2],
                                   scalar1=1.0 / 4096.0, scalar2=None, op0=OP.mult)
        def st3(e):
            return e.tensor_tensor(out=mrow_sb[:, 4:5], in0=mrow_sb[:, 2:3],
                                   in1=mrow_sb[:, 2:3], op=OP.mult)
        def st4(e):
            return e.tensor_tensor(out=mrow_sb[:, 5:6], in0=mrow_sb[:, 3:4],
                                   in1=mrow_sb[:, 4:5], op=OP.subtract)
        vchain([st1, st2, st3, st4], waits0=[(ld0, xmk), (a0, amk)],
               incs_last=[(v0, 1)])
        P.op("scalar", lambda e: e.activation(out=mrow_sb[:, 6:7],
                                              in_=mrow_sb[:, 5:6],
                                              func=AF.Sqrt, bias=np.float32(1e-5)),
             waits=[(v0, v0.n)])
        P.op("scalar", lambda e: e.drain(), incs=[(a0, 1)])

        def rs1(e):
            return e.reciprocal(out=mrow_sb[:, 7:8], in_=mrow_sb[:, 6:7])
        def rs2(e):
            return e.tensor_tensor(out=mrow_sb[:, 0:1], in0=mrow_sb[:, 2:3],
                                   in1=mrow_sb[:, 7:8], op=OP.mult)
        def rs3(e):
            return e.tensor_scalar(out=mrow_sb[:, 1:2], in0=mrow_sb[:, 0:1],
                                   scalar1=-1.0, scalar2=None, op0=OP.mult)
        vchain([rs1, rs2, rs3], waits0=[(a0, a0.n)], incs_last=[(v0, 1)])
        P.op("vector", lambda e, tt=tt: e.tensor_copy(
            out=mrowall_sb[:, tt * 8:tt * 8 + 8], in_=mrow_sb[:, 0:8]),
            waits=[(v0, v0.n)])
        P.op("vector", lambda e, xns=xns, xts=xts: e.tensor_scalar(
            out=xns, in0=xts, scalar1=mrow_sb[:, 7:8], scalar2=mrow_sb[:, 1:2],
            op0=OP.mult, op1=OP.add), waits=[(v0, v0.n)], incs=[(v0, 1)])
        aff_marks[tt] = v0.n
        if debug and tt == 2:
            P.op("gpsimd", lambda e, xts=xts: e.dma_start(out=xt2_dbg[:], in_=xts),
                 waits=[(v0, aff_marks[tt])], incs=[(g0, 16)])
            P.op("gpsimd", lambda e, xns=xns: e.dma_start(out=xn2_dbg[:], in_=xns),
                 waits=[(v0, aff_marks[tt])], incs=[(g0, 16)])
        for gi in range(8):
            bank = ps[6 + gi % 2]
            twl = [(v0, aff_marks[tt])]
            if (tt, gi) > (0, 1):
                pj = (tt, gi - 2) if gi >= 2 else (tt - 1, 6 + gi)
                twl.append((v0, tp_marks[pj]))
            for i in range(4):
                jc = gi * 4 + i
                P.op("tensor", lambda e, xns=xns, jc=jc, i=i, bank=bank: e.transpose(
                    bank[:, i * 128:i * 128 + 128],
                    xns[:, jc * 128:jc * 128 + 128], cs["identf"][:]),
                    waits=twl if i == 0 else [])
            P.op("tensor", lambda e: e.drain(), incs=[(t0, 1)])
            tmark = t0.n
            P.op("vector", lambda e, gi=gi, tt=tt, bank=bank: e.tensor_copy(
                out=xnT[:, gi * 4:gi * 4 + 4, tt * 128:tt * 128 + 128],
                in_=bank[:, 0:512].rearrange("p (a b) -> p a b", a=4)),
                waits=[(t0, tmark)], incs=[(v0, 1)])
            tp_marks[(tt, gi)] = v0.n
        tpe_marks[tt] = t0.n

    if debug:
        P.op("gpsimd", lambda e: e.dma_start(out=mrow_dbg[:], in_=mrowall_sb[:]),
             waits=[(v0, v0.n)], incs=[(g0, 16)])
        xnT_dbg = nc.dram_tensor("xnT_dbg", [128, 32768], BF16, kind="ExternalOutput")
        tb_dbg = nc.dram_tensor("tbuf_dbg", [128, 4608], BF16, kind="ExternalOutput")
        P.op("gpsimd", lambda e: e.dma_start(
            out=xnT_dbg[:], in_=xnT.rearrange("p j t -> p (j t)")),
            waits=[(v0, v0.n)], incs=[(g0, 16)])
        P.op("gpsimd", lambda e: e.dma_start(out=tb_dbg[:], in_=tbuf),
            waits=[(v0, v0.n)], incs=[(g0, 16)])
    for half in range(2):
        for kc in range(32):
            P.op("tensor", lambda e, kc=kc, half=half: e.matmul(
                ps[4][0:32, :], cs["esl"][:, 31 - kc:63 - kc],
                xnT[:, kc, half * 512:half * 512 + 512],
                start=(kc == 0), stop=(kc == 31)),
                waits=[(v0, v0.n)] if kc == 0 else [])
        P.op("tensor", lambda e: e.drain(), incs=[(t0, 1)])
        P.op("vector", lambda e, half=half: e.tensor_copy(
            out=xt32_sb[0:32, half * 512:half * 512 + 512], in_=ps[4][0:32, :]),
            waits=[(t0, t0.n)], incs=[(v0, 1)])
    XT_DONE = v0.n

    barrier(prewaits=[("gpsimd", g0, g0.n), ("sync", ld0, ld0.n),
                      ("vector", v0, v0.n), ("scalar", a0, a0.n),
                      ("tensor", t0, t0.n)])

    qw4 = qwgu_h.ap().rearrange("(kc p) w -> p kc w", p=128)
    spm = sgupm_b_hbm.ap().rearrange("(pr kc l) -> pr kc l", kc=32, l=512)
    pkr = pk_v.rearrange("p (s g k w) -> p s g k w", s=4, g=2, k=4)
    wqr = wq_v.rearrange("p (k g w j) -> p k g w j", k=4, g=2, w=32)
    wbr = wbf_v.rearrange("p (s n) -> p s n", s=WRING)

    pk_marks, rep_marks, dq_marks = {}, {}, {}
    mm1_marks, ea_marks, ev_marks, ho_marks = {}, {}, {}, {}

    def p1_dmas(p, hq):
        G = p * 8 + hq
        s4 = G % 4
        wl = [(dq1, G - 3)] if G >= 4 else []
        for gi, woff in ((0, p * 32), (1, 1376 + p * 32)):
            dma("sync", pkr[:, s4, gi],
                qw4[:, 4 * hq:4 * hq + 4, woff:woff + 32], pk1, waits=wl)
        pk_marks[G] = pk1.n
        wl = [(dq1, G - 1)] if G >= 2 else []
        s2 = G % 2
        srr = sr_v.rearrange("p (s a l) -> p s a l", s=2, a=4)
        dma("sync", srr[:, s2],
            spm[p, 4 * hq:4 * hq + 4, :].partition_broadcast(128), rep1, waits=wl)
        rep_marks[G] = rep1.n

    def p1_deq(p, hq):
        G = p * 8 + hq
        s4, s2, sw = G % 4, G % 2, G % WRING

        def unpack(e):
            last = None
            for gi in range(2):
                for j in range(8):
                    last = e.tensor_scalar(
                        out=wqr[:, :, gi, :, j], in0=pkr[:, s4, gi],
                        scalar1=28 - 4 * j, scalar2=28,
                        op0=OP.logical_shift_left, op1=OP.logical_shift_right)
            return last

        wl = [(pk1, pk_marks[G])]
        if G >= WRING:
            wl.append((wf1, G - WRING + 1))
        P.op("vector", unpack, waits=wl, incs=[(vch, 1)])
        P.op("vector", lambda e: e.tensor_copy(out=wbr[:, sw], in_=wq_v),
             waits=[(vch, vch.n)], incs=[(vch, 1)])
        P.op("vector", lambda e: e.tensor_tensor(
            out=wbr[:, sw], in0=wbr[:, sw],
            in1=sr_v[:, s2 * 2048:s2 * 2048 + 2048], op=OP.mult),
            waits=[(rep1, rep_marks[G]), (vch, vch.n)], incs=[(dq1, 1)])
        dq_marks[G] = dq1.n

    def p1_zsn(p):
        wl = [(mm1, 2 * (p - 1))] if p >= 2 else []
        zv = zsneg_hbm.ap().rearrange("(g n) -> g n", n=22016)
        dma("sync", zsn_sb[0:32, (p % 2) * 512:(p % 2) * 512 + 256],
            zv[:, p * 256:p * 256 + 256], rep1, waits=wl)
        dma("sync", zsn_sb[0:32, (p % 2) * 512 + 256:(p % 2) * 512 + 512],
            zv[:, 11008 + p * 256:11008 + p * 256 + 256], rep1)
        zsn_marks[p] = rep1.n

    def p1_pe(p, half):
        u = 2 * p + half
        st = u % 2
        for c in range(4):
            cwl = [(rep1, zsn_marks[p]), (v0, XT_DONE)] if c == 0 else []
            if c == 0 and u >= 2:
                cwl += [(ea1, ea_marks[u - 2]), (ev1, ev_marks[u - 2])]
            P.op("tensor", lambda e, c=c, st=st, half=half, p=p: e.matmul(
                ps[4 * st + c][:],
                zsn_sb[0:32, (p % 2) * 512 + c * 128:(p % 2) * 512 + c * 128 + 128],
                xt32_sb[0:32, half * 512:half * 512 + 512],
                start=True, stop=False), waits=cwl)
        for kc in range(32):
            G = p * 8 + kc // 4
            sw = G % WRING
            wl = []
            if kc % 4 == 0:
                wl = [(dq1, dq_marks[G])]
            for c in range(4):
                P.op("tensor", lambda e, sw=sw, kc=kc, c=c, half=half, st=st: e.matmul(
                    ps[4 * st + c][:],
                    wbr[:, sw, (kc % 4) * 512 + c * 128:(kc % 4) * 512 + c * 128 + 128],
                    xnT[:, kc, half * 512:half * 512 + 512],
                    start=False, stop=(kc == 31)),
                    waits=wl if c == 0 else [])
        P.op("tensor", lambda e: e.drain(), incs=[(mm1, 1)])
        if half == 1:
            P.op("tensor", lambda e: e.nop(), incs=[(wf1, 8)])
        mm1_marks[u] = mm1.n

    def p1_act(p, half):
        u = 2 * p + half
        st = u % 2
        for c01 in range(2):
            wl = []
            if c01 == 0:
                wl.append((mm1, mm1_marks[u]))
                if u >= 1:
                    wl.append((ev1, ev_marks[u - 1]))
            P.op("scalar", lambda e, st=st, c01=c01: e.activation(
                out=sil_v[:, c01 * 512:c01 * 512 + 512], in_=ps[4 * st + c01][:],
                func=AF.Silu),
                waits=wl)
        P.op("scalar", lambda e: e.drain(), incs=[(ea1, 1)])
        ea_marks[u] = ea1.n

    def p1_evac(p, half):
        u = 2 * p + half
        st = u % 2
        for c01 in range(2):
            hs = 2 * (u % 2) + c01
            wl = [(ea1, ea_marks[u])]
            if u >= 2:
                wl.append((ho1, ho_marks.get((u - 2, c01), 0)))
            P.op("vector", lambda e, hs=hs, st=st, c01=c01: e.tensor_tensor(
                out=hbf_v[:, hs * 512:hs * 512 + 512],
                in0=sil_v[:, c01 * 512:c01 * 512 + 512],
                in1=ps[4 * st + 2 + c01][:], op=OP.mult),
                waits=wl, incs=[(ev1, 1)])
        ev_marks[u] = ev1.n

    def p1_hout(p, half):
        u = 2 * p + half
        for c01 in range(2):
            hs = 2 * (u % 2) + c01
            chunk = 2 * p + c01
            P.op("gpsimd", lambda e, hs=hs, chunk=chunk, half=half: e.dma_start(
                out=h_hbm[chunk * 128:chunk * 128 + 128,
                          half * 512:half * 512 + 512],
                in_=hbf_v[:, hs * 512:hs * 512 + 512]),
                waits=[(ev1, ev_marks[u] - 1 + c01)], incs=[(ho1, 16)])
            ho_marks[(u, c01)] = ho1.n

    zsn_marks = {}
    for p in range(NPAIR + 1):
        if p < NPAIR:
            p1_zsn(p)
            for hq in range(4):
                p1_dmas(p, hq)
                p1_deq(p, hq)
        if p >= 1:
            p1_pe(p - 1, 0)
            p1_act(p - 1, 0)
            p1_evac(p - 1, 0)
            p1_hout(p - 1, 0)
            p1_pe(p - 1, 1)
        if p < NPAIR:
            for hq in range(4, 8):
                p1_dmas(p, hq)
                p1_deq(p, hq)
        if p >= 1:
            p1_act(p - 1, 1)
            p1_evac(p - 1, 1)
            p1_hout(p - 1, 1)

    barrier(prewaits=[("gpsimd", ho1, ho1.n), ("vector", ev1, ev1.n),
                      ("scalar", ea1, ea1.n), ("tensor", ev1, ev1.n),
                      ("sync", rep1, rep1.n)])


    hre = h_hbm.ap().rearrange("(c p) t -> p c t", p=128)
    qd4 = qwdn_h.ap().rearrange("(kc p) w -> p kc w", p=128)
    wqdr = wqd_v.rearrange("p (C j c) -> p C j c", j=8, c=16)
    pkd_marks, repd_marks, wdq_marks = {}, {}, {}
    def p25_chunk(cch):
        s2 = cch % 2
        wl = [(wdq, cch - 1)] if cch >= 2 else []
        dma("sync", pkd_v[:, s2 * 512:s2 * 512 + 512], qd4[:, cch, :], pkd,
            waits=wl)
        pkd_marks[cch] = pkd.n
        dma("sync", zrd_v[:, s2 * 4096:s2 * 4096 + 4096],
            zdn2[cch:cch + 1, :].partition_broadcast(128).squeeze(1),
            repd, waits=wl)
        dma("sync", srd_v[:, s2 * 4096:s2 * 4096 + 4096],
            sdnb2[cch:cch + 1, :].partition_broadcast(128).squeeze(1), repd)
        repd_marks[cch] = repd.n

        def unpackd(e, s2=s2):
            last = None
            pin = pkd_v[:, s2 * 512:s2 * 512 + 512].rearrange(
                "p (C c) -> p C c", c=16)
            for j in range(8):
                last = e.tensor_scalar(
                    out=wqdr[:, :, j, :], in0=pin,
                    scalar1=28 - 4 * j, scalar2=28,
                    op0=OP.logical_shift_left, op1=OP.logical_shift_right)
            return last

        wl = [(pkd, pkd_marks[cch])]
        if cch >= 2:
            wl.append((wout, 16 * (cch - 1)))
        P.op("vector", unpackd, waits=wl, incs=[(vch, 1)])
        P.op("vector", lambda e, s2=s2: e.tensor_tensor(
            out=wdb_v[:, s2 * 4096:s2 * 4096 + 4096], in0=wqd_v,
            in1=zrd_v[:, s2 * 4096:s2 * 4096 + 4096], op=OP.subtract),
            waits=[(repd, repd_marks[cch]), (vch, vch.n)], incs=[(vch, 1)])
        P.op("vector", lambda e, s2=s2: e.tensor_tensor(
            out=wdb_v[:, s2 * 4096:s2 * 4096 + 4096],
            in0=wdb_v[:, s2 * 4096:s2 * 4096 + 4096],
            in1=srd_v[:, s2 * 4096:s2 * 4096 + 4096], op=OP.mult),
            waits=[(vch, vch.n)], incs=[(wdq, 1)])
        wdq_marks[cch] = wdq.n
        P.op("gpsimd", lambda e, cch=cch, s2=s2: e.dma_start(
            out=wdn_hbm[cch * 128:cch * 128 + 128, :],
            in_=wdb_v[:, s2 * 4096:s2 * 4096 + 4096]),
            waits=[(wdq, wdq_marks[cch])], incs=[(wout, 16)])

    cev_marks, cmm_marks = {}, {}
    p25_done = 0
    for k in range(64):
        tt, ip = k // 8, k % 8
        bank = ps[6 + k % 2]
        njc = min(32, 4 * ip + 4)
        wl = [(cev, cev_marks.get(k - 2, 0))]
        for jc in range(njc):
            P.op("tensor", lambda e, jc=jc, tt=tt, ip=ip, bank=bank: e.matmul(
                bank[:], xnT[:, jc, tt * 128:tt * 128 + 128],
                tbuf[:, 512 + 128 * (4 * ip - jc):512 + 128 * (4 * ip - jc) + 512],
                start=(jc == 0), stop=(jc == njc - 1)),
                waits=wl if jc == 0 else [])
        P.op("tensor", lambda e: e.drain(), incs=[(cmm, 1)])
        cmm_marks[k] = cmm.n
        P.op("vector", lambda e, k=k, bank=bank: e.tensor_copy(
            out=stile_v[:, (k % 2) * 512:(k % 2) * 512 + 512], in_=bank[:]),
            waits=[(cmm, cmm_marks[k]), (cso, 16 * (k - 1))], incs=[(cev, 1)])
        cev_marks[k] = cev.n
        P.op("gpsimd", lambda e, k=k, tt=tt, ip=ip: e.dma_start(
            out=s_hbm[tt * 128:tt * 128 + 128, ip * 512:ip * 512 + 512],
            in_=stile_v[:, (k % 2) * 512:(k % 2) * 512 + 512]),
            waits=[(cev, cev_marks[k])], incs=[(cso, 16)])
        want = (k + 1) * 86 // 64
        while p25_done < want:
            p25_chunk(p25_done)
            p25_done += 1


    hvr0 = h_v.rearrange("p (c t) -> p c t", c=86)
    for hc in range(86):
        dma("sync", hvr0[:, hc], hre[:, hc, 0:512], hld,
            waits=[(cmm, 64)] if hc == 0 else [])

    barrier(prewaits=[("gpsimd", wout, wout.n), ("vector", wdq, wdq.n),
                      ("gpsimd", cso, cso.n), ("vector", cev, cev.n),
                      ("tensor", cev, cev.n)])

    g3_marks, e3_marks, sx_marks, fin_marks = {}, {}, {}, {}
    mm3_marks = {}

    def p3_sx(u):
        half, panel = u // 8, u % 8
        for t in range(4):
            ttg = half * 4 + t
            slot = (u * 4 + t) % 8
            wl = []
            if u >= 2:
                wl = [(e3, e3_marks.get((u - 2, t), 0))]
            dma("scalar", sx_v[:, slot * 512:slot * 512 + 512],
                s_hbm[ttg * 128:ttg * 128 + 128, panel * 512:panel * 512 + 512],
                sx3, waits=wl)
            dma("scalar", xx_v[:, slot * 512:slot * 512 + 512],
                x_h[ttg * 128:ttg * 128 + 128, panel * 512:panel * 512 + 512],
                sx3)
            sx_marks[(u, t)] = sx3.n

    for u in range(16):
        half, panel = u // 8, u % 8
        if panel == 0 and half == 1:
            wl = [(mmd, 8 * half)]
            hvr = h_v.rearrange("p (c t) -> p c t", c=86)
            for hc in range(86):
                dma("sync", hvr[:, hc], hre[:, hc, half * 512:half * 512 + 512],
                    hld, waits=wl if hc == 0 else [])
        if u == 0:
            p3_sx(0)
        st = u % 2
        for cch in range(86):
            G = u * 86 + cch
            if G == 0:
                for Gp in range(4):
                    dma("sync", wt_v[:, (Gp % 24) * 512:(Gp % 24) * 512 + 512],
                        wdn_hbm[Gp * 128:Gp * 128 + 128, 0:512], wt3)
            Gn = G + 4
            if Gn < 16 * 86:
                un, cn = Gn // 86, Gn % 86
                gw = [(mm3, Gn - 19)] if Gn >= 20 else []
                dma("sync", wt_v[:, (Gn % 24) * 512:(Gn % 24) * 512 + 512],
                    wdn_hbm[cn * 128:cn * 128 + 128,
                            (un % 8) * 512:(un % 8) * 512 + 512], wt3, waits=gw)
            wl = [(wt3, 16 * min(G + 4, 16 * 86))]
            if cch == 0:
                wl.append((hld, 16 * 86 * (half + 1)))
                if u >= 1:
                    wl.append((g3s, g3s.n))
                    wl.append((e3, e3.n))
                    wl.append((fin, fin.n))
            sw = G % 24
            for t in range(4):
                P.op("tensor", lambda e, sw=sw, cch=cch, t=t, st=st, half=half: e.matmul(
                    ps[4 * st + t][:],
                    h_v[:, cch * 512 + t * 128:cch * 512 + t * 128 + 128],
                    wt_v[:, sw * 512:sw * 512 + 512],
                    start=(cch == 0), stop=(cch == 85)),
                    waits=wl if t == 0 else [],
                    incs=[(mm3, 1)] if t == 3 else [])
        P.op("tensor", lambda e: e.drain(), incs=[(mmd, 1)])
        mm3_marks[u] = mmd.n
        if u < 15:
            p3_sx(u + 1)
        for t in range(4):
            gsl = t % 2
            wl = [(mmd, mm3_marks[u])] if t == 0 else []
            wl.append((e3, e3.n))
            P.op("scalar", lambda e, gsl=gsl, st=st, t=t: e.activation(
                out=g3_v[:, gsl * 512:gsl * 512 + 512], in_=ps[4 * st + t][:],
                func=AF.Silu), waits=wl)
            P.op("scalar", lambda e: e.drain(), incs=[(g3s, 1)])
            g3_marks[u] = g3s.n
            slot = (u * 4 + t) % 8
            sxtgt = min(sx_marks[(u, t)] + 32, sx3.n)
            wl = [(g3s, g3s.n), (sx3, sxtgt)]
            if u >= 1:
                wl.append((fin, fin_marks.get((u - 1, t), 0)))
            P.op("vector", lambda e, gsl=gsl, slot=slot: e.tensor_tensor(
                out=t3_v[:, gsl * 512:gsl * 512 + 512].rearrange(
                    "p (C c j) -> p C c j", c=16, j=8),
                in0=g3_v[:, gsl * 512:gsl * 512 + 512].rearrange(
                    "p (C j c) -> p C j c", j=8, c=16).transpose([0, 1, 3, 2]),
                in1=sx_v[:, slot * 512:slot * 512 + 512].rearrange(
                    "p (C c j) -> p C c j", c=16, j=8), op=OP.mult),
                waits=wl, incs=[(vch, 1)])
            P.op("vector", lambda e, gsl=gsl, slot=slot, t=t: e.tensor_tensor(
                out=ob_v[:, t * 512:t * 512 + 512],
                in0=t3_v[:, gsl * 512:gsl * 512 + 512],
                in1=xx_v[:, slot * 512:slot * 512 + 512], op=OP.add),
                waits=[(vch, vch.n)], incs=[(e3, 1)])
            e3_marks[(u, t)] = e3.n
            ttg = half * 4 + t
            P.op("gpsimd", lambda e, t=t, ttg=ttg, panel=panel: e.dma_start(
                out=out_h[ttg * 128:ttg * 128 + 128, panel * 512:panel * 512 + 512],
                in_=ob_v[:, t * 512:t * 512 + 512]),
                waits=[(e3, e3_marks[(u, t)])], incs=[(fin, 16)])
            fin_marks[(u, t)] = fin.n

    P.op("gpsimd", lambda e: e.nop(), waits=[(fin, fin.n)])

    with nc.Block() as block:
        for ename in ENGINES:
            def emit(eng, ename=ename):
                for fn, wl, il in P.ops[ename]:
                    for h, vv in wl:
                        eng.wait_ge(h, vv)
                    if fn is not None:
                        inst = fn(eng)
                        for h, k in il:
                            inst.then_inc(h, k)
            getattr(block, ename)(emit)

    return nc


_PROG = None


def kernel(**inputs):
    global _PROG
    if _PROG is None:
        _PROG = build_program()
    nc = _PROG

    f = lambda a: np.ascontiguousarray(a, dtype=np.float32)
    x = f(inputs["L0_attn_u"]).reshape(8192, 4096)
    lam = np.concatenate([f(inputs["L0_attn_Lambda_ri"][:, 0]),
                          f(inputs["L0_attn_Lambda_ri"][:, 1])]).reshape(128, 1)
    vct = np.concatenate([f(inputs["L0_attn_Vc_ri"][..., 0]).T,
                          f(inputs["L0_attn_Vc_ri"][..., 1]).T], axis=1)
    ctpq = np.concatenate([
        f(inputs["L0_attn_Ct_ri"][0, :, 0]), f(inputs["L0_attn_Ct_ri"][0, :, 1]),
        f(inputs["L0_attn_p_ri"][:, 0]), f(inputs["L0_attn_p_ri"][:, 1]),
        f(inputs["L0_attn_q_ri"][:, 0]), f(inputs["L0_attn_q_ri"][:, 1])]
    ).reshape(1, 384)
    dls = np.array([[inputs["L0_attn_D"][0], inputs["L0_attn_log_step"][0]]],
                   dtype=np.float32)
    sg = f(inputs["L0_ffn_scales_gu"])
    sgupm = np.empty((NPAIR, 32, 512), np.float32)
    for p in range(NPAIR):
        sgupm[p, :, 0:256] = sg[:, p * 256:p * 256 + 256]
        sgupm[p, :, 256:512] = sg[:, 11008 + p * 256:11008 + p * 256 + 256]
    base = {
        "qwgu": np.ascontiguousarray(inputs["L0_ffn_qweight_gu"], np.int32),
        "qzgu": np.ascontiguousarray(inputs["L0_ffn_qzeros_gu"], np.int32).reshape(-1),
        "sgupm": sgupm.reshape(-1),
        "sgun": sg.reshape(-1),
        "qwdn": np.ascontiguousarray(inputs["L0_ffn_qweight_dn"], np.int32),
        "qzdn": np.ascontiguousarray(inputs["L0_ffn_qzeros_dn"], np.int32).reshape(-1),
        "sdn": f(inputs["L0_ffn_scales_dn"]).reshape(-1),
        "lam": lam, "vct": np.ascontiguousarray(vct),
        "bp": f(inputs["L0_attn_Bp"]), "ctpq": ctpq, "dls": dls,
    }
    for k, v in _CONSTS.items():
        base["c_" + k] = v

    in_maps = []
    for c in range(8):
        m = dict(base)
        m["x"] = np.ascontiguousarray(x[c * 1024:(c + 1) * 1024])
        in_maps.append(m)

    res = run_bass_kernel_spmd(nc, in_maps, core_ids=list(range(8)))
    out = np.concatenate([np.asarray(r["out"]) for r in res.results], axis=0)
    return out.reshape(2, 4096, 4096).astype(np.float32)


# version 11
# speedup vs baseline: 1.1920x; 1.0434x over previous
import numpy as np
import ml_dtypes
import concourse.bass as bass
import concourse.mybir as mybir
from concourse.bass_utils import run_bass_kernel_spmd

AF = mybir.ActivationFunctionType
OP = mybir.AluOpType
F32 = mybir.dt.float32
BF16 = mybir.dt.bfloat16
I32 = mybir.dt.int32
U8 = mybir.dt.uint8
XAX = mybir.AxisListType.X
bf16 = ml_dtypes.bfloat16

T = 1024
NPAIR = 43
WRING = 12
INTER = 11008
ENGINES = ("sync", "vector", "scalar", "tensor", "gpsimd")


def _host_consts():
    n = np.arange(4096, dtype=np.float64)
    Om = np.exp(-2j * np.pi * n / 4096.0)
    w = (1.0 - Om) / (1.0 + Om)
    c = 2.0 / (1.0 + Om)
    f = lambda a: np.ascontiguousarray(a, dtype=np.float32)
    w0 = np.arange(64)
    E1 = np.exp(2j * np.pi * np.outer(w0, w0) / 64.0)[:, ::-1]
    TW = np.exp(2j * np.pi * np.outer(w0, w0) / 4096.0)[:, ::-1]
    E2 = np.exp(2j * np.pi * np.outer(w0, w0) / 64.0)[:, ::-1]
    return {
        "wre": f(np.broadcast_to(w.real, (64, 4096))),
        "wim": f(np.broadcast_to(w.imag, (64, 4096))),
        "cre64": f(c.real.reshape(64, 64)),
        "cim64": f(c.imag.reshape(64, 64)),
        "E1c": f(np.block([[E1.real, E1.imag], [-E1.imag, E1.real]])),
        "TWr": f(TW.real), "TWi": f(TW.imag),
        "E2c": f(np.vstack([E2.real, -E2.imag])),
        "identf": f(np.eye(128)),
        "esl": np.ascontiguousarray(
            (np.arange(63)[None, :] == 31).astype(np.float32)
            * np.ones((128, 1), np.float32)).astype(bf16),
        "identb": np.eye(128, dtype=np.float32).astype(bf16),
    }


_CONSTS = _host_consts()


class Sem:
    def __init__(self, nc, name):
        self.h = nc.alloc_semaphore(name)
        self.name = name
        self.n = 0


class Plan:
    def __init__(self):
        self.ops = {e: [] for e in ENGINES}

    def op(self, eng, fn, waits=(), incs=()):
        wl = []
        for s, v in waits:
            v = max(0, v)
            assert v <= s.n, f"wait {s.name}>={v} but only {s.n} planned"
            if v > 0:
                wl.append((s.h, v))
        il = []
        for s, k in incs:
            s.n += k
            il.append((s.h, k))
        self.ops[eng].append((fn, wl, il))


def build_program(debug=False):
    nc = bass.Bass("TRN2", target_bir_lowering=False)

    x_h = nc.dram_tensor("x", [T, 4096], F32, kind="ExternalInput")
    qwgu_h = nc.dram_tensor("qwgu", [4096, 2752], I32, kind="ExternalInput")
    qzgu_h = nc.dram_tensor("qzgu", [32 * 2752], I32, kind="ExternalInput")
    sgupm_h = nc.dram_tensor("sgupm", [NPAIR * 32 * 512], F32, kind="ExternalInput")
    sgun_h = nc.dram_tensor("sgun", [32 * 22016], F32, kind="ExternalInput")
    qwdn_h = nc.dram_tensor("qwdn", [INTER, 512], I32, kind="ExternalInput")
    qzdn_h = nc.dram_tensor("qzdn", [86 * 512], I32, kind="ExternalInput")
    sdn_h = nc.dram_tensor("sdn", [86 * 4096], F32, kind="ExternalInput")
    lam_h = nc.dram_tensor("lam", [128, 1], F32, kind="ExternalInput")
    vct_h = nc.dram_tensor("vct", [64, 128], F32, kind="ExternalInput")
    bp_h = nc.dram_tensor("bp", [64, 1], F32, kind="ExternalInput")
    ctpq_h = nc.dram_tensor("ctpq", [1, 384], F32, kind="ExternalInput")
    dls_h = nc.dram_tensor("dls", [1, 2], F32, kind="ExternalInput")
    c_h = {k: nc.dram_tensor("c_" + k, list(v.shape),
                             BF16 if v.dtype == bf16 else F32, kind="ExternalInput")
           for k, v in _CONSTS.items()}
    out_h = nc.dram_tensor("out", [T, 4096], F32, kind="ExternalOutput")

    _k = dict(kind="ExternalOutput") if debug else {}
    if debug:
        mrow_dbg = nc.dram_tensor("mrow_dbg", [128, 64], F32, kind="ExternalOutput")
        xt2_dbg = nc.dram_tensor("xt2_dbg", [128, 4096], F32, kind="ExternalOutput")
        xn2_dbg = nc.dram_tensor("xn2_dbg", [128, 4096], F32, kind="ExternalOutput")
    h_hbm = nc.dram_tensor("h_scr", [INTER, T], BF16, **_k)
    s_hbm = nc.dram_tensor("s_scr", [T, 4096], F32, **_k)
    wdn_hbm = nc.dram_tensor("wdn_scr", [INTER, 4096], BF16, **_k)
    krev_hbm = nc.dram_tensor("krev_scr", [4352], F32, **_k)
    kk_hbm = nc.dram_tensor("kk_scr", [8, 4096], F32, **_k)
    zsneg_hbm = nc.dram_tensor("zsneg_scr", [32 * 22016], BF16)
    sgupm_b_hbm = nc.dram_tensor("sgupm_b_scr", [NPAIR * 32 * 512], BF16)
    zdn_hbm = nc.dram_tensor("zdn_scr", [86 * 4096], BF16)
    sdnb_hbm = nc.dram_tensor("sdnb_scr", [86 * 4096], BF16)
    zdn2 = zdn_hbm.ap().rearrange("(g d) -> g d", d=4096)
    sdnb2 = sdnb_hbm.ap().rearrange("(g d) -> g d", d=4096)
    bnc_hbm = nc.dram_tensor("bounce_scr", [2048], F32, **_k)

    cs = {k: nc.alloc_sbuf_tensor("k_" + k, list(v.shape),
                                  BF16 if v.dtype == bf16 else F32)
          for k, v in _CONSTS.items() if k not in ("wre", "wim")}
    lam_sb = nc.alloc_sbuf_tensor("lam_sb", [128, 1], F32)
    vct_sb = nc.alloc_sbuf_tensor("vct_sb", [64, 128], F32)
    bp_sb = nc.alloc_sbuf_tensor("bp_sb", [64, 1], F32)
    ctpq_sb = nc.alloc_sbuf_tensor("ctpq_sb", [1, 384], F32)
    dls_sb = nc.alloc_sbuf_tensor("dls_sb", [1, 2], F32)
    step_sb = nc.alloc_sbuf_tensor("step_sb", [1, 4], F32)
    ts64_sb = nc.alloc_sbuf_tensor("ts64_sb", [64, 1], F32)
    d128_sb = nc.alloc_sbuf_tensor("d128_sb", [128, 1], F32)
    vmat_sb = nc.alloc_sbuf_tensor("vmat_sb", [128, 8], F32)
    mrow_sb = nc.alloc_sbuf_tensor("mrow_sb", [128, 8], F32)
    mrowall_sb = nc.alloc_sbuf_tensor("mrowall_sb", [128, 64], F32)
    hank_sb = nc.alloc_sbuf_tensor("hank_sb", [128, 256], F32)
    xt32_sb = nc.alloc_sbuf_tensor("xt32_sb", [32, 1024], BF16)
    zsn_sb = nc.alloc_sbuf_tensor("zsn_sb", [32, 2 * 512], BF16)

    ARENA = 161792
    arena = nc.alloc_sbuf_tensor("arena", [128, ARENA], U8)

    def view(off, parts, n, dt):
        esz = {F32: 4, I32: 4, BF16: 2, U8: 1}[dt]
        assert off % 4 == 0 and off + n * esz <= ARENA, (off, n)
        ap = arena[0:parts, off:off + n * esz]
        if dt != U8:
            ap = ap.bitcast(dt)
        return ap

    xnT = view(0, 128, 32768, BF16).rearrange("p (j t) -> p j t", t=1024)
    tbuf = view(65536, 128, 4608, BF16)
    C0 = 74752
    zq_gu = view(90112, 128, 688, I32)
    zi_gu = view(90112 + 2752, 128, 5504, I32)
    zb_gu = view(90112 + 24768, 128, 5504, BF16)
    sf2_gu = view(90112 + 35776, 128, 5504, F32)
    zb2_gu = view(90112 + 57792, 128, 5504, BF16)
    sf_gu = view(90112, 128, 5504, F32)
    sb_gu = view(90112 + 22016, 128, 5504, BF16)
    zq_dn = view(90112, 86, 512, I32)
    zi_dn = view(90112 + 2048, 86, 4096, I32)
    zb_dn = view(90112 + 18432, 86, 4096, BF16)
    sf_dn = view(90112 + 26624, 86, 4096, F32)
    sb_dn = view(90112 + 43008, 86, 4096, BF16)
    wre_v = view(90112, 64, 1024, F32)
    wim_v = view(90112 + 4096, 64, 1024, F32)
    dre_v = view(90112 + 8192, 64, 1024, F32)
    dim_v = view(90112 + 12288, 64, 1024, F32)
    rin_v = view(90112 + 16384, 64, 1024, F32)
    tmp_v = view(90112 + 20480, 64, 1024, F32)
    rcat_v = view(90112 + 24576, 128, 1024, F32)
    kk8_v = view(90112 + 28672, 8, 4096, F32)
    vstage_v = view(90112 + 45056, 1, 1024, F32)
    vrow_v = view(90112 + 49152, 1, 1024, F32)
    brow_v = view(90112 + 53248, 1, 128, F32)
    bst_v = view(90112 + 53760, 128, 1, F32)
    ast_v = view(90112 + 54272, 128, 64, F32)
    s1_v = view(90112 + 54528, 64, 128, F32)
    s2_v = view(90112 + 55040, 128, 64, F32)
    krev_v = view(90112 + 55296, 64, 64, F32)
    krevt_v = view(90112 + 55552, 64, 64, F32)
    KCN = ["k00r", "k00i", "k01r", "k01i", "k10r", "k10i", "k11r", "k11i",
           "ta", "tb", "tc", "td", "te", "tf", "atr", "ati"]
    kc_v = {nm: view(90112 + 55808 + i * 256, 64, 64, F32) for i, nm in enumerate(KCN)}
    xt_v = view(C0, 128, 2 * 4096, F32)
    xn_v = view(C0 + 32768, 128, 2 * 4096, F32)
    wbf_v = view(C0, 128, WRING * 2048, BF16)
    pk_v = view(C0 + 49152, 128, 4 * 256, I32)
    wq_v = view(C0 + 53248, 128, 2048, I32)
    sr_v = view(C0 + 69632, 128, 2 * 2048, BF16)
    hbf_v = view(C0 + 77824, 128, 4 * 512, BF16)
    sil_v = view(C0 + 81920, 128, 2 * 512, F32)
    stile_v = view(157696, 128, 2 * 512, F32)
    pkd_v = view(88064, 128, 2 * 512, I32)
    wqd_v = view(92160, 128, 4096, I32)
    zrd_v = view(108544, 128, 2 * 4096, BF16)
    srd_v = view(124928, 128, 2 * 4096, BF16)
    wdb_v = view(141312, 128, 2 * 4096, BF16)
    h_v = view(0, 128, 86 * 512, BF16)
    wt_v = view(88064, 128, 24 * 512, BF16)
    sx_v = view(112640, 128, 8 * 512, F32)
    xx_v = view(129024, 128, 8 * 512, F32)
    ob_v = view(145408, 128, 4 * 512, F32)
    g3_v = view(153600, 128, 2 * 512, F32)
    t3_v = view(157696, 128, 2 * 512, F32)

    ps = [nc.alloc_psum_tensor("ps%d" % i, [128, 512], F32) for i in range(8)]

    SN = ["ld0", "v0", "a0", "t0", "g0", "ph",
          "pk1", "rep1", "dq1", "wf1", "mm1", "ea1", "ev1", "ho1",
          "cmm", "cev", "cso",
          "pkd", "repd", "wdq", "wout",
          "hld", "wt3", "sx3", "mm3", "g3", "e3", "fin", "vch", "mmd"]
    S = {name: Sem(nc, "s_" + name) for name in SN}
    (ld0, v0, a0, t0, g0, ph, pk1, rep1, dq1, wf1, mm1, ea1, ev1, ho1,
     cmm, cev, cso, pkd, repd, wdq, wout, hld, wt3, sx3, mm3, g3s, e3,
     fin, vch, mmd) = [S[n] for n in SN]
    P = Plan()

    def barrier(prewaits=()):
        pw = {e: [] for e in ENGINES}
        for e, s, vv in prewaits:
            assert vv <= s.n
            pw[e].append((s.h, vv))
        for e in ENGINES:
            ph.n += 1
            wl = pw[e]
            if e in ("tensor", "scalar", "vector", "gpsimd"):
                P.ops[e].append((lambda eng: eng.drain(), [], []))
            P.ops[e].append(
                (lambda eng: eng.sem_inc(ph.h, 1), wl, []))
        tgt = ph.n
        for e in ENGINES:
            P.ops[e].append((None, [(ph.h, tgt)], []))

    def vchain(steps, waits0=(), incs_last=()):
        for i, fn in enumerate(steps):
            w = list(waits0) + [(vch, vch.n)] if i == 0 else [(vch, vch.n)]
            inc = list(incs_last) if i == len(steps) - 1 else [(vch, 1)]
            P.op("vector", fn, waits=w, incs=inc)

    def dma(eng, dst, src, sem, waits=()):
        P.op(eng, lambda e, d=dst, s=src: e.dma_start(out=d, in_=s),
             waits=waits, incs=[(sem, 16)])

    small = [(lam_sb[:], lam_h[:]), (vct_sb[:], vct_h[:]), (bp_sb[:], bp_h[:]),
             (ctpq_sb[:], ctpq_h[:]), (dls_sb[:], dls_h[:])]
    small += [(cs[k][:], c_h[k][:]) for k in cs]
    for dst, src in small:
        dma("sync", dst, src, ld0)
    N_SMALL = ld0.n

    dma("sync", zq_gu, qzgu_h.ap().rearrange("(p w) -> p w", p=128), ld0)
    zq_mark = ld0.n
    wvj = zi_gu.rearrange("p (w j) -> p w j", j=8)
    for j in range(8):
        P.op("vector",
             lambda e, j=j: e.tensor_scalar(
                 out=wvj[:, :, j], in0=zq_gu, scalar1=28 - 4 * j, scalar2=28,
                 op0=OP.logical_shift_left, op1=OP.logical_shift_right),
             waits=[(ld0, zq_mark)])
    P.op("vector", lambda e: e.tensor_copy(out=zb_gu, in_=zi_gu), incs=[(v0, 1)])
    dma("sync", sf2_gu, bass.AP(sgun_h, 0, [[5504, 128], [1, 5504]]), ld0)
    sf2_mark = ld0.n
    P.op("vector", lambda e: e.scalar_tensor_tensor(
        out=zb2_gu, in0=zb_gu, scalar=-1.0, in1=sf2_gu, op0=OP.mult, op1=OP.mult),
        waits=[(ld0, sf2_mark), (vch, vch.n)], incs=[(v0, 1)])
    P.op("gpsimd", lambda e: e.dma_start(
        out=zsneg_hbm.ap().rearrange("(p w) -> p w", p=128), in_=zb2_gu),
        waits=[(v0, v0.n)], incs=[(g0, 16)])
    ZPM_G0 = g0.n

    dma("sync", sf_gu, bass.AP(sgupm_h, 0, [[5504, 128], [1, 5504]]), ld0,
        waits=[(g0, ZPM_G0)])
    P.op("vector", lambda e: e.tensor_copy(out=sb_gu, in_=sf_gu),
         waits=[(ld0, ld0.n)], incs=[(v0, 1)])
    P.op("gpsimd", lambda e: e.dma_start(
        out=bass.AP(sgupm_b_hbm, 0, [[5504, 128], [1, 5504]]), in_=sb_gu),
        waits=[(v0, v0.n)], incs=[(g0, 16)])
    SPM_G0 = g0.n

    dma("sync", zq_dn, qzdn_h.ap().rearrange("(g w) -> g w", g=86), ld0,
        waits=[(g0, SPM_G0)])
    dma("sync", sf_dn, bass.AP(sdn_h, 0, [[4096, 86], [1, 4096]]), ld0)
    dn_mark = ld0.n
    wvj2 = zi_dn.rearrange("g (C j c) -> g C j c", j=8, c=16)
    zqr = zq_dn.rearrange("g (C c) -> g C c", c=16)
    for j in range(8):
        P.op("vector",
             lambda e, j=j: e.tensor_scalar(
                 out=wvj2[:, :, j, :], in0=zqr, scalar1=28 - 4 * j, scalar2=28,
                 op0=OP.logical_shift_left, op1=OP.logical_shift_right),
             waits=[(ld0, dn_mark)])
    P.op("vector", lambda e: e.tensor_copy(out=zb_dn, in_=zi_dn), incs=[(v0, 1)])
    zbdn_v0 = v0.n
    P.op("vector", lambda e: e.tensor_copy(
        out=sb_dn.rearrange("g (C j c) -> g C j c", j=8, c=16),
        in_=sf_dn.rearrange("g (C c j) -> g C c j", c=16, j=8).transpose([0, 1, 3, 2])),
        incs=[(v0, 1)])
    sbdn_v0 = v0.n
    P.op("gpsimd", lambda e: e.dma_start(
        out=zdn_hbm.ap().rearrange("(g w) -> g w", g=86), in_=zb_dn),
        waits=[(v0, zbdn_v0)], incs=[(g0, 16)])
    P.op("gpsimd", lambda e: e.dma_start(
        out=sdnb_hbm.ap().rearrange("(g w) -> g w", g=86), in_=sb_dn),
        waits=[(v0, sbdn_v0)], incs=[(g0, 16)])
    STAGE_G0 = g0.n

    P.op("scalar", lambda e: e.activation(out=step_sb[:, 0:1], in_=dls_sb[:, 1:2],
                                          func=AF.Exp),
         waits=[(ld0, N_SMALL)])
    P.op("scalar", lambda e: e.drain(), incs=[(a0, 1)])
    vchain([
        lambda e: e.reciprocal(out=step_sb[:, 1:2], in_=step_sb[:, 0:1]),
        lambda e: e.tensor_scalar(out=step_sb[:, 2:3], in0=step_sb[:, 1:2],
                                  scalar1=2.0, scalar2=None, op0=OP.mult),
    ], waits0=[(a0, a0.n)], incs_last=[(v0, 1)])
    P.op("gpsimd", lambda e: e.dma_start(out=bnc_hbm[0:1], in_=step_sb[0:1, 2:3]),
         waits=[(v0, v0.n)], incs=[(g0, 16)])
    P.op("gpsimd", lambda e: e.dma_start(out=bnc_hbm[1:2], in_=dls_sb[0:1, 0:1]),
         incs=[(g0, 16)])
    P.op("gpsimd", lambda e: e.dma_start(
        out=ts64_sb[:], in_=bass.AP(bnc_hbm, 0, [[0, 64], [1, 1]])),
        waits=[(g0, g0.n)], incs=[(g0, 16)])
    P.op("gpsimd", lambda e: e.dma_start(
        out=d128_sb[:], in_=bass.AP(bnc_hbm, 1, [[0, 128], [1, 1]])),
        incs=[(g0, 16)])
    BNC_G0 = g0.n

    P.op("tensor", lambda e: e.matmul(ps[0][:, 0:1], vct_sb[:], bp_sb[:],
                                      start=True, stop=True),
         waits=[(ld0, N_SMALL)])
    P.op("tensor", lambda e: e.drain(), incs=[(t0, 1)])
    P.op("vector", lambda e: e.tensor_copy(out=bst_v, in_=ps[0][:, 0:1]),
         waits=[(t0, t0.n), (g0, STAGE_G0)], incs=[(v0, 1)])
    P.op("gpsimd", lambda e: e.dma_start(out=bnc_hbm[128:256], in_=bst_v),
         waits=[(v0, v0.n)], incs=[(g0, 16)])
    P.op("gpsimd", lambda e: e.dma_start(out=brow_v, in_=bnc_hbm[128:256]),
         waits=[(g0, g0.n)], incs=[(g0, 16)])
    BROW_G0 = g0.n

    ct_re, ct_im = ctpq_sb[:, 0:64], ctpq_sb[:, 64:128]
    p_re, p_im = ctpq_sb[:, 128:192], ctpq_sb[:, 192:256]
    q_re, q_im = ctpq_sb[:, 256:320], ctpq_sb[:, 320:384]
    b_re, b_im = brow_v[:, 0:64], brow_v[:, 64:128]
    w1, w2 = vrow_v[:, 0:64], vrow_v[:, 64:128]
    vre, vim = vrow_v[:, 128:192], vrow_v[:, 192:256]
    vst = vstage_v.rearrange("o (n c) -> o n c", c=8)
    vsteps = []
    for kk, (ar, ai, br, bi) in enumerate(
            [(ct_re, ct_im, b_re, b_im), (ct_re, ct_im, p_re, p_im),
             (q_re, q_im, b_re, b_im), (q_re, q_im, p_re, p_im)]):
        cA, cB = 2 * kk, 2 * kk + 1
        def m1(e, ar=ar, br=br, ai=ai, bi=bi):
            e.tensor_tensor(out=w1, in0=ar, in1=br, op=OP.mult)
            return e.tensor_tensor(out=w2, in0=ai, in1=bi, op=OP.mult)
        def m2(e):
            return e.tensor_tensor(out=vre, in0=w1, in1=w2, op=OP.add)
        def m3(e, ar=ar, bi=bi, ai=ai, br=br):
            e.tensor_tensor(out=w1, in0=ar, in1=bi, op=OP.mult)
            return e.tensor_tensor(out=w2, in0=ai, in1=br, op=OP.mult)
        def m4(e):
            return e.tensor_tensor(out=vim, in0=w1, in1=w2, op=OP.subtract)
        def m5(e, cA=cA, cB=cB):
            e.tensor_copy(out=vst[:, 0:64, cA], in_=vre)
            e.tensor_copy(out=vst[:, 64:128, cA], in_=vim)
            e.tensor_copy(out=vst[:, 0:64, cB], in_=vim)
            return e.tensor_scalar(out=vst[:, 64:128, cB], in0=vre,
                                   scalar1=-1.0, scalar2=None, op0=OP.mult)
        vsteps += [m1, m2, m3, m4, m5]
    vchain(vsteps, waits0=[(g0, BROW_G0)], incs_last=[(v0, 1)])
    P.op("gpsimd", lambda e: e.dma_start(out=bnc_hbm[256:1280], in_=vstage_v),
         waits=[(v0, v0.n)], incs=[(g0, 16)])
    P.op("gpsimd", lambda e: e.dma_start(
        out=vmat_sb[:], in_=bass.AP(bnc_hbm, 256, [[8, 128], [1, 8]])),
        waits=[(g0, g0.n)], incs=[(g0, 16)])
    VMAT_G0 = g0.n

    deno_marks = {}
    for q in range(4):
        sl = slice(q * 1024, q * 1024 + 1024)
        qwl = [(v0, sbdn_v0)] if q == 0 else [(v0, deno_marks[q - 1])]
        dma("sync", wre_v, c_h["wre"][0:64, sl], ld0, waits=qwl)
        dma("sync", wim_v, c_h["wim"][0:64, sl], ld0)
        qld = ld0.n

        def dn1(e):
            e.tensor_scalar(out=dre_v, in0=wre_v, scalar1=ts64_sb[:],
                            scalar2=lam_sb[0:64, :], op0=OP.mult, op1=OP.subtract)
            return e.tensor_scalar(out=dim_v, in0=wim_v, scalar1=ts64_sb[:],
                                   scalar2=lam_sb[64:128, :], op0=OP.mult,
                                   op1=OP.subtract)
        def dn2(e):
            e.tensor_tensor(out=rin_v, in0=dre_v, in1=dre_v, op=OP.mult)
            return e.tensor_tensor(out=tmp_v, in0=dim_v, in1=dim_v, op=OP.mult)
        def dn3(e):
            return e.tensor_tensor(out=rin_v, in0=rin_v, in1=tmp_v, op=OP.add)
        def dn4(e):
            return e.reciprocal(out=rin_v, in_=rin_v)
        def dn5(e):
            e.tensor_tensor(out=rcat_v[0:64, :], in0=dre_v, in1=rin_v, op=OP.mult)
            return e.tensor_tensor(out=rcat_v[64:128, :], in0=dim_v, in1=rin_v,
                                   op=OP.mult)
        vchain([dn1, dn2, dn3, dn4, dn5],
               waits0=[(ld0, qld), (g0, BNC_G0), (t0, t0.n)], incs_last=[(v0, 1)])
        deno_marks[q] = v0.n
        for half in range(2):
            P.op("tensor", lambda e, hh=half: e.matmul(
                ps[1][0:8, :], vmat_sb[:], rcat_v[:, hh * 512:hh * 512 + 512],
                start=True, stop=True),
                waits=[(v0, v0.n), (cev, cev.n), (g0, VMAT_G0)])
            P.op("tensor", lambda e: e.drain(), incs=[(t0, 1)])
            P.op("vector", lambda e, q=q, hh=half: e.tensor_copy(
                out=kk8_v[:, (q * 2 + hh) * 512:(q * 2 + hh) * 512 + 512],
                in_=ps[1][0:8, :]),
                waits=[(t0, t0.n)], incs=[(cev, 1)])

    P.op("gpsimd", lambda e: e.dma_start(out=kk_hbm[:], in_=kk8_v),
         waits=[(cev, cev.n)], incs=[(g0, 16)])
    for r, nm in enumerate(KCN[:8]):
        P.op("gpsimd", lambda e, r=r, nm=nm: e.dma_start(
            out=kc_v[nm], in_=kk_hbm[r].rearrange("(a b) -> a b", a=64)),
            waits=[(g0, g0.n)], incs=[(g0, 16)])
    KC_G0 = g0.n

    K_ = kc_v
    def _tt(o, a, b, op):
        return lambda e: e.tensor_tensor(out=K_[o], in0=K_[a], in1=K_[b], op=op)
    at_steps = [
        lambda e: e.tensor_scalar(out=K_["ta"], in0=K_["k11r"], scalar1=1.0,
                                  scalar2=None, op0=OP.add),
        _tt("tb", "ta", "ta", OP.mult),
        _tt("tc", "k11i", "k11i", OP.mult),
        _tt("tb", "tb", "tc", OP.add),
        lambda e: e.reciprocal(out=K_["tb"], in_=K_["tb"]),
        _tt("tc", "ta", "tb", OP.mult),
        _tt("td", "k11i", "tb", OP.mult),
        _tt("ta", "k01r", "tc", OP.mult),
        _tt("tb", "k01i", "td", OP.mult),
        _tt("te", "ta", "tb", OP.add),
        _tt("ta", "k01i", "tc", OP.mult),
        _tt("tb", "k01r", "td", OP.mult),
        _tt("tf", "ta", "tb", OP.subtract),
        _tt("ta", "te", "k10r", OP.mult),
        _tt("tb", "tf", "k10i", OP.mult),
        _tt("ta", "ta", "tb", OP.subtract),
        _tt("tc", "te", "k10i", OP.mult),
        _tt("td", "tf", "k10r", OP.mult),
        _tt("tc", "tc", "td", OP.add),
        _tt("ta", "k00r", "ta", OP.subtract),
        _tt("tc", "k00i", "tc", OP.subtract),
        lambda e: e.tensor_tensor(out=K_["te"], in0=cs["cre64"][:], in1=K_["ta"], op=OP.mult),
        lambda e: e.tensor_tensor(out=K_["tf"], in0=cs["cim64"][:], in1=K_["tc"], op=OP.mult),
        _tt("atr", "te", "tf", OP.subtract),
        lambda e: e.tensor_tensor(out=K_["te"], in0=cs["cre64"][:], in1=K_["tc"], op=OP.mult),
        lambda e: e.tensor_tensor(out=K_["tf"], in0=cs["cim64"][:], in1=K_["ta"], op=OP.mult),
        _tt("ati", "te", "tf", OP.add),
        lambda e: e.tensor_copy(out=ast_v[0:64, :], in_=K_["atr"]),
        lambda e: e.tensor_copy(out=ast_v[64:128, :], in_=K_["ati"]),
    ]
    vchain(at_steps, waits0=[(g0, KC_G0)], incs_last=[(v0, 1)])
    if debug:
        ast_dbg = nc.dram_tensor("ast_dbg", [128, 64], F32, kind="ExternalOutput")
        rc_dbg = nc.dram_tensor("rc_dbg", [128, 1024], F32, kind="ExternalOutput")
        P.op("gpsimd", lambda e: e.dma_start(out=ast_dbg[:], in_=ast_v),
             waits=[(v0, v0.n)], incs=[(g0, 16)])
        P.op("gpsimd", lambda e: e.dma_start(out=rc_dbg[:], in_=rcat_v),
             waits=[(v0, v0.n)], incs=[(g0, 16)])
    P.op("tensor", lambda e: e.matmul(ps[1][0:64, 0:128], ast_v, cs["E1c"][:],
                                      start=True, stop=True),
         waits=[(v0, v0.n)])
    P.op("tensor", lambda e: e.drain(), incs=[(t0, 1)])
    P.op("vector", lambda e: e.tensor_copy(out=s1_v, in_=ps[1][0:64, 0:128]),
         waits=[(t0, t0.n)], incs=[(v0, 1)])

    s1r, s1i = s1_v[:, 0:64], s1_v[:, 64:128]
    twa, twb = kc_v["ta"], kc_v["tb"]
    tw_steps = [
        lambda e: (e.tensor_tensor(out=twa, in0=s1r, in1=cs["TWr"][:], op=OP.mult),
                   e.tensor_tensor(out=twb, in0=s1i, in1=cs["TWi"][:], op=OP.mult))[-1],
        lambda e: e.tensor_tensor(out=s2_v[0:64, :], in0=twa, in1=twb, op=OP.subtract),
        lambda e: (e.tensor_tensor(out=twa, in0=s1r, in1=cs["TWi"][:], op=OP.mult),
                   e.tensor_tensor(out=twb, in0=s1i, in1=cs["TWr"][:], op=OP.mult))[-1],
        lambda e: e.tensor_tensor(out=s2_v[64:128, :], in0=twa, in1=twb, op=OP.add),
    ]
    vchain(tw_steps, incs_last=[(v0, 1)])
    P.op("tensor", lambda e: e.matmul(ps[1][0:64, 0:64], s2_v, cs["E2c"][:],
                                      start=True, stop=True),
         waits=[(v0, v0.n)])
    P.op("tensor", lambda e: e.drain(), incs=[(t0, 1)])
    P.op("vector", lambda e: e.tensor_scalar(
        out=krev_v, in0=ps[1][0:64, 0:64], scalar1=1.0 / 4096.0, scalar2=None,
        op0=OP.mult), waits=[(t0, t0.n)], incs=[(v0, 1)])
    P.op("tensor", lambda e: e.transpose(ps[2][0:64, 0:64], krev_v,
                                         cs["identf"][0:64, 0:64]),
         waits=[(v0, v0.n)])
    P.op("tensor", lambda e: e.drain(), incs=[(t0, 1)])
    P.op("vector", lambda e: e.tensor_copy(out=krevt_v, in_=ps[2][0:64, 0:64]),
         waits=[(t0, t0.n)], incs=[(v0, 1)])
    P.op("vector", lambda e: e.memset(vrow_v[:, 0:256], 0.0), incs=[(v0, 1)])
    P.op("gpsimd", lambda e: e.dma_start(
        out=bass.AP(krev_hbm, 0, [[64, 64], [1, 64]]), in_=krevt_v),
        waits=[(v0, v0.n)], incs=[(g0, 16)])
    P.op("gpsimd", lambda e: e.dma_start(out=krev_hbm[4096:4352],
                                         in_=vrow_v[0:1, 0:256]),
         incs=[(g0, 16)])
    KREV_G0 = g0.n

    P.op("vector", lambda e: e.memset(tbuf[:, 0:512], 0.0), incs=[(v0, 1)])
    hk_marks = []
    for d in range(32):
        wl = [(g0, KREV_G0)]
        if d >= 2:
            wl.append((v0, hk_marks[d - 2]))
        dma("sync", hank_sb[:, (d % 2) * 128:(d % 2) * 128 + 128],
            bass.AP(krev_hbm, 3968 - 128 * d, [[1, 128], [1, 128]]), ld0,
            waits=wl)
        mk = ld0.n
        P.op("vector", lambda e, d=d: e.tensor_copy(
            out=tbuf[:, 512 + 128 * d:640 + 128 * d],
            in_=bass.AP(hank_sb, (d % 2) * 128 + 127, [[256, 128], [-1, 128]])),
            waits=[(ld0, mk)], incs=[(v0, 1)])
        hk_marks.append(v0.n)
    P.op("vector", lambda e: e.scalar_tensor_tensor(
        out=tbuf[:, 512:640],
        in0=cs["identb"][:], scalar=d128_sb[:], in1=tbuf[:, 512:640],
        op0=OP.mult, op1=OP.add),
        waits=[(g0, BNC_G0)], incs=[(v0, 1)])
    TBUF_V0 = v0.n
    TBUF_T0 = t0.n

    aff_marks = {}
    tp_marks = {}
    tpe_marks = {}
    for tt in range(8):
        slot = tt % 2
        xts = xt_v[:, slot * 4096:slot * 4096 + 4096]
        xns = xn_v[:, slot * 4096:slot * 4096 + 4096]
        wl = [(v0, TBUF_V0)] if tt < 2 else [(v0, aff_marks[tt - 2])]
        dma("sync", xts, x_h[tt * 128:tt * 128 + 128, :], ld0, waits=wl)
        xmk = ld0.n
        awl = [(ld0, xmk)]
        if tt >= 2:
            awl.append((t0, tpe_marks[tt - 2]))
        else:
            awl += [(v0, TBUF_V0), (t0, TBUF_T0), (g0, KREV_G0)]
        P.op("scalar", lambda e, xns=xns, xts=xts: e.activation(
            out=xns, in_=xts, func=AF.Square), waits=awl)
        P.op("scalar", lambda e: e.drain(), incs=[(a0, 1)])
        amk = a0.n

        def st1(e, xts=xts, xns=xns):
            e.tensor_reduce(out=mrow_sb[:, 0:1], in_=xts, axis=XAX, op=OP.add)
            return e.tensor_reduce(out=mrow_sb[:, 1:2], in_=xns, axis=XAX, op=OP.add)
        def st2(e):
            e.tensor_scalar(out=mrow_sb[:, 2:3], in0=mrow_sb[:, 0:1],
                            scalar1=1.0 / 4096.0, scalar2=None, op0=OP.mult)
            return e.tensor_scalar(out=mrow_sb[:, 3:4], in0=mrow_sb[:, 1:2],
                                   scalar1=1.0 / 4096.0, scalar2=None, op0=OP.mult)
        def st3(e):
            return e.tensor_tensor(out=mrow_sb[:, 4:5], in0=mrow_sb[:, 2:3],
                                   in1=mrow_sb[:, 2:3], op=OP.mult)
        def st4(e):
            return e.tensor_tensor(out=mrow_sb[:, 5:6], in0=mrow_sb[:, 3:4],
                                   in1=mrow_sb[:, 4:5], op=OP.subtract)
        vchain([st1, st2, st3, st4], waits0=[(ld0, xmk), (a0, amk)],
               incs_last=[(v0, 1)])
        P.op("scalar", lambda e: e.activation(out=mrow_sb[:, 6:7],
                                              in_=mrow_sb[:, 5:6],
                                              func=AF.Sqrt, bias=np.float32(1e-5)),
             waits=[(v0, v0.n)])
        P.op("scalar", lambda e: e.drain(), incs=[(a0, 1)])

        def rs1(e):
            return e.reciprocal(out=mrow_sb[:, 7:8], in_=mrow_sb[:, 6:7])
        def rs2(e):
            return e.tensor_tensor(out=mrow_sb[:, 0:1], in0=mrow_sb[:, 2:3],
                                   in1=mrow_sb[:, 7:8], op=OP.mult)
        def rs3(e):
            return e.tensor_scalar(out=mrow_sb[:, 1:2], in0=mrow_sb[:, 0:1],
                                   scalar1=-1.0, scalar2=None, op0=OP.mult)
        vchain([rs1, rs2, rs3], waits0=[(a0, a0.n)], incs_last=[(v0, 1)])
        P.op("vector", lambda e, tt=tt: e.tensor_copy(
            out=mrowall_sb[:, tt * 8:tt * 8 + 8], in_=mrow_sb[:, 0:8]),
            waits=[(v0, v0.n)])
        P.op("vector", lambda e, xns=xns, xts=xts: e.tensor_scalar(
            out=xns, in0=xts, scalar1=mrow_sb[:, 7:8], scalar2=mrow_sb[:, 1:2],
            op0=OP.mult, op1=OP.add), waits=[(v0, v0.n)], incs=[(v0, 1)])
        aff_marks[tt] = v0.n
        if debug and tt == 2:
            P.op("gpsimd", lambda e, xts=xts: e.dma_start(out=xt2_dbg[:], in_=xts),
                 waits=[(v0, aff_marks[tt])], incs=[(g0, 16)])
            P.op("gpsimd", lambda e, xns=xns: e.dma_start(out=xn2_dbg[:], in_=xns),
                 waits=[(v0, aff_marks[tt])], incs=[(g0, 16)])
        for gi in range(8):
            bank = ps[6 + gi % 2]
            twl = [(v0, aff_marks[tt])]
            if (tt, gi) > (0, 1):
                pj = (tt, gi - 2) if gi >= 2 else (tt - 1, 6 + gi)
                twl.append((v0, tp_marks[pj]))
            for i in range(4):
                jc = gi * 4 + i
                P.op("tensor", lambda e, xns=xns, jc=jc, i=i, bank=bank: e.transpose(
                    bank[:, i * 128:i * 128 + 128],
                    xns[:, jc * 128:jc * 128 + 128], cs["identf"][:]),
                    waits=twl if i == 0 else [])
            P.op("tensor", lambda e: e.drain(), incs=[(t0, 1)])
            tmark = t0.n
            P.op("vector", lambda e, gi=gi, tt=tt, bank=bank: e.tensor_copy(
                out=xnT[:, gi * 4:gi * 4 + 4, tt * 128:tt * 128 + 128],
                in_=bank[:, 0:512].rearrange("p (a b) -> p a b", a=4)),
                waits=[(t0, tmark)], incs=[(v0, 1)])
            tp_marks[(tt, gi)] = v0.n
        tpe_marks[tt] = t0.n

    if debug:
        P.op("gpsimd", lambda e: e.dma_start(out=mrow_dbg[:], in_=mrowall_sb[:]),
             waits=[(v0, v0.n)], incs=[(g0, 16)])
        xnT_dbg = nc.dram_tensor("xnT_dbg", [128, 32768], BF16, kind="ExternalOutput")
        tb_dbg = nc.dram_tensor("tbuf_dbg", [128, 4608], BF16, kind="ExternalOutput")
        P.op("gpsimd", lambda e: e.dma_start(
            out=xnT_dbg[:], in_=xnT.rearrange("p j t -> p (j t)")),
            waits=[(v0, v0.n)], incs=[(g0, 16)])
        P.op("gpsimd", lambda e: e.dma_start(out=tb_dbg[:], in_=tbuf),
            waits=[(v0, v0.n)], incs=[(g0, 16)])
    for half in range(2):
        for kc in range(32):
            P.op("tensor", lambda e, kc=kc, half=half: e.matmul(
                ps[4][0:32, :], cs["esl"][:, 31 - kc:63 - kc],
                xnT[:, kc, half * 512:half * 512 + 512],
                start=(kc == 0), stop=(kc == 31)),
                waits=[(v0, v0.n)] if kc == 0 else [])
        P.op("tensor", lambda e: e.drain(), incs=[(t0, 1)])
        P.op("vector", lambda e, half=half: e.tensor_copy(
            out=xt32_sb[0:32, half * 512:half * 512 + 512], in_=ps[4][0:32, :]),
            waits=[(t0, t0.n)], incs=[(v0, 1)])
    XT_DONE = v0.n

    barrier(prewaits=[("gpsimd", g0, g0.n), ("sync", ld0, ld0.n),
                      ("vector", v0, v0.n), ("scalar", a0, a0.n),
                      ("tensor", t0, t0.n)])

    qw4 = qwgu_h.ap().rearrange("(kc p) w -> p kc w", p=128)
    spm = sgupm_b_hbm.ap().rearrange("(pr kc l) -> pr kc l", kc=32, l=512)
    pkr = pk_v.rearrange("p (s g k w) -> p s g k w", s=4, g=2, k=4)
    wqr = wq_v.rearrange("p (k g w j) -> p k g w j", k=4, g=2, w=32)
    wbr = wbf_v.rearrange("p (s n) -> p s n", s=WRING)

    pk_marks, rep_marks, dq_marks = {}, {}, {}
    mm1_marks, ea_marks, ev_marks, ho_marks = {}, {}, {}, {}

    def p1_dmas(p, hq):
        G = p * 8 + hq
        s4 = G % 4
        wl = [(dq1, G - 3)] if G >= 4 else []
        for gi, woff in ((0, p * 32), (1, 1376 + p * 32)):
            dma("sync", pkr[:, s4, gi],
                qw4[:, 4 * hq:4 * hq + 4, woff:woff + 32], pk1, waits=wl)
        pk_marks[G] = pk1.n
        wl = [(dq1, G - 1)] if G >= 2 else []
        s2 = G % 2
        srr = sr_v.rearrange("p (s a l) -> p s a l", s=2, a=4)
        dma("sync", srr[:, s2],
            spm[p, 4 * hq:4 * hq + 4, :].partition_broadcast(128), rep1, waits=wl)
        rep_marks[G] = rep1.n

    def p1_deq(p, hq):
        G = p * 8 + hq
        s4, s2, sw = G % 4, G % 2, G % WRING

        def unpack(e):
            last = None
            for gi in range(2):
                for j in range(8):
                    last = e.tensor_scalar(
                        out=wqr[:, :, gi, :, j], in0=pkr[:, s4, gi],
                        scalar1=28 - 4 * j, scalar2=28,
                        op0=OP.logical_shift_left, op1=OP.logical_shift_right)
            return last

        wl = [(pk1, pk_marks[G])]
        if G >= WRING:
            wl.append((wf1, G - WRING + 1))
        P.op("vector", unpack, waits=wl, incs=[(vch, 1)])
        P.op("vector", lambda e: e.tensor_copy(out=wbr[:, sw], in_=wq_v),
             waits=[(vch, vch.n)], incs=[(vch, 1)])
        P.op("vector", lambda e: e.tensor_tensor(
            out=wbr[:, sw], in0=wbr[:, sw],
            in1=sr_v[:, s2 * 2048:s2 * 2048 + 2048], op=OP.mult),
            waits=[(rep1, rep_marks[G]), (vch, vch.n)], incs=[(dq1, 1)])
        dq_marks[G] = dq1.n

    def p1_zsn(p):
        wl = [(mm1, 2 * (p - 1))] if p >= 2 else []
        zv = zsneg_hbm.ap().rearrange("(g n) -> g n", n=22016)
        dma("sync", zsn_sb[0:32, (p % 2) * 512:(p % 2) * 512 + 256],
            zv[:, p * 256:p * 256 + 256], rep1, waits=wl)
        dma("sync", zsn_sb[0:32, (p % 2) * 512 + 256:(p % 2) * 512 + 512],
            zv[:, 11008 + p * 256:11008 + p * 256 + 256], rep1)
        zsn_marks[p] = rep1.n

    def p1_pe(p, half):
        u = 2 * p + half
        st = u % 2
        for c in range(4):
            cwl = [(rep1, zsn_marks[p]), (v0, XT_DONE)] if c == 0 else []
            if c == 0 and u >= 2:
                cwl += [(ea1, ea_marks[u - 2]), (ev1, ev_marks[u - 2])]
            P.op("tensor", lambda e, c=c, st=st, half=half, p=p: e.matmul(
                ps[4 * st + c][:],
                zsn_sb[0:32, (p % 2) * 512 + c * 128:(p % 2) * 512 + c * 128 + 128],
                xt32_sb[0:32, half * 512:half * 512 + 512],
                start=True, stop=False), waits=cwl)
        for kc in range(32):
            G = p * 8 + kc // 4
            sw = G % WRING
            wl = []
            if kc % 4 == 0:
                wl = [(dq1, dq_marks[G])]
            for c in range(4):
                P.op("tensor", lambda e, sw=sw, kc=kc, c=c, half=half, st=st: e.matmul(
                    ps[4 * st + c][:],
                    wbr[:, sw, (kc % 4) * 512 + c * 128:(kc % 4) * 512 + c * 128 + 128],
                    xnT[:, kc, half * 512:half * 512 + 512],
                    start=False, stop=(kc == 31)),
                    waits=wl if c == 0 else [])
        P.op("tensor", lambda e: e.drain(), incs=[(mm1, 1)])
        if half == 1:
            P.op("tensor", lambda e: e.nop(), incs=[(wf1, 8)])
        mm1_marks[u] = mm1.n

    def p1_act(p, half):
        u = 2 * p + half
        st = u % 2
        for c01 in range(2):
            wl = []
            if c01 == 0:
                wl.append((mm1, mm1_marks[u]))
                if u >= 1:
                    wl.append((ev1, ev_marks[u - 1]))
            P.op("scalar", lambda e, st=st, c01=c01: e.activation(
                out=sil_v[:, c01 * 512:c01 * 512 + 512], in_=ps[4 * st + c01][:],
                func=AF.Silu),
                waits=wl)
        P.op("scalar", lambda e: e.drain(), incs=[(ea1, 1)])
        ea_marks[u] = ea1.n

    def p1_evac(p, half):
        u = 2 * p + half
        st = u % 2
        for c01 in range(2):
            hs = 2 * (u % 2) + c01
            wl = [(ea1, ea_marks[u])]
            if u >= 2:
                wl.append((ho1, ho_marks.get((u - 2, c01), 0)))
            P.op("vector", lambda e, hs=hs, st=st, c01=c01: e.tensor_tensor(
                out=hbf_v[:, hs * 512:hs * 512 + 512],
                in0=sil_v[:, c01 * 512:c01 * 512 + 512],
                in1=ps[4 * st + 2 + c01][:], op=OP.mult),
                waits=wl, incs=[(ev1, 1)])
        ev_marks[u] = ev1.n

    def p1_hout(p, half):
        u = 2 * p + half
        for c01 in range(2):
            hs = 2 * (u % 2) + c01
            chunk = 2 * p + c01
            P.op("gpsimd", lambda e, hs=hs, chunk=chunk, half=half: e.dma_start(
                out=h_hbm[chunk * 128:chunk * 128 + 128,
                          half * 512:half * 512 + 512],
                in_=hbf_v[:, hs * 512:hs * 512 + 512]),
                waits=[(ev1, ev_marks[u] - 1 + c01)], incs=[(ho1, 16)])
            ho_marks[(u, c01)] = ho1.n

    zsn_marks = {}
    for p in range(NPAIR + 1):
        if p < NPAIR:
            p1_zsn(p)
            for hq in range(4):
                p1_dmas(p, hq)
                p1_deq(p, hq)
        if p >= 1:
            p1_pe(p - 1, 0)
            p1_act(p - 1, 0)
            p1_evac(p - 1, 0)
            p1_hout(p - 1, 0)
            p1_pe(p - 1, 1)
        if p < NPAIR:
            for hq in range(4, 8):
                p1_dmas(p, hq)
                p1_deq(p, hq)
        if p >= 1:
            p1_act(p - 1, 1)
            p1_evac(p - 1, 1)
            p1_hout(p - 1, 1)

    barrier(prewaits=[("gpsimd", ho1, ho1.n), ("vector", ev1, ev1.n),
                      ("scalar", ea1, ea1.n), ("tensor", ev1, ev1.n),
                      ("sync", rep1, rep1.n)])


    hre = h_hbm.ap().rearrange("(c p) t -> p c t", p=128)
    qd4 = qwdn_h.ap().rearrange("(kc p) w -> p kc w", p=128)
    wqdr = wqd_v.rearrange("p (C j c) -> p C j c", j=8, c=16)
    pkd_marks, repd_marks, wdq_marks = {}, {}, {}
    def p25_chunk(cch):
        s2 = cch % 2
        wl = [(wdq, cch - 1)] if cch >= 2 else []
        dma("sync", pkd_v[:, s2 * 512:s2 * 512 + 512], qd4[:, cch, :], pkd,
            waits=wl)
        pkd_marks[cch] = pkd.n
        dma("sync", zrd_v[:, s2 * 4096:s2 * 4096 + 4096],
            zdn2[cch:cch + 1, :].partition_broadcast(128).squeeze(1),
            repd, waits=wl)
        dma("sync", srd_v[:, s2 * 4096:s2 * 4096 + 4096],
            sdnb2[cch:cch + 1, :].partition_broadcast(128).squeeze(1), repd)
        repd_marks[cch] = repd.n

        def unpackd(e, s2=s2):
            last = None
            pin = pkd_v[:, s2 * 512:s2 * 512 + 512].rearrange(
                "p (C c) -> p C c", c=16)
            for j in range(8):
                last = e.tensor_scalar(
                    out=wqdr[:, :, j, :], in0=pin,
                    scalar1=28 - 4 * j, scalar2=28,
                    op0=OP.logical_shift_left, op1=OP.logical_shift_right)
            return last

        wl = [(pkd, pkd_marks[cch])]
        if cch >= 2:
            wl.append((wout, 16 * (cch - 1)))
        P.op("vector", unpackd, waits=wl, incs=[(vch, 1)])
        P.op("vector", lambda e, s2=s2: e.tensor_tensor(
            out=wdb_v[:, s2 * 4096:s2 * 4096 + 4096], in0=wqd_v,
            in1=zrd_v[:, s2 * 4096:s2 * 4096 + 4096], op=OP.subtract),
            waits=[(repd, repd_marks[cch]), (vch, vch.n)], incs=[(vch, 1)])
        P.op("vector", lambda e, s2=s2: e.tensor_tensor(
            out=wdb_v[:, s2 * 4096:s2 * 4096 + 4096],
            in0=wdb_v[:, s2 * 4096:s2 * 4096 + 4096],
            in1=srd_v[:, s2 * 4096:s2 * 4096 + 4096], op=OP.mult),
            waits=[(vch, vch.n)], incs=[(wdq, 1)])
        wdq_marks[cch] = wdq.n
        P.op("gpsimd", lambda e, cch=cch, s2=s2: e.dma_start(
            out=wdn_hbm[cch * 128:cch * 128 + 128, :],
            in_=wdb_v[:, s2 * 4096:s2 * 4096 + 4096]),
            waits=[(wdq, wdq_marks[cch])], incs=[(wout, 16)])

    cev_marks, cmm_marks = {}, {}
    p25_done = 0
    for k in range(64):
        tt, ip = k // 8, k % 8
        bank = ps[6 + k % 2]
        njc = min(32, 4 * ip + 4)
        wl = [(cev, cev_marks.get(k - 2, 0))]
        for jc in range(njc):
            P.op("tensor", lambda e, jc=jc, tt=tt, ip=ip, bank=bank: e.matmul(
                bank[:], xnT[:, jc, tt * 128:tt * 128 + 128],
                tbuf[:, 512 + 128 * (4 * ip - jc):512 + 128 * (4 * ip - jc) + 512],
                start=(jc == 0), stop=(jc == njc - 1)),
                waits=wl if jc == 0 else [])
        P.op("tensor", lambda e: e.drain(), incs=[(cmm, 1)])
        cmm_marks[k] = cmm.n
        P.op("scalar", lambda e, k=k, bank=bank: e.copy(
            out=stile_v[:, (k % 2) * 512:(k % 2) * 512 + 512], in_=bank[:]),
            waits=[(cmm, cmm_marks[k]), (cso, 16 * (k - 1))])
        P.op("scalar", lambda e: e.drain(), incs=[(cev, 1)])
        cev_marks[k] = cev.n
        P.op("gpsimd", lambda e, k=k, tt=tt, ip=ip: e.dma_start(
            out=s_hbm[tt * 128:tt * 128 + 128, ip * 512:ip * 512 + 512],
            in_=stile_v[:, (k % 2) * 512:(k % 2) * 512 + 512]),
            waits=[(cev, cev_marks[k])], incs=[(cso, 16)])
        want = (k + 1) * 86 // 64
        while p25_done < want:
            p25_chunk(p25_done)
            p25_done += 1


    hvr0 = h_v.rearrange("p (c t) -> p c t", c=86)
    for hc in range(86):
        dma("sync", hvr0[:, hc], hre[:, hc, 0:512], hld,
            waits=[(cmm, 64)] if hc == 0 else [])

    barrier(prewaits=[("gpsimd", wout, wout.n), ("vector", wdq, wdq.n),
                      ("gpsimd", cso, cso.n), ("vector", cev, cev.n),
                      ("tensor", cev, cev.n)])

    g3_marks, e3_marks, sx_marks, fin_marks = {}, {}, {}, {}
    mm3_marks = {}

    def p3_sx(u):
        half, panel = u // 8, u % 8
        for t in range(4):
            ttg = half * 4 + t
            slot = (u * 4 + t) % 8
            wl = []
            if u >= 2:
                wl = [(e3, e3_marks.get((u - 2, t), 0))]
            dma("scalar", sx_v[:, slot * 512:slot * 512 + 512],
                s_hbm[ttg * 128:ttg * 128 + 128, panel * 512:panel * 512 + 512],
                sx3, waits=wl)
            dma("scalar", xx_v[:, slot * 512:slot * 512 + 512],
                x_h[ttg * 128:ttg * 128 + 128, panel * 512:panel * 512 + 512],
                sx3)
            sx_marks[(u, t)] = sx3.n

    for u in range(16):
        half, panel = u // 8, u % 8
        if panel == 0 and half == 1:
            wl = [(mmd, 8 * half)]
            hvr = h_v.rearrange("p (c t) -> p c t", c=86)
            for hc in range(86):
                dma("sync", hvr[:, hc], hre[:, hc, half * 512:half * 512 + 512],
                    hld, waits=wl if hc == 0 else [])
        if u == 0:
            p3_sx(0)
        st = u % 2
        for cch in range(86):
            G = u * 86 + cch
            if G == 0:
                for Gp in range(4):
                    dma("sync", wt_v[:, (Gp % 24) * 512:(Gp % 24) * 512 + 512],
                        wdn_hbm[Gp * 128:Gp * 128 + 128, 0:512], wt3)
            Gn = G + 4
            if Gn < 16 * 86:
                un, cn = Gn // 86, Gn % 86
                gw = [(mm3, Gn - 19)] if Gn >= 20 else []
                dma("sync", wt_v[:, (Gn % 24) * 512:(Gn % 24) * 512 + 512],
                    wdn_hbm[cn * 128:cn * 128 + 128,
                            (un % 8) * 512:(un % 8) * 512 + 512], wt3, waits=gw)
            wl = [(wt3, 16 * min(G + 4, 16 * 86))]
            if cch == 0:
                wl.append((hld, 16 * 86 * (half + 1)))
                if u >= 1:
                    wl.append((g3s, g3s.n))
                    wl.append((e3, e3.n))
                    wl.append((fin, fin.n))
            sw = G % 24
            for t in range(4):
                P.op("tensor", lambda e, sw=sw, cch=cch, t=t, st=st, half=half: e.matmul(
                    ps[4 * st + t][:],
                    h_v[:, cch * 512 + t * 128:cch * 512 + t * 128 + 128],
                    wt_v[:, sw * 512:sw * 512 + 512],
                    start=(cch == 0), stop=(cch == 85)),
                    waits=wl if t == 0 else [],
                    incs=[(mm3, 1)] if t == 3 else [])
        P.op("tensor", lambda e: e.drain(), incs=[(mmd, 1)])
        mm3_marks[u] = mmd.n
        if u < 15:
            p3_sx(u + 1)
        for t in range(4):
            gsl = t % 2
            wl = [(mmd, mm3_marks[u])] if t == 0 else []
            if t >= 2:
                wl.append((e3, e3_marks[(u, t - 2)]))
            elif u >= 1:
                wl.append((e3, e3_marks[(u - 1, t + 2)]))
            P.op("scalar", lambda e, gsl=gsl, st=st, t=t: e.activation(
                out=g3_v[:, gsl * 512:gsl * 512 + 512], in_=ps[4 * st + t][:],
                func=AF.Silu), waits=wl)
            P.op("scalar", lambda e: e.drain(), incs=[(g3s, 1)])
            g3_marks[u] = g3s.n
            slot = (u * 4 + t) % 8
            sxtgt = min(sx_marks[(u, t)] + 32, sx3.n)
            wl = [(g3s, g3s.n), (sx3, sxtgt)]
            if u >= 1:
                wl.append((fin, fin_marks.get((u - 1, t), 0)))
            P.op("vector", lambda e, gsl=gsl, slot=slot: e.tensor_tensor(
                out=t3_v[:, gsl * 512:gsl * 512 + 512].rearrange(
                    "p (C c j) -> p C c j", c=16, j=8),
                in0=g3_v[:, gsl * 512:gsl * 512 + 512].rearrange(
                    "p (C j c) -> p C j c", j=8, c=16).transpose([0, 1, 3, 2]),
                in1=sx_v[:, slot * 512:slot * 512 + 512].rearrange(
                    "p (C c j) -> p C c j", c=16, j=8), op=OP.mult),
                waits=wl, incs=[(vch, 1)])
            P.op("vector", lambda e, gsl=gsl, slot=slot, t=t: e.tensor_tensor(
                out=ob_v[:, t * 512:t * 512 + 512],
                in0=t3_v[:, gsl * 512:gsl * 512 + 512],
                in1=xx_v[:, slot * 512:slot * 512 + 512], op=OP.add),
                waits=[(vch, vch.n)], incs=[(e3, 1)])
            e3_marks[(u, t)] = e3.n
            ttg = half * 4 + t
            P.op("gpsimd", lambda e, t=t, ttg=ttg, panel=panel: e.dma_start(
                out=out_h[ttg * 128:ttg * 128 + 128, panel * 512:panel * 512 + 512],
                in_=ob_v[:, t * 512:t * 512 + 512]),
                waits=[(e3, e3_marks[(u, t)])], incs=[(fin, 16)])
            fin_marks[(u, t)] = fin.n

    P.op("gpsimd", lambda e: e.nop(), waits=[(fin, fin.n)])

    with nc.Block() as block:
        for ename in ENGINES:
            def emit(eng, ename=ename):
                for fn, wl, il in P.ops[ename]:
                    for h, vv in wl:
                        eng.wait_ge(h, vv)
                    if fn is not None:
                        inst = fn(eng)
                        for h, k in il:
                            inst.then_inc(h, k)
            getattr(block, ename)(emit)

    return nc


_PROG = None


def kernel(**inputs):
    global _PROG
    if _PROG is None:
        _PROG = build_program()
    nc = _PROG

    f = lambda a: np.ascontiguousarray(a, dtype=np.float32)
    x = f(inputs["L0_attn_u"]).reshape(8192, 4096)
    lam = np.concatenate([f(inputs["L0_attn_Lambda_ri"][:, 0]),
                          f(inputs["L0_attn_Lambda_ri"][:, 1])]).reshape(128, 1)
    vct = np.concatenate([f(inputs["L0_attn_Vc_ri"][..., 0]).T,
                          f(inputs["L0_attn_Vc_ri"][..., 1]).T], axis=1)
    ctpq = np.concatenate([
        f(inputs["L0_attn_Ct_ri"][0, :, 0]), f(inputs["L0_attn_Ct_ri"][0, :, 1]),
        f(inputs["L0_attn_p_ri"][:, 0]), f(inputs["L0_attn_p_ri"][:, 1]),
        f(inputs["L0_attn_q_ri"][:, 0]), f(inputs["L0_attn_q_ri"][:, 1])]
    ).reshape(1, 384)
    dls = np.array([[inputs["L0_attn_D"][0], inputs["L0_attn_log_step"][0]]],
                   dtype=np.float32)
    sg = f(inputs["L0_ffn_scales_gu"])
    sgupm = np.empty((NPAIR, 32, 512), np.float32)
    for p in range(NPAIR):
        sgupm[p, :, 0:256] = sg[:, p * 256:p * 256 + 256]
        sgupm[p, :, 256:512] = sg[:, 11008 + p * 256:11008 + p * 256 + 256]
    base = {
        "qwgu": np.ascontiguousarray(inputs["L0_ffn_qweight_gu"], np.int32),
        "qzgu": np.ascontiguousarray(inputs["L0_ffn_qzeros_gu"], np.int32).reshape(-1),
        "sgupm": sgupm.reshape(-1),
        "sgun": sg.reshape(-1),
        "qwdn": np.ascontiguousarray(inputs["L0_ffn_qweight_dn"], np.int32),
        "qzdn": np.ascontiguousarray(inputs["L0_ffn_qzeros_dn"], np.int32).reshape(-1),
        "sdn": f(inputs["L0_ffn_scales_dn"]).reshape(-1),
        "lam": lam, "vct": np.ascontiguousarray(vct),
        "bp": f(inputs["L0_attn_Bp"]), "ctpq": ctpq, "dls": dls,
    }
    for k, v in _CONSTS.items():
        base["c_" + k] = v

    in_maps = []
    for c in range(8):
        m = dict(base)
        m["x"] = np.ascontiguousarray(x[c * 1024:(c + 1) * 1024])
        in_maps.append(m)

    res = run_bass_kernel_spmd(nc, in_maps, core_ids=list(range(8)))
    out = np.concatenate([np.asarray(r["out"]) for r in res.results], axis=0)
    return out.reshape(2, 4096, 4096).astype(np.float32)


# version 13
# speedup vs baseline: 1.2197x; 1.0232x over previous
import numpy as np
import ml_dtypes
import concourse.bass as bass
import concourse.mybir as mybir
from concourse.bass_utils import run_bass_kernel_spmd

AF = mybir.ActivationFunctionType
OP = mybir.AluOpType
F32 = mybir.dt.float32
BF16 = mybir.dt.bfloat16
I32 = mybir.dt.int32
U8 = mybir.dt.uint8
XAX = mybir.AxisListType.X
bf16 = ml_dtypes.bfloat16

T = 1024
NPAIR = 43
WRING = 12
INTER = 11008
ENGINES = ("sync", "vector", "scalar", "tensor", "gpsimd")


def _host_consts():
    n = np.arange(4096, dtype=np.float64)
    Om = np.exp(-2j * np.pi * n / 4096.0)
    w = (1.0 - Om) / (1.0 + Om)
    c = 2.0 / (1.0 + Om)
    f = lambda a: np.ascontiguousarray(a, dtype=np.float32)
    w0 = np.arange(64)
    E1 = np.exp(2j * np.pi * np.outer(w0, w0) / 64.0)[:, ::-1]
    TW = np.exp(2j * np.pi * np.outer(w0, w0) / 4096.0)[:, ::-1]
    E2 = np.exp(2j * np.pi * np.outer(w0, w0) / 64.0)[:, ::-1]
    return {
        "wre": f(np.broadcast_to(w.real, (64, 4096))),
        "wim": f(np.broadcast_to(w.imag, (64, 4096))),
        "cre64": f(c.real.reshape(64, 64)),
        "cim64": f(c.imag.reshape(64, 64)),
        "E1c": f(np.block([[E1.real, E1.imag], [-E1.imag, E1.real]])),
        "TWr": f(TW.real), "TWi": f(TW.imag),
        "E2c": f(np.vstack([E2.real, -E2.imag])),
        "identf": f(np.eye(128)),
        "esl": np.ascontiguousarray(
            (np.arange(63)[None, :] == 31).astype(np.float32)
            * np.ones((128, 1), np.float32)).astype(bf16),
        "identb": np.eye(128, dtype=np.float32).astype(bf16),
    }


_CONSTS = _host_consts()


class Sem:
    def __init__(self, nc, name):
        self.h = nc.alloc_semaphore(name)
        self.name = name
        self.n = 0


class Plan:
    def __init__(self):
        self.ops = {e: [] for e in ENGINES}

    def op(self, eng, fn, waits=(), incs=()):
        wl = []
        for s, v in waits:
            v = max(0, v)
            assert v <= s.n, f"wait {s.name}>={v} but only {s.n} planned"
            if v > 0:
                wl.append((s.h, v))
        il = []
        for s, k in incs:
            s.n += k
            il.append((s.h, k))
        self.ops[eng].append((fn, wl, il))


def build_program(debug=False):
    nc = bass.Bass("TRN2", target_bir_lowering=False)

    x_h = nc.dram_tensor("x", [T, 4096], F32, kind="ExternalInput")
    qwgu_h = nc.dram_tensor("qwgu", [4096, 2752], I32, kind="ExternalInput")
    qzgu_h = nc.dram_tensor("qzgu", [32 * 2752], I32, kind="ExternalInput")
    sgupm_h = nc.dram_tensor("sgupm", [NPAIR * 32 * 512], F32, kind="ExternalInput")
    sgun_h = nc.dram_tensor("sgun", [32 * 22016], F32, kind="ExternalInput")
    qwdn_h = nc.dram_tensor("qwdn", [INTER, 512], I32, kind="ExternalInput")
    qzdn_h = nc.dram_tensor("qzdn", [86 * 512], I32, kind="ExternalInput")
    sdn_h = nc.dram_tensor("sdn", [86 * 4096], F32, kind="ExternalInput")
    lam_h = nc.dram_tensor("lam", [128, 1], F32, kind="ExternalInput")
    vct_h = nc.dram_tensor("vct", [64, 128], F32, kind="ExternalInput")
    bp_h = nc.dram_tensor("bp", [64, 1], F32, kind="ExternalInput")
    ctpq_h = nc.dram_tensor("ctpq", [1, 384], F32, kind="ExternalInput")
    dls_h = nc.dram_tensor("dls", [1, 2], F32, kind="ExternalInput")
    c_h = {k: nc.dram_tensor("c_" + k, list(v.shape),
                             BF16 if v.dtype == bf16 else F32, kind="ExternalInput")
           for k, v in _CONSTS.items()}
    out_h = nc.dram_tensor("out", [T, 4096], F32, kind="ExternalOutput")

    _k = dict(kind="ExternalOutput") if debug else {}
    if debug:
        mrow_dbg = nc.dram_tensor("mrow_dbg", [128, 64], F32, kind="ExternalOutput")
        xt2_dbg = nc.dram_tensor("xt2_dbg", [128, 4096], F32, kind="ExternalOutput")
        xn2_dbg = nc.dram_tensor("xn2_dbg", [128, 4096], F32, kind="ExternalOutput")
    h_hbm = nc.dram_tensor("h_scr", [INTER, T], BF16, **_k)
    s_hbm = nc.dram_tensor("s_scr", [T, 4096], F32, **_k)
    wdn_hbm = nc.dram_tensor("wdn_scr", [INTER, 4096], BF16, **_k)
    krev_hbm = nc.dram_tensor("krev_scr", [4352], F32, **_k)
    kk_hbm = nc.dram_tensor("kk_scr", [8, 4096], F32, **_k)
    zsneg_hbm = nc.dram_tensor("zsneg_scr", [32 * 22016], BF16)
    sgupm_b_hbm = nc.dram_tensor("sgupm_b_scr", [NPAIR * 32 * 512], BF16)
    zdn_hbm = nc.dram_tensor("zdn_scr", [86 * 4096], BF16)
    sdnb_hbm = nc.dram_tensor("sdnb_scr", [86 * 4096], BF16)
    zdn2 = zdn_hbm.ap().rearrange("(g d) -> g d", d=4096)
    sdnb2 = sdnb_hbm.ap().rearrange("(g d) -> g d", d=4096)
    bnc_hbm = nc.dram_tensor("bounce_scr", [2048], F32, **_k)

    cs = {k: nc.alloc_sbuf_tensor("k_" + k, list(v.shape),
                                  BF16 if v.dtype == bf16 else F32)
          for k, v in _CONSTS.items() if k not in ("wre", "wim")}
    lam_sb = nc.alloc_sbuf_tensor("lam_sb", [128, 1], F32)
    vct_sb = nc.alloc_sbuf_tensor("vct_sb", [64, 128], F32)
    bp_sb = nc.alloc_sbuf_tensor("bp_sb", [64, 1], F32)
    ctpq_sb = nc.alloc_sbuf_tensor("ctpq_sb", [1, 384], F32)
    dls_sb = nc.alloc_sbuf_tensor("dls_sb", [1, 2], F32)
    step_sb = nc.alloc_sbuf_tensor("step_sb", [1, 4], F32)
    ts64_sb = nc.alloc_sbuf_tensor("ts64_sb", [64, 1], F32)
    d128_sb = nc.alloc_sbuf_tensor("d128_sb", [128, 1], F32)
    vmat_sb = nc.alloc_sbuf_tensor("vmat_sb", [128, 8], F32)
    mrow_sb = nc.alloc_sbuf_tensor("mrow_sb", [128, 8], F32)
    mrowall_sb = nc.alloc_sbuf_tensor("mrowall_sb", [128, 64], F32)
    hank_sb = nc.alloc_sbuf_tensor("hank_sb", [128, 256], F32)
    xt32_sb = nc.alloc_sbuf_tensor("xt32_sb", [32, 1024], BF16)
    zsn_sb = nc.alloc_sbuf_tensor("zsn_sb", [32, 2 * 512], BF16)

    ARENA = 161792
    arena = nc.alloc_sbuf_tensor("arena", [128, ARENA], U8)

    def view(off, parts, n, dt):
        esz = {F32: 4, I32: 4, BF16: 2, U8: 1}[dt]
        assert off % 4 == 0 and off + n * esz <= ARENA, (off, n)
        ap = arena[0:parts, off:off + n * esz]
        if dt != U8:
            ap = ap.bitcast(dt)
        return ap

    xnT = view(0, 128, 32768, BF16).rearrange("p (j t) -> p j t", t=1024)
    tbuf = view(65536, 128, 4608, BF16)
    C0 = 74752
    zq_gu = view(90112, 128, 688, I32)
    zi_gu = view(90112 + 2752, 128, 5504, I32)
    zb_gu = view(90112 + 24768, 128, 5504, BF16)
    sf2_gu = view(90112 + 35776, 128, 5504, F32)
    zb2_gu = view(90112 + 57792, 128, 5504, BF16)
    sf_gu = view(90112, 128, 5504, F32)
    sb_gu = view(90112 + 22016, 128, 5504, BF16)
    zq_dn = view(90112, 86, 512, I32)
    zi_dn = view(90112 + 2048, 86, 4096, I32)
    zb_dn = view(90112 + 18432, 86, 4096, BF16)
    sf_dn = view(90112 + 26624, 86, 4096, F32)
    sb_dn = view(90112 + 43008, 86, 4096, BF16)
    wre_v = view(90112, 64, 1024, F32)
    wim_v = view(90112 + 4096, 64, 1024, F32)
    dre_v = view(90112 + 8192, 64, 1024, F32)
    dim_v = view(90112 + 12288, 64, 1024, F32)
    rin_v = view(90112 + 16384, 64, 1024, F32)
    tmp_v = view(90112 + 20480, 64, 1024, F32)
    rcat_v = view(90112 + 24576, 128, 1024, F32)
    kk8_v = view(90112 + 28672, 8, 4096, F32)
    vstage_v = view(90112 + 45056, 1, 1024, F32)
    vrow_v = view(90112 + 49152, 1, 1024, F32)
    brow_v = view(90112 + 53248, 1, 128, F32)
    bst_v = view(90112 + 53760, 128, 1, F32)
    ast_v = view(90112 + 54272, 128, 64, F32)
    s1_v = view(90112 + 54528, 64, 128, F32)
    s2_v = view(90112 + 55040, 128, 64, F32)
    krev_v = view(90112 + 55296, 64, 64, F32)
    krevt_v = view(90112 + 55552, 64, 64, F32)
    KCN = ["k00r", "k00i", "k01r", "k01i", "k10r", "k10i", "k11r", "k11i",
           "ta", "tb", "tc", "td", "te", "tf", "atr", "ati"]
    kc_v = {nm: view(90112 + 55808 + i * 256, 64, 64, F32) for i, nm in enumerate(KCN)}
    xt_v = view(C0, 128, 2 * 4096, F32)
    xn_v = view(C0 + 32768, 128, 2 * 4096, F32)
    wbf_v = view(C0, 128, WRING * 2048, BF16)
    pk_v = view(C0 + 49152, 128, 4 * 256, I32)
    wq_v = view(C0 + 53248, 128, 2048, I32)
    sr_v = view(C0 + 69632, 128, 2 * 2048, BF16)
    hbf_v = view(C0 + 77824, 128, 4 * 512, BF16)
    sil_v = view(C0 + 81920, 128, 2 * 512, F32)
    stile_v = view(157696, 128, 2 * 512, F32)
    pkd_v = view(88064, 128, 2 * 512, I32)
    wqd_v = view(92160, 128, 4096, I32)
    zrd_v = view(108544, 128, 2 * 4096, BF16)
    srd_v = view(124928, 128, 2 * 4096, BF16)
    wdb_v = view(141312, 128, 2 * 4096, BF16)
    h_v = view(0, 128, 86 * 512, BF16)
    wt_v = view(88064, 128, 24 * 512, BF16)
    sx_v = view(112640, 128, 8 * 512, F32)
    xx_v = view(129024, 128, 8 * 512, F32)
    ob_v = view(145408, 128, 4 * 512, F32)
    g3_v = view(153600, 128, 2 * 512, F32)
    t3_v = view(157696, 128, 2 * 512, F32)

    ps = [nc.alloc_psum_tensor("ps%d" % i, [128, 512], F32) for i in range(8)]

    SN = ["ld0", "v0", "a0", "t0", "g0", "ph",
          "pk1", "rep1", "dq1", "wf1", "mm1", "ea1", "ev1", "ho1",
          "cmm", "cev", "cso",
          "pkd", "repd", "wdq", "wout",
          "hld", "wt3", "sx3", "mm3", "g3", "e3", "fin", "vch", "mmd"]
    S = {name: Sem(nc, "s_" + name) for name in SN}
    (ld0, v0, a0, t0, g0, ph, pk1, rep1, dq1, wf1, mm1, ea1, ev1, ho1,
     cmm, cev, cso, pkd, repd, wdq, wout, hld, wt3, sx3, mm3, g3s, e3,
     fin, vch, mmd) = [S[n] for n in SN]
    P = Plan()

    def barrier(prewaits=()):
        pw = {e: [] for e in ENGINES}
        for e, s, vv in prewaits:
            assert vv <= s.n
            pw[e].append((s.h, vv))
        for e in ENGINES:
            ph.n += 1
            wl = pw[e]
            if e in ("tensor", "scalar", "vector", "gpsimd"):
                P.ops[e].append((lambda eng: eng.drain(), [], []))
            P.ops[e].append(
                (lambda eng: eng.sem_inc(ph.h, 1), wl, []))
        tgt = ph.n
        for e in ENGINES:
            P.ops[e].append((None, [(ph.h, tgt)], []))

    def vchain(steps, waits0=(), incs_last=()):
        for i, fn in enumerate(steps):
            w = list(waits0) + [(vch, vch.n)] if i == 0 else [(vch, vch.n)]
            inc = list(incs_last) if i == len(steps) - 1 else [(vch, 1)]
            P.op("vector", fn, waits=w, incs=inc)

    def dma(eng, dst, src, sem, waits=()):
        P.op(eng, lambda e, d=dst, s=src: e.dma_start(out=d, in_=s),
             waits=waits, incs=[(sem, 16)])

    small = [(lam_sb[:], lam_h[:]), (vct_sb[:], vct_h[:]), (bp_sb[:], bp_h[:]),
             (ctpq_sb[:], ctpq_h[:]), (dls_sb[:], dls_h[:])]
    small += [(cs[k][:], c_h[k][:]) for k in cs]
    for dst, src in small:
        dma("sync", dst, src, ld0)
    N_SMALL = ld0.n

    dma("sync", zq_gu, qzgu_h.ap().rearrange("(p w) -> p w", p=128), ld0)
    zq_mark = ld0.n
    wvj = zi_gu.rearrange("p (w j) -> p w j", j=8)
    for j in range(8):
        P.op("vector",
             lambda e, j=j: e.tensor_scalar(
                 out=wvj[:, :, j], in0=zq_gu, scalar1=28 - 4 * j, scalar2=28,
                 op0=OP.logical_shift_left, op1=OP.logical_shift_right),
             waits=[(ld0, zq_mark)])
    P.op("vector", lambda e: e.tensor_copy(out=zb_gu, in_=zi_gu), incs=[(v0, 1)])
    dma("sync", sf2_gu, bass.AP(sgun_h, 0, [[5504, 128], [1, 5504]]), ld0)
    sf2_mark = ld0.n
    P.op("vector", lambda e: e.scalar_tensor_tensor(
        out=zb2_gu, in0=zb_gu, scalar=-1.0, in1=sf2_gu, op0=OP.mult, op1=OP.mult),
        waits=[(ld0, sf2_mark), (vch, vch.n)], incs=[(v0, 1)])
    P.op("gpsimd", lambda e: e.dma_start(
        out=zsneg_hbm.ap().rearrange("(p w) -> p w", p=128), in_=zb2_gu),
        waits=[(v0, v0.n)], incs=[(g0, 16)])
    ZPM_G0 = g0.n

    dma("sync", sf_gu, bass.AP(sgupm_h, 0, [[5504, 128], [1, 5504]]), ld0,
        waits=[(g0, ZPM_G0)])
    P.op("vector", lambda e: e.tensor_copy(out=sb_gu, in_=sf_gu),
         waits=[(ld0, ld0.n)], incs=[(v0, 1)])
    P.op("gpsimd", lambda e: e.dma_start(
        out=bass.AP(sgupm_b_hbm, 0, [[5504, 128], [1, 5504]]), in_=sb_gu),
        waits=[(v0, v0.n)], incs=[(g0, 16)])
    SPM_G0 = g0.n

    dma("sync", zq_dn, qzdn_h.ap().rearrange("(g w) -> g w", g=86), ld0,
        waits=[(g0, SPM_G0)])
    dma("sync", sf_dn, bass.AP(sdn_h, 0, [[4096, 86], [1, 4096]]), ld0)
    dn_mark = ld0.n
    wvj2 = zi_dn.rearrange("g (C j c) -> g C j c", j=8, c=16)
    zqr = zq_dn.rearrange("g (C c) -> g C c", c=16)
    for j in range(8):
        P.op("vector",
             lambda e, j=j: e.tensor_scalar(
                 out=wvj2[:, :, j, :], in0=zqr, scalar1=28 - 4 * j, scalar2=28,
                 op0=OP.logical_shift_left, op1=OP.logical_shift_right),
             waits=[(ld0, dn_mark)])
    P.op("vector", lambda e: e.tensor_copy(out=zb_dn, in_=zi_dn), incs=[(v0, 1)])
    zbdn_v0 = v0.n
    P.op("vector", lambda e: e.tensor_copy(
        out=sb_dn.rearrange("g (C j c) -> g C j c", j=8, c=16),
        in_=sf_dn.rearrange("g (C c j) -> g C c j", c=16, j=8).transpose([0, 1, 3, 2])),
        incs=[(v0, 1)])
    sbdn_v0 = v0.n
    P.op("gpsimd", lambda e: e.dma_start(
        out=zdn_hbm.ap().rearrange("(g w) -> g w", g=86), in_=zb_dn),
        waits=[(v0, zbdn_v0)], incs=[(g0, 16)])
    P.op("gpsimd", lambda e: e.dma_start(
        out=sdnb_hbm.ap().rearrange("(g w) -> g w", g=86), in_=sb_dn),
        waits=[(v0, sbdn_v0)], incs=[(g0, 16)])
    STAGE_G0 = g0.n

    P.op("scalar", lambda e: e.activation(out=step_sb[:, 0:1], in_=dls_sb[:, 1:2],
                                          func=AF.Exp),
         waits=[(ld0, N_SMALL)])
    P.op("scalar", lambda e: e.drain(), incs=[(a0, 1)])
    vchain([
        lambda e: e.reciprocal(out=step_sb[:, 1:2], in_=step_sb[:, 0:1]),
        lambda e: e.tensor_scalar(out=step_sb[:, 2:3], in0=step_sb[:, 1:2],
                                  scalar1=2.0, scalar2=None, op0=OP.mult),
    ], waits0=[(a0, a0.n)], incs_last=[(v0, 1)])
    P.op("gpsimd", lambda e: e.dma_start(out=bnc_hbm[0:1], in_=step_sb[0:1, 2:3]),
         waits=[(v0, v0.n)], incs=[(g0, 16)])
    P.op("gpsimd", lambda e: e.dma_start(out=bnc_hbm[1:2], in_=dls_sb[0:1, 0:1]),
         incs=[(g0, 16)])
    P.op("gpsimd", lambda e: e.dma_start(
        out=ts64_sb[:], in_=bass.AP(bnc_hbm, 0, [[0, 64], [1, 1]])),
        waits=[(g0, g0.n)], incs=[(g0, 16)])
    P.op("gpsimd", lambda e: e.dma_start(
        out=d128_sb[:], in_=bass.AP(bnc_hbm, 1, [[0, 128], [1, 1]])),
        incs=[(g0, 16)])
    BNC_G0 = g0.n

    P.op("tensor", lambda e: e.matmul(ps[0][:, 0:1], vct_sb[:], bp_sb[:],
                                      start=True, stop=True),
         waits=[(ld0, N_SMALL)])
    P.op("tensor", lambda e: e.drain(), incs=[(t0, 1)])
    P.op("vector", lambda e: e.tensor_copy(out=bst_v, in_=ps[0][:, 0:1]),
         waits=[(t0, t0.n), (g0, STAGE_G0)], incs=[(v0, 1)])
    P.op("gpsimd", lambda e: e.dma_start(out=bnc_hbm[128:256], in_=bst_v),
         waits=[(v0, v0.n)], incs=[(g0, 16)])
    P.op("gpsimd", lambda e: e.dma_start(out=brow_v, in_=bnc_hbm[128:256]),
         waits=[(g0, g0.n)], incs=[(g0, 16)])
    BROW_G0 = g0.n

    ct_re, ct_im = ctpq_sb[:, 0:64], ctpq_sb[:, 64:128]
    p_re, p_im = ctpq_sb[:, 128:192], ctpq_sb[:, 192:256]
    q_re, q_im = ctpq_sb[:, 256:320], ctpq_sb[:, 320:384]
    b_re, b_im = brow_v[:, 0:64], brow_v[:, 64:128]
    w1, w2 = vrow_v[:, 0:64], vrow_v[:, 64:128]
    vre, vim = vrow_v[:, 128:192], vrow_v[:, 192:256]
    vst = vstage_v.rearrange("o (n c) -> o n c", c=8)
    vsteps = []
    for kk, (ar, ai, br, bi) in enumerate(
            [(ct_re, ct_im, b_re, b_im), (ct_re, ct_im, p_re, p_im),
             (q_re, q_im, b_re, b_im), (q_re, q_im, p_re, p_im)]):
        cA, cB = 2 * kk, 2 * kk + 1
        def m1(e, ar=ar, br=br, ai=ai, bi=bi):
            e.tensor_tensor(out=w1, in0=ar, in1=br, op=OP.mult)
            return e.tensor_tensor(out=w2, in0=ai, in1=bi, op=OP.mult)
        def m2(e):
            return e.tensor_tensor(out=vre, in0=w1, in1=w2, op=OP.add)
        def m3(e, ar=ar, bi=bi, ai=ai, br=br):
            e.tensor_tensor(out=w1, in0=ar, in1=bi, op=OP.mult)
            return e.tensor_tensor(out=w2, in0=ai, in1=br, op=OP.mult)
        def m4(e):
            return e.tensor_tensor(out=vim, in0=w1, in1=w2, op=OP.subtract)
        def m5(e, cA=cA, cB=cB):
            e.tensor_copy(out=vst[:, 0:64, cA], in_=vre)
            e.tensor_copy(out=vst[:, 64:128, cA], in_=vim)
            e.tensor_copy(out=vst[:, 0:64, cB], in_=vim)
            return e.tensor_scalar(out=vst[:, 64:128, cB], in0=vre,
                                   scalar1=-1.0, scalar2=None, op0=OP.mult)
        vsteps += [m1, m2, m3, m4, m5]
    vchain(vsteps, waits0=[(g0, BROW_G0)], incs_last=[(v0, 1)])
    P.op("gpsimd", lambda e: e.dma_start(out=bnc_hbm[256:1280], in_=vstage_v),
         waits=[(v0, v0.n)], incs=[(g0, 16)])
    P.op("gpsimd", lambda e: e.dma_start(
        out=vmat_sb[:], in_=bass.AP(bnc_hbm, 256, [[8, 128], [1, 8]])),
        waits=[(g0, g0.n)], incs=[(g0, 16)])
    VMAT_G0 = g0.n

    deno_marks = {}
    for q in range(4):
        sl = slice(q * 1024, q * 1024 + 1024)
        qwl = [(v0, sbdn_v0)] if q == 0 else [(v0, deno_marks[q - 1])]
        dma("sync", wre_v, c_h["wre"][0:64, sl], ld0, waits=qwl)
        dma("sync", wim_v, c_h["wim"][0:64, sl], ld0)
        qld = ld0.n

        def dn1(e):
            e.tensor_scalar(out=dre_v, in0=wre_v, scalar1=ts64_sb[:],
                            scalar2=lam_sb[0:64, :], op0=OP.mult, op1=OP.subtract)
            return e.tensor_scalar(out=dim_v, in0=wim_v, scalar1=ts64_sb[:],
                                   scalar2=lam_sb[64:128, :], op0=OP.mult,
                                   op1=OP.subtract)
        def dn2(e):
            e.tensor_tensor(out=rin_v, in0=dre_v, in1=dre_v, op=OP.mult)
            return e.tensor_tensor(out=tmp_v, in0=dim_v, in1=dim_v, op=OP.mult)
        def dn3(e):
            return e.tensor_tensor(out=rin_v, in0=rin_v, in1=tmp_v, op=OP.add)
        def dn4(e):
            return e.reciprocal(out=rin_v, in_=rin_v)
        def dn5(e):
            e.tensor_tensor(out=rcat_v[0:64, :], in0=dre_v, in1=rin_v, op=OP.mult)
            return e.tensor_tensor(out=rcat_v[64:128, :], in0=dim_v, in1=rin_v,
                                   op=OP.mult)
        vchain([dn1, dn2, dn3, dn4, dn5],
               waits0=[(ld0, qld), (g0, BNC_G0), (t0, t0.n)], incs_last=[(v0, 1)])
        deno_marks[q] = v0.n
        for half in range(2):
            P.op("tensor", lambda e, hh=half: e.matmul(
                ps[1][0:8, :], vmat_sb[:], rcat_v[:, hh * 512:hh * 512 + 512],
                start=True, stop=True),
                waits=[(v0, v0.n), (cev, cev.n), (g0, VMAT_G0)])
            P.op("tensor", lambda e: e.drain(), incs=[(t0, 1)])
            P.op("vector", lambda e, q=q, hh=half: e.tensor_copy(
                out=kk8_v[:, (q * 2 + hh) * 512:(q * 2 + hh) * 512 + 512],
                in_=ps[1][0:8, :]),
                waits=[(t0, t0.n)], incs=[(cev, 1)])

    P.op("gpsimd", lambda e: e.dma_start(out=kk_hbm[:], in_=kk8_v),
         waits=[(cev, cev.n)], incs=[(g0, 16)])
    for r, nm in enumerate(KCN[:8]):
        P.op("gpsimd", lambda e, r=r, nm=nm: e.dma_start(
            out=kc_v[nm], in_=kk_hbm[r].rearrange("(a b) -> a b", a=64)),
            waits=[(g0, g0.n)], incs=[(g0, 16)])
    KC_G0 = g0.n

    K_ = kc_v
    def _tt(o, a, b, op):
        return lambda e: e.tensor_tensor(out=K_[o], in0=K_[a], in1=K_[b], op=op)
    at_steps = [
        lambda e: e.tensor_scalar(out=K_["ta"], in0=K_["k11r"], scalar1=1.0,
                                  scalar2=None, op0=OP.add),
        _tt("tb", "ta", "ta", OP.mult),
        _tt("tc", "k11i", "k11i", OP.mult),
        _tt("tb", "tb", "tc", OP.add),
        lambda e: e.reciprocal(out=K_["tb"], in_=K_["tb"]),
        _tt("tc", "ta", "tb", OP.mult),
        _tt("td", "k11i", "tb", OP.mult),
        _tt("ta", "k01r", "tc", OP.mult),
        _tt("tb", "k01i", "td", OP.mult),
        _tt("te", "ta", "tb", OP.add),
        _tt("ta", "k01i", "tc", OP.mult),
        _tt("tb", "k01r", "td", OP.mult),
        _tt("tf", "ta", "tb", OP.subtract),
        _tt("ta", "te", "k10r", OP.mult),
        _tt("tb", "tf", "k10i", OP.mult),
        _tt("ta", "ta", "tb", OP.subtract),
        _tt("tc", "te", "k10i", OP.mult),
        _tt("td", "tf", "k10r", OP.mult),
        _tt("tc", "tc", "td", OP.add),
        _tt("ta", "k00r", "ta", OP.subtract),
        _tt("tc", "k00i", "tc", OP.subtract),
        lambda e: e.tensor_tensor(out=K_["te"], in0=cs["cre64"][:], in1=K_["ta"], op=OP.mult),
        lambda e: e.tensor_tensor(out=K_["tf"], in0=cs["cim64"][:], in1=K_["tc"], op=OP.mult),
        _tt("atr", "te", "tf", OP.subtract),
        lambda e: e.tensor_tensor(out=K_["te"], in0=cs["cre64"][:], in1=K_["tc"], op=OP.mult),
        lambda e: e.tensor_tensor(out=K_["tf"], in0=cs["cim64"][:], in1=K_["ta"], op=OP.mult),
        _tt("ati", "te", "tf", OP.add),
        lambda e: e.tensor_copy(out=ast_v[0:64, :], in_=K_["atr"]),
        lambda e: e.tensor_copy(out=ast_v[64:128, :], in_=K_["ati"]),
    ]
    vchain(at_steps, waits0=[(g0, KC_G0)], incs_last=[(v0, 1)])
    if debug:
        ast_dbg = nc.dram_tensor("ast_dbg", [128, 64], F32, kind="ExternalOutput")
        rc_dbg = nc.dram_tensor("rc_dbg", [128, 1024], F32, kind="ExternalOutput")
        P.op("gpsimd", lambda e: e.dma_start(out=ast_dbg[:], in_=ast_v),
             waits=[(v0, v0.n)], incs=[(g0, 16)])
        P.op("gpsimd", lambda e: e.dma_start(out=rc_dbg[:], in_=rcat_v),
             waits=[(v0, v0.n)], incs=[(g0, 16)])
    P.op("tensor", lambda e: e.matmul(ps[1][0:64, 0:128], ast_v, cs["E1c"][:],
                                      start=True, stop=True),
         waits=[(v0, v0.n)])
    P.op("tensor", lambda e: e.drain(), incs=[(t0, 1)])
    P.op("vector", lambda e: e.tensor_copy(out=s1_v, in_=ps[1][0:64, 0:128]),
         waits=[(t0, t0.n)], incs=[(v0, 1)])

    s1r, s1i = s1_v[:, 0:64], s1_v[:, 64:128]
    twa, twb = kc_v["ta"], kc_v["tb"]
    tw_steps = [
        lambda e: (e.tensor_tensor(out=twa, in0=s1r, in1=cs["TWr"][:], op=OP.mult),
                   e.tensor_tensor(out=twb, in0=s1i, in1=cs["TWi"][:], op=OP.mult))[-1],
        lambda e: e.tensor_tensor(out=s2_v[0:64, :], in0=twa, in1=twb, op=OP.subtract),
        lambda e: (e.tensor_tensor(out=twa, in0=s1r, in1=cs["TWi"][:], op=OP.mult),
                   e.tensor_tensor(out=twb, in0=s1i, in1=cs["TWr"][:], op=OP.mult))[-1],
        lambda e: e.tensor_tensor(out=s2_v[64:128, :], in0=twa, in1=twb, op=OP.add),
    ]
    vchain(tw_steps, incs_last=[(v0, 1)])
    P.op("tensor", lambda e: e.matmul(ps[1][0:64, 0:64], s2_v, cs["E2c"][:],
                                      start=True, stop=True),
         waits=[(v0, v0.n)])
    P.op("tensor", lambda e: e.drain(), incs=[(t0, 1)])
    P.op("vector", lambda e: e.tensor_scalar(
        out=krev_v, in0=ps[1][0:64, 0:64], scalar1=1.0 / 4096.0, scalar2=None,
        op0=OP.mult), waits=[(t0, t0.n)], incs=[(v0, 1)])
    P.op("tensor", lambda e: e.transpose(ps[2][0:64, 0:64], krev_v,
                                         cs["identf"][0:64, 0:64]),
         waits=[(v0, v0.n)])
    P.op("tensor", lambda e: e.drain(), incs=[(t0, 1)])
    P.op("vector", lambda e: e.tensor_copy(out=krevt_v, in_=ps[2][0:64, 0:64]),
         waits=[(t0, t0.n)], incs=[(v0, 1)])
    P.op("vector", lambda e: e.memset(vrow_v[:, 0:256], 0.0), incs=[(v0, 1)])
    P.op("gpsimd", lambda e: e.dma_start(
        out=bass.AP(krev_hbm, 0, [[64, 64], [1, 64]]), in_=krevt_v),
        waits=[(v0, v0.n)], incs=[(g0, 16)])
    P.op("gpsimd", lambda e: e.dma_start(out=krev_hbm[4096:4352],
                                         in_=vrow_v[0:1, 0:256]),
         incs=[(g0, 16)])
    KREV_G0 = g0.n

    P.op("vector", lambda e: e.memset(tbuf[:, 0:512], 0.0), incs=[(v0, 1)])
    hk_marks = []
    for d in range(32):
        wl = [(g0, KREV_G0)]
        if d >= 2:
            wl.append((v0, hk_marks[d - 2]))
        dma("sync", hank_sb[:, (d % 2) * 128:(d % 2) * 128 + 128],
            bass.AP(krev_hbm, 3968 - 128 * d, [[1, 128], [1, 128]]), ld0,
            waits=wl)
        mk = ld0.n
        P.op("vector", lambda e, d=d: e.tensor_copy(
            out=tbuf[:, 512 + 128 * d:640 + 128 * d],
            in_=bass.AP(hank_sb, (d % 2) * 128 + 127, [[256, 128], [-1, 128]])),
            waits=[(ld0, mk)], incs=[(v0, 1)])
        hk_marks.append(v0.n)
    P.op("vector", lambda e: e.scalar_tensor_tensor(
        out=tbuf[:, 512:640],
        in0=cs["identb"][:], scalar=d128_sb[:], in1=tbuf[:, 512:640],
        op0=OP.mult, op1=OP.add),
        waits=[(g0, BNC_G0)], incs=[(v0, 1)])
    TBUF_V0 = v0.n
    TBUF_T0 = t0.n

    aff_marks = {}
    tp_marks = {}
    tpe_marks = {}
    for tt in range(8):
        slot = tt % 2
        xts = xt_v[:, slot * 4096:slot * 4096 + 4096]
        xns = xn_v[:, slot * 4096:slot * 4096 + 4096]
        wl = [(v0, TBUF_V0)] if tt < 2 else [(v0, aff_marks[tt - 2])]
        dma("sync", xts, x_h[tt * 128:tt * 128 + 128, :], ld0, waits=wl)
        xmk = ld0.n
        awl = [(ld0, xmk)]
        if tt >= 2:
            awl.append((t0, tpe_marks[tt - 2]))
        else:
            awl += [(v0, TBUF_V0), (t0, TBUF_T0), (g0, KREV_G0)]
        P.op("scalar", lambda e, xns=xns, xts=xts: e.activation(
            out=xns, in_=xts, func=AF.Square), waits=awl)
        P.op("scalar", lambda e: e.drain(), incs=[(a0, 1)])
        amk = a0.n

        def st1(e, xts=xts, xns=xns):
            e.tensor_reduce(out=mrow_sb[:, 0:1], in_=xts, axis=XAX, op=OP.add)
            return e.tensor_reduce(out=mrow_sb[:, 1:2], in_=xns, axis=XAX, op=OP.add)
        def st2(e):
            e.tensor_scalar(out=mrow_sb[:, 2:3], in0=mrow_sb[:, 0:1],
                            scalar1=1.0 / 4096.0, scalar2=None, op0=OP.mult)
            return e.tensor_scalar(out=mrow_sb[:, 3:4], in0=mrow_sb[:, 1:2],
                                   scalar1=1.0 / 4096.0, scalar2=None, op0=OP.mult)
        def st3(e):
            return e.tensor_tensor(out=mrow_sb[:, 4:5], in0=mrow_sb[:, 2:3],
                                   in1=mrow_sb[:, 2:3], op=OP.mult)
        def st4(e):
            return e.tensor_tensor(out=mrow_sb[:, 5:6], in0=mrow_sb[:, 3:4],
                                   in1=mrow_sb[:, 4:5], op=OP.subtract)
        vchain([st1, st2, st3, st4], waits0=[(ld0, xmk), (a0, amk)],
               incs_last=[(v0, 1)])
        P.op("scalar", lambda e: e.activation(out=mrow_sb[:, 6:7],
                                              in_=mrow_sb[:, 5:6],
                                              func=AF.Sqrt, bias=np.float32(1e-5)),
             waits=[(v0, v0.n)])
        P.op("scalar", lambda e: e.drain(), incs=[(a0, 1)])

        def rs1(e):
            return e.reciprocal(out=mrow_sb[:, 7:8], in_=mrow_sb[:, 6:7])
        def rs2(e):
            return e.tensor_tensor(out=mrow_sb[:, 0:1], in0=mrow_sb[:, 2:3],
                                   in1=mrow_sb[:, 7:8], op=OP.mult)
        def rs3(e):
            return e.tensor_scalar(out=mrow_sb[:, 1:2], in0=mrow_sb[:, 0:1],
                                   scalar1=-1.0, scalar2=None, op0=OP.mult)
        vchain([rs1, rs2, rs3], waits0=[(a0, a0.n)], incs_last=[(v0, 1)])
        P.op("vector", lambda e, tt=tt: e.tensor_copy(
            out=mrowall_sb[:, tt * 8:tt * 8 + 8], in_=mrow_sb[:, 0:8]),
            waits=[(v0, v0.n)])
        P.op("vector", lambda e, xns=xns, xts=xts: e.tensor_scalar(
            out=xns, in0=xts, scalar1=mrow_sb[:, 7:8], scalar2=mrow_sb[:, 1:2],
            op0=OP.mult, op1=OP.add), waits=[(v0, v0.n)], incs=[(v0, 1)])
        aff_marks[tt] = v0.n
        if debug and tt == 2:
            P.op("gpsimd", lambda e, xts=xts: e.dma_start(out=xt2_dbg[:], in_=xts),
                 waits=[(v0, aff_marks[tt])], incs=[(g0, 16)])
            P.op("gpsimd", lambda e, xns=xns: e.dma_start(out=xn2_dbg[:], in_=xns),
                 waits=[(v0, aff_marks[tt])], incs=[(g0, 16)])
        for gi in range(8):
            bank = ps[6 + gi % 2]
            twl = [(v0, aff_marks[tt])]
            if (tt, gi) > (0, 1):
                pj = (tt, gi - 2) if gi >= 2 else (tt - 1, 6 + gi)
                twl.append((v0, tp_marks[pj]))
            for i in range(4):
                jc = gi * 4 + i
                P.op("tensor", lambda e, xns=xns, jc=jc, i=i, bank=bank: e.transpose(
                    bank[:, i * 128:i * 128 + 128],
                    xns[:, jc * 128:jc * 128 + 128], cs["identf"][:]),
                    waits=twl if i == 0 else [])
            P.op("tensor", lambda e: e.drain(), incs=[(t0, 1)])
            tmark = t0.n
            P.op("vector", lambda e, gi=gi, tt=tt, bank=bank: e.tensor_copy(
                out=xnT[:, gi * 4:gi * 4 + 4, tt * 128:tt * 128 + 128],
                in_=bank[:, 0:512].rearrange("p (a b) -> p a b", a=4)),
                waits=[(t0, tmark)], incs=[(v0, 1)])
            tp_marks[(tt, gi)] = v0.n
        tpe_marks[tt] = t0.n

    if debug:
        P.op("gpsimd", lambda e: e.dma_start(out=mrow_dbg[:], in_=mrowall_sb[:]),
             waits=[(v0, v0.n)], incs=[(g0, 16)])
        xnT_dbg = nc.dram_tensor("xnT_dbg", [128, 32768], BF16, kind="ExternalOutput")
        tb_dbg = nc.dram_tensor("tbuf_dbg", [128, 4608], BF16, kind="ExternalOutput")
        P.op("gpsimd", lambda e: e.dma_start(
            out=xnT_dbg[:], in_=xnT.rearrange("p j t -> p (j t)")),
            waits=[(v0, v0.n)], incs=[(g0, 16)])
        P.op("gpsimd", lambda e: e.dma_start(out=tb_dbg[:], in_=tbuf),
            waits=[(v0, v0.n)], incs=[(g0, 16)])
    for half in range(2):
        for kc in range(32):
            P.op("tensor", lambda e, kc=kc, half=half: e.matmul(
                ps[4][0:32, :], cs["esl"][:, 31 - kc:63 - kc],
                xnT[:, kc, half * 512:half * 512 + 512],
                start=(kc == 0), stop=(kc == 31)),
                waits=[(v0, v0.n)] if kc == 0 else [])
        P.op("tensor", lambda e: e.drain(), incs=[(t0, 1)])
        P.op("vector", lambda e, half=half: e.tensor_copy(
            out=xt32_sb[0:32, half * 512:half * 512 + 512], in_=ps[4][0:32, :]),
            waits=[(t0, t0.n)], incs=[(v0, 1)])
    XT_DONE = v0.n

    barrier(prewaits=[("gpsimd", g0, g0.n), ("sync", ld0, ld0.n),
                      ("vector", v0, v0.n), ("scalar", a0, a0.n),
                      ("tensor", t0, t0.n)])

    qw4 = qwgu_h.ap().rearrange("(kc p) w -> p kc w", p=128)
    spm = sgupm_b_hbm.ap().rearrange("(pr kc l) -> pr kc l", kc=32, l=512)
    pkr = pk_v.rearrange("p (s g k w) -> p s g k w", s=4, g=2, k=4)
    wqr = wq_v.rearrange("p (k g w j) -> p k g w j", k=4, g=2, w=32)
    wbr = wbf_v.rearrange("p (s n) -> p s n", s=WRING)

    pk_marks, rep_marks, dq_marks = {}, {}, {}
    mm1_marks, ea_marks, ev_marks, ho_marks = {}, {}, {}, {}

    def p1_dmas(p, hq):
        G = p * 8 + hq
        s4 = G % 4
        wl = [(dq1, G - 3)] if G >= 4 else []
        for gi, woff in ((0, p * 32), (1, 1376 + p * 32)):
            dma("sync", pkr[:, s4, gi],
                qw4[:, 4 * hq:4 * hq + 4, woff:woff + 32], pk1, waits=wl)
        pk_marks[G] = pk1.n
        wl = [(dq1, G - 1)] if G >= 2 else []
        s2 = G % 2
        srr = sr_v.rearrange("p (s a l) -> p s a l", s=2, a=4)
        dma("sync", srr[:, s2],
            spm[p, 4 * hq:4 * hq + 4, :].partition_broadcast(128), rep1, waits=wl)
        rep_marks[G] = rep1.n

    def p1_deq(p, hq):
        G = p * 8 + hq
        s4, s2, sw = G % 4, G % 2, G % WRING

        def unpack(e):
            last = None
            for gi in range(2):
                for j in range(8):
                    last = e.tensor_scalar(
                        out=wqr[:, :, gi, :, j], in0=pkr[:, s4, gi],
                        scalar1=28 - 4 * j, scalar2=28,
                        op0=OP.logical_shift_left, op1=OP.logical_shift_right)
            return last

        wl = [(pk1, pk_marks[G])]
        if G >= WRING:
            wl.append((wf1, G - WRING + 1))
        P.op("vector", unpack, waits=wl, incs=[(vch, 1)])
        P.op("vector", lambda e: e.tensor_copy(out=wbr[:, sw], in_=wq_v),
             waits=[(vch, vch.n)], incs=[(vch, 1)])
        P.op("vector", lambda e: e.tensor_tensor(
            out=wbr[:, sw], in0=wbr[:, sw],
            in1=sr_v[:, s2 * 2048:s2 * 2048 + 2048], op=OP.mult),
            waits=[(rep1, rep_marks[G]), (vch, vch.n)], incs=[(dq1, 1)])
        dq_marks[G] = dq1.n

    def p1_zsn(p):
        wl = [(mm1, 2 * (p - 1))] if p >= 2 else []
        zv = zsneg_hbm.ap().rearrange("(g n) -> g n", n=22016)
        dma("sync", zsn_sb[0:32, (p % 2) * 512:(p % 2) * 512 + 256],
            zv[:, p * 256:p * 256 + 256], rep1, waits=wl)
        dma("sync", zsn_sb[0:32, (p % 2) * 512 + 256:(p % 2) * 512 + 512],
            zv[:, 11008 + p * 256:11008 + p * 256 + 256], rep1)
        zsn_marks[p] = rep1.n

    def p1_pe(p, half):
        u = 2 * p + half
        st = u % 2
        for c in range(4):
            cwl = [(rep1, zsn_marks[p]), (v0, XT_DONE)] if c == 0 else []
            if c == 0 and u >= 2:
                cwl += [(ea1, ea_marks[u - 2]), (ev1, ev_marks[u - 2])]
            P.op("tensor", lambda e, c=c, st=st, half=half, p=p: e.matmul(
                ps[4 * st + c][:],
                zsn_sb[0:32, (p % 2) * 512 + c * 128:(p % 2) * 512 + c * 128 + 128],
                xt32_sb[0:32, half * 512:half * 512 + 512],
                start=True, stop=False), waits=cwl)
        for kc in range(32):
            G = p * 8 + kc // 4
            sw = G % WRING
            wl = []
            if kc % 4 == 0:
                wl = [(dq1, dq_marks[G])]
            for c in range(4):
                P.op("tensor", lambda e, sw=sw, kc=kc, c=c, half=half, st=st: e.matmul(
                    ps[4 * st + c][:],
                    wbr[:, sw, (kc % 4) * 512 + c * 128:(kc % 4) * 512 + c * 128 + 128],
                    xnT[:, kc, half * 512:half * 512 + 512],
                    start=False, stop=(kc == 31)),
                    waits=wl if c == 0 else [])
        P.op("tensor", lambda e: e.drain(), incs=[(mm1, 1)])
        if half == 1:
            P.op("tensor", lambda e: e.nop(), incs=[(wf1, 8)])
        mm1_marks[u] = mm1.n

    def p1_act(p, half):
        u = 2 * p + half
        st = u % 2
        for c01 in range(2):
            wl = []
            if c01 == 0:
                wl.append((mm1, mm1_marks[u]))
                if u >= 1:
                    wl.append((ev1, ev_marks[u - 1]))
            P.op("scalar", lambda e, st=st, c01=c01: e.activation(
                out=sil_v[:, c01 * 512:c01 * 512 + 512], in_=ps[4 * st + c01][:],
                func=AF.Silu),
                waits=wl)
        P.op("scalar", lambda e: e.drain(), incs=[(ea1, 1)])
        ea_marks[u] = ea1.n

    def p1_evac(p, half):
        u = 2 * p + half
        st = u % 2
        for c01 in range(2):
            hs = 2 * (u % 2) + c01
            wl = [(ea1, ea_marks[u])]
            if u >= 2:
                wl.append((ho1, ho_marks.get((u - 2, c01), 0)))
            P.op("vector", lambda e, hs=hs, st=st, c01=c01: e.tensor_tensor(
                out=hbf_v[:, hs * 512:hs * 512 + 512],
                in0=sil_v[:, c01 * 512:c01 * 512 + 512],
                in1=ps[4 * st + 2 + c01][:], op=OP.mult),
                waits=wl, incs=[(ev1, 1)])
        ev_marks[u] = ev1.n

    def p1_hout(p, half):
        u = 2 * p + half
        for c01 in range(2):
            hs = 2 * (u % 2) + c01
            chunk = 2 * p + c01
            P.op("gpsimd", lambda e, hs=hs, chunk=chunk, half=half: e.dma_start(
                out=h_hbm[chunk * 128:chunk * 128 + 128,
                          half * 512:half * 512 + 512],
                in_=hbf_v[:, hs * 512:hs * 512 + 512]),
                waits=[(ev1, ev_marks[u] - 1 + c01)], incs=[(ho1, 16)])
            ho_marks[(u, c01)] = ho1.n

    zsn_marks = {}
    for p in range(NPAIR + 1):
        if p < NPAIR:
            p1_zsn(p)
            for hq in range(4):
                p1_dmas(p, hq)
                p1_deq(p, hq)
        if p >= 1:
            p1_pe(p - 1, 0)
            p1_act(p - 1, 0)
            p1_evac(p - 1, 0)
            p1_hout(p - 1, 0)
            p1_pe(p - 1, 1)
        if p < NPAIR:
            for hq in range(4, 8):
                p1_dmas(p, hq)
                p1_deq(p, hq)
        if p >= 1:
            p1_act(p - 1, 1)
            p1_evac(p - 1, 1)
            p1_hout(p - 1, 1)

    barrier(prewaits=[("gpsimd", ho1, ho1.n), ("vector", ev1, ev1.n),
                      ("scalar", ea1, ea1.n), ("tensor", ev1, ev1.n),
                      ("sync", rep1, rep1.n)])


    hre = h_hbm.ap().rearrange("(c p) t -> p c t", p=128)
    qd4 = qwdn_h.ap().rearrange("(kc p) w -> p kc w", p=128)
    wqdr = wqd_v.rearrange("p (C j c) -> p C j c", j=8, c=16)
    pkd_marks, repd_marks, wdq_marks = {}, {}, {}
    def p25_chunk(cch):
        s2 = cch % 2
        wl = [(wdq, cch - 1)] if cch >= 2 else []
        dma("sync", pkd_v[:, s2 * 512:s2 * 512 + 512], qd4[:, cch, :], pkd,
            waits=wl)
        pkd_marks[cch] = pkd.n
        dma("sync", zrd_v[:, s2 * 4096:s2 * 4096 + 4096],
            zdn2[cch:cch + 1, :].partition_broadcast(128).squeeze(1),
            repd, waits=wl)
        dma("sync", srd_v[:, s2 * 4096:s2 * 4096 + 4096],
            sdnb2[cch:cch + 1, :].partition_broadcast(128).squeeze(1), repd)
        repd_marks[cch] = repd.n

        def unpackd(e, s2=s2):
            last = None
            pin = pkd_v[:, s2 * 512:s2 * 512 + 512].rearrange(
                "p (C c) -> p C c", c=16)
            for j in range(8):
                last = e.tensor_scalar(
                    out=wqdr[:, :, j, :], in0=pin,
                    scalar1=28 - 4 * j, scalar2=28,
                    op0=OP.logical_shift_left, op1=OP.logical_shift_right)
            return last

        wl = [(pkd, pkd_marks[cch])]
        if cch >= 2:
            wl.append((wout, 16 * (cch - 1)))
        P.op("vector", unpackd, waits=wl, incs=[(vch, 1)])
        P.op("vector", lambda e, s2=s2: e.tensor_tensor(
            out=wdb_v[:, s2 * 4096:s2 * 4096 + 4096], in0=wqd_v,
            in1=zrd_v[:, s2 * 4096:s2 * 4096 + 4096], op=OP.subtract),
            waits=[(repd, repd_marks[cch]), (vch, vch.n)], incs=[(vch, 1)])
        P.op("vector", lambda e, s2=s2: e.tensor_tensor(
            out=wdb_v[:, s2 * 4096:s2 * 4096 + 4096],
            in0=wdb_v[:, s2 * 4096:s2 * 4096 + 4096],
            in1=srd_v[:, s2 * 4096:s2 * 4096 + 4096], op=OP.mult),
            waits=[(vch, vch.n)], incs=[(wdq, 1)])
        wdq_marks[cch] = wdq.n
        P.op("gpsimd", lambda e, cch=cch, s2=s2: e.dma_start(
            out=wdn_hbm[cch * 128:cch * 128 + 128, :],
            in_=wdb_v[:, s2 * 4096:s2 * 4096 + 4096]),
            waits=[(wdq, wdq_marks[cch])], incs=[(wout, 16)])

    cev_marks, cmm_marks = {}, {}
    p25_done = 0
    for k in range(64):
        tt, ip = k // 8, k % 8
        bank = ps[2 + k % 6]
        njc = min(32, 4 * ip + 4)
        wl = [(cev, cev_marks.get(k - 6, 0))]
        for jc in range(njc):
            P.op("tensor", lambda e, jc=jc, tt=tt, ip=ip, bank=bank: e.matmul(
                bank[:], xnT[:, jc, tt * 128:tt * 128 + 128],
                tbuf[:, 512 + 128 * (4 * ip - jc):512 + 128 * (4 * ip - jc) + 512],
                start=(jc == 0), stop=(jc == njc - 1)),
                waits=wl if jc == 0 else [])
        P.op("tensor", lambda e: e.drain(), incs=[(cmm, 1)])
        cmm_marks[k] = cmm.n
        P.op("scalar", lambda e, k=k, bank=bank: e.copy(
            out=stile_v[:, (k % 2) * 512:(k % 2) * 512 + 512], in_=bank[:]),
            waits=[(cmm, cmm_marks[k]), (cso, 16 * (k - 1))])
        P.op("scalar", lambda e: e.drain(), incs=[(cev, 1)])
        cev_marks[k] = cev.n
        P.op("scalar", lambda e, k=k, tt=tt, ip=ip: e.dma_start(
            out=s_hbm[tt * 128:tt * 128 + 128, ip * 512:ip * 512 + 512],
            in_=stile_v[:, (k % 2) * 512:(k % 2) * 512 + 512]),
            waits=[(cev, cev_marks[k])], incs=[(cso, 16)])
        want = (k + 1) * 86 // 64
        while p25_done < want:
            p25_chunk(p25_done)
            p25_done += 1


    hvr0 = h_v.rearrange("p (c t) -> p c t", c=86)
    for hc in range(86):
        dma("sync", hvr0[:, hc], hre[:, hc, 0:512], hld,
            waits=[(cmm, 64)] if hc == 0 else [])

    barrier(prewaits=[("gpsimd", wout, wout.n), ("vector", wdq, wdq.n),
                      ("gpsimd", cso, cso.n), ("vector", cev, cev.n),
                      ("tensor", cev, cev.n)])

    g3_marks, e3_marks, sx_marks, fin_marks = {}, {}, {}, {}
    mm3_marks = {}

    def p3_sx(u):
        half, panel = u // 8, u % 8
        for t in range(4):
            ttg = half * 4 + t
            slot = (u * 4 + t) % 8
            wl = []
            if u >= 2:
                wl = [(e3, e3_marks.get((u - 2, t), 0))]
            dma("scalar", sx_v[:, slot * 512:slot * 512 + 512],
                s_hbm[ttg * 128:ttg * 128 + 128, panel * 512:panel * 512 + 512],
                sx3, waits=wl)
            dma("scalar", xx_v[:, slot * 512:slot * 512 + 512],
                x_h[ttg * 128:ttg * 128 + 128, panel * 512:panel * 512 + 512],
                sx3)
            sx_marks[(u, t)] = sx3.n

    for u in range(16):
        half, panel = u // 8, u % 8
        if panel == 0 and half == 1:
            wl = [(mmd, 8 * half)]
            hvr = h_v.rearrange("p (c t) -> p c t", c=86)
            for hc in range(86):
                dma("sync", hvr[:, hc], hre[:, hc, half * 512:half * 512 + 512],
                    hld, waits=wl if hc == 0 else [])
        if u == 0:
            p3_sx(0)
        st = u % 2
        for cch in range(86):
            G = u * 86 + cch
            if G == 0:
                for Gp in range(4):
                    dma("sync", wt_v[:, (Gp % 24) * 512:(Gp % 24) * 512 + 512],
                        wdn_hbm[Gp * 128:Gp * 128 + 128, 0:512], wt3)
            Gn = G + 4
            if Gn < 16 * 86:
                un, cn = Gn // 86, Gn % 86
                gw = [(mm3, Gn - 19)] if Gn >= 20 else []
                dma("sync", wt_v[:, (Gn % 24) * 512:(Gn % 24) * 512 + 512],
                    wdn_hbm[cn * 128:cn * 128 + 128,
                            (un % 8) * 512:(un % 8) * 512 + 512], wt3, waits=gw)
            wl = [(wt3, 16 * min(G + 4, 16 * 86))]
            if half == 1 and u == 8:
                wl.append((hld, 16 * (86 + min(cch + 4, 86))))
            elif cch == 0:
                wl.append((hld, 16 * 86 * (half + 1)))
                if u >= 2:
                    wl.append((g3s, g3_marks[u - 2]))
                    wl.append((e3, e3_marks[(u - 2, 3)]))
            sw = G % 24
            for t in range(4):
                P.op("tensor", lambda e, sw=sw, cch=cch, t=t, st=st, half=half: e.matmul(
                    ps[4 * st + t][:],
                    h_v[:, cch * 512 + t * 128:cch * 512 + t * 128 + 128],
                    wt_v[:, sw * 512:sw * 512 + 512],
                    start=(cch == 0), stop=(cch == 85)),
                    waits=wl if t == 0 else [],
                    incs=[(mm3, 1)] if t == 3 else [])
        P.op("tensor", lambda e: e.drain(), incs=[(mmd, 1)])
        mm3_marks[u] = mmd.n
        if u < 15:
            p3_sx(u + 1)
        for t in range(4):
            gsl = t % 2
            wl = [(mmd, mm3_marks[u])] if t == 0 else []
            if t >= 2:
                wl.append((e3, e3_marks[(u, t - 2)]))
            elif u >= 1:
                wl.append((e3, e3_marks[(u - 1, t + 2)]))
            P.op("scalar", lambda e, gsl=gsl, st=st, t=t: e.activation(
                out=g3_v[:, gsl * 512:gsl * 512 + 512], in_=ps[4 * st + t][:],
                func=AF.Silu), waits=wl)
            P.op("scalar", lambda e: e.drain(), incs=[(g3s, 1)])
            g3_marks[u] = g3s.n
            slot = (u * 4 + t) % 8
            sxtgt = min(sx_marks[(u, t)] + 32, sx3.n)
            wl = [(g3s, g3s.n), (sx3, sxtgt)]
            if u >= 1:
                wl.append((fin, fin_marks.get((u - 1, t), 0)))
            P.op("vector", lambda e, gsl=gsl, slot=slot: e.tensor_tensor(
                out=t3_v[:, gsl * 512:gsl * 512 + 512].rearrange(
                    "p (C c j) -> p C c j", c=16, j=8),
                in0=g3_v[:, gsl * 512:gsl * 512 + 512].rearrange(
                    "p (C j c) -> p C j c", j=8, c=16).transpose([0, 1, 3, 2]),
                in1=sx_v[:, slot * 512:slot * 512 + 512].rearrange(
                    "p (C c j) -> p C c j", c=16, j=8), op=OP.mult),
                waits=wl, incs=[(vch, 1)])
            P.op("vector", lambda e, gsl=gsl, slot=slot, t=t: e.tensor_tensor(
                out=ob_v[:, t * 512:t * 512 + 512],
                in0=t3_v[:, gsl * 512:gsl * 512 + 512],
                in1=xx_v[:, slot * 512:slot * 512 + 512], op=OP.add),
                waits=[(vch, vch.n)], incs=[(e3, 1)])
            e3_marks[(u, t)] = e3.n
            ttg = half * 4 + t
            P.op("gpsimd", lambda e, t=t, ttg=ttg, panel=panel: e.dma_start(
                out=out_h[ttg * 128:ttg * 128 + 128, panel * 512:panel * 512 + 512],
                in_=ob_v[:, t * 512:t * 512 + 512]),
                waits=[(e3, e3_marks[(u, t)])], incs=[(fin, 16)])
            fin_marks[(u, t)] = fin.n

    P.op("gpsimd", lambda e: e.nop(), waits=[(fin, fin.n)])

    with nc.Block() as block:
        for ename in ENGINES:
            def emit(eng, ename=ename):
                for fn, wl, il in P.ops[ename]:
                    for h, vv in wl:
                        eng.wait_ge(h, vv)
                    if fn is not None:
                        inst = fn(eng)
                        for h, k in il:
                            inst.then_inc(h, k)
            getattr(block, ename)(emit)

    return nc


_PROG = None


def kernel(**inputs):
    global _PROG
    if _PROG is None:
        _PROG = build_program()
    nc = _PROG

    f = lambda a: np.ascontiguousarray(a, dtype=np.float32)
    x = f(inputs["L0_attn_u"]).reshape(8192, 4096)
    lam = np.concatenate([f(inputs["L0_attn_Lambda_ri"][:, 0]),
                          f(inputs["L0_attn_Lambda_ri"][:, 1])]).reshape(128, 1)
    vct = np.concatenate([f(inputs["L0_attn_Vc_ri"][..., 0]).T,
                          f(inputs["L0_attn_Vc_ri"][..., 1]).T], axis=1)
    ctpq = np.concatenate([
        f(inputs["L0_attn_Ct_ri"][0, :, 0]), f(inputs["L0_attn_Ct_ri"][0, :, 1]),
        f(inputs["L0_attn_p_ri"][:, 0]), f(inputs["L0_attn_p_ri"][:, 1]),
        f(inputs["L0_attn_q_ri"][:, 0]), f(inputs["L0_attn_q_ri"][:, 1])]
    ).reshape(1, 384)
    dls = np.array([[inputs["L0_attn_D"][0], inputs["L0_attn_log_step"][0]]],
                   dtype=np.float32)
    sg = f(inputs["L0_ffn_scales_gu"])
    sgupm = np.empty((NPAIR, 32, 512), np.float32)
    for p in range(NPAIR):
        sgupm[p, :, 0:256] = sg[:, p * 256:p * 256 + 256]
        sgupm[p, :, 256:512] = sg[:, 11008 + p * 256:11008 + p * 256 + 256]
    base = {
        "qwgu": np.ascontiguousarray(inputs["L0_ffn_qweight_gu"], np.int32),
        "qzgu": np.ascontiguousarray(inputs["L0_ffn_qzeros_gu"], np.int32).reshape(-1),
        "sgupm": sgupm.reshape(-1),
        "sgun": sg.reshape(-1),
        "qwdn": np.ascontiguousarray(inputs["L0_ffn_qweight_dn"], np.int32),
        "qzdn": np.ascontiguousarray(inputs["L0_ffn_qzeros_dn"], np.int32).reshape(-1),
        "sdn": f(inputs["L0_ffn_scales_dn"]).reshape(-1),
        "lam": lam, "vct": np.ascontiguousarray(vct),
        "bp": f(inputs["L0_attn_Bp"]), "ctpq": ctpq, "dls": dls,
    }
    for k, v in _CONSTS.items():
        base["c_" + k] = v

    in_maps = []
    for c in range(8):
        m = dict(base)
        m["x"] = np.ascontiguousarray(x[c * 1024:(c + 1) * 1024])
        in_maps.append(m)

    res = run_bass_kernel_spmd(nc, in_maps, core_ids=list(range(8)))
    out = np.concatenate([np.asarray(r["out"]) for r in res.results], axis=0)
    return out.reshape(2, 4096, 4096).astype(np.float32)
